# version 18
# baseline (speedup 1.0000x reference)
import math
import numpy as np
import concourse.bass as bass
import concourse.mybir as mybir
from concourse.bass_utils import run_bass_kernel_spmd
from contextlib import ExitStack

F32 = mybir.dt.float32
BF16 = mybir.dt.bfloat16
AF = mybir.ActivationFunctionType
ALU = mybir.AluOpType
AX = mybir.AxisListType

D = 1024
T = 2048
NT = 16
IC = 8
NB = 2
NE = 8
LN_EPS = 1e-5

STAGES = 6
RACE_CHECK = False
DBG = 0
L1RAW = 0

ENGINES = "tvsgy"


class Bld:

    def __init__(self, nc, ctx):
        self.nc = nc
        self.e = {
            "t": nc.tensor, "v": nc.vector, "s": nc.scalar,
            "g": nc.gpsimd, "y": nc.sync,
        }
        self.sem = {k: ctx.enter_context(nc.semaphore(f"sp_{k}")) for k in ENGINES}
        dkeys = ["pp", "ppst", "wa", "wb", "xls", "sto", "mw", "m2", "m3", "m4", "qa", "qb", "ta", "tb",
                 "ha", "hb", "hg", "hh", "pw1", "pw2", "pp2", "pp3", "pp4", "pst2",
                 "pw3", "pw4", "pw5", "pw6", "pw7", "pw8", "pw9", "pw10"]
        self.dsem = {k: ctx.enter_context(nc.semaphore(f"sd_{k}")) for k in dkeys}
        self.cnt = {k: 0 for k in ENGINES}
        self.dcnt = {k: 0 for k in dkeys}
        self.waited = {}
        self.ps = [ctx.enter_context(nc.psum_tensor(f"ps{i}", [128, 512], F32))
                   for i in range(6)]
        self.pb = [ctx.enter_context(nc.psum_tensor(f"pb{i}", [128, 1024], BF16))
                   for i in range(2)]
        self.ps_free = [None] * 6
        self.pb_free = [None] * 2

    def sig(self, k, inst):
        self.cnt[k] += 1
        inst.then_inc(self.sem[k], 1)
        return ("p", k, self.cnt[k])

    def seq(self, k, inst):
        tk = self.sig(k, inst)
        self.wait(k, tk)
        return tk

    def tok(self, k):
        return ("p", k, self.cnt[k])

    def dma(self, k, out, in_, sem="mw", **kw):
        inst = self.e[k].dma_start(out=out, in_=in_, **kw)
        self.dcnt[sem] += 16
        inst.then_inc(self.dsem[sem], 16)
        return ("d", sem, self.dcnt[sem])

    ORDERED = {"ppst", "pw3", "pw4"}

    def wait(self, ck, tok):
        if tok is None:
            return
        kind, pk, v = tok
        key = (ck, kind, pk)
        if self.waited.get(key, 0) >= v:
            return
        if kind == "d" and pk not in self.ORDERED:
            assert v == self.dcnt[pk], (pk, v, self.dcnt[pk])
        sem = self.sem[pk] if kind == "p" else self.dsem[pk]
        self.e[ck].wait_ge(sem, v)
        self.waited[key] = v

    def waits(self, ck, toks):
        for t in toks:
            self.wait(ck, t)

    def barrier(self):
        for dk, v in self.dcnt.items():
            if v:
                self.wait("y", ("d", dk, v))
        toks = {}
        for k in ENGINES:
            self.e[k].drain()
            toks[k] = self.sig(k, self.e[k].nop())
        for ck in ENGINES:
            for pk in ENGINES:
                self.wait(ck, toks[pk])
        return toks


def ts(i, n=128):
    return slice(i * n, (i + 1) * n)


def build():
    nc = bass.Bass("TRN2", target_bir_lowering=False,
                   detect_race_conditions=RACE_CHECK)

    H = {}
    def di(name, shape):
        H[name] = nc.dram_tensor(name, list(shape), F32, kind="ExternalInput").ap()
    di("L0_attn_x", (NB, T, D))
    for n, s in [
        ("L0_attn_Wq", (D, D)), ("L0_attn_Wk", (D, D)), ("L0_attn_Wv", (D, D)),
        ("L0_attn_Wo", (D, D)), ("L0_attn_splat_positions", (64, D)),
        ("L0_attn_splat_scales", (64,)), ("L0_attn_splat_importance", (64,)),
        ("L0_attn_We", (D, 512)), ("L0_attn_be", (512,)),
        ("L0_attn_Wt", (64, 512, 512)), ("L0_attn_bt", (64, 512)),
        ("L0_attn_Wd", (512, D)), ("L0_attn_bd", (D,)),
        ("L0_attn_Wg", (D, D)), ("L0_attn_bg", (D,)),
        ("L0_ffn_Wg", (D, NE)), ("L0_ffn_bg", (NE,)),
        ("L0_ffn_We", (NE, D, D)), ("L0_ffn_be", (NE, D)),
        ("L1_attn_in_w", (2 * D, D)), ("L1_attn_in_b", (2 * D,)),
        ("L1_attn_conv_w", (2 * D, 1, 4)), ("L1_attn_conv_b", (2 * D,)),
        ("L1_attn_xp_w", (32, 2 * D)), ("L1_attn_xp_b", (32,)),
        ("L1_attn_out_w", (D, 16)), ("L1_attn_out_b", (D,)),
        ("L1_ffn_routing_token", (D,)), ("L1_ffn_gamma_light", (D,)),
        ("L1_ffn_w1_light", (D, 512)), ("L1_ffn_b1_light", (512,)),
        ("L1_ffn_w2_light", (512, D)), ("L1_ffn_b2_light", (D,)),
        ("L1_ffn_gamma_heavy", (D,)), ("L1_ffn_w1_heavy", (D, 4096)),
        ("L1_ffn_b1_heavy", (4096,)), ("L1_ffn_w2_heavy", (4096, D)),
        ("L1_ffn_b2_heavy", (D,)),
    ]:
        di(n, s)
    out_h = nc.dram_tensor("out", [NB, T, D], F32, kind="ExternalOutput").ap()
    if DBG:
        dbg_lnx = nc.dram_tensor("dbg_lnx", [128, NT, D], BF16, kind="ExternalOutput").ap()
        dbg_lnxt = nc.dram_tensor("dbg_lnxt", [128, IC, T], BF16, kind="ExternalOutput").ap()
        dbg_g05 = nc.dram_tensor("dbg_g05", [128, NT, NE], F32, kind="ExternalOutput").ap()
        dbg_gexp = nc.dram_tensor("dbg_gexp", [128, NT, NE], F32, kind="ExternalOutput").ap()
        dbg_stat = nc.dram_tensor("dbg_stat", [128, 4 * NT], F32, kind="ExternalOutput").ap()
        dbg_va = nc.dram_tensor("dbg_va", [128, 4 * NT], F32, kind="ExternalOutput").ap()
        dbg_wet = nc.dram_tensor("dbg_wet", [2, 128, IC * 1024], BF16, kind="ExternalOutput").ap()

    wet_h = nc.dram_tensor("wet_scr", [NE, 128, IC * 1024], BF16, kind="Internal").ap()
    trd_h = nc.dram_tensor("trd_scr", [64, 512], BF16, kind="Internal").ap()
    w1hp_h = nc.dram_tensor("w1hp_scr", [IC, 128, 4096], BF16, kind="Internal").ap()
    w1pk_h = nc.dram_tensor("w1pk_scr", [32, 128, 1024], BF16, kind="Internal").ap()
    wtb_h = nc.dram_tensor("wtb_scr", [64, 128, 2048], BF16, kind="Internal").ap()
    w2hp_h = nc.dram_tensor("w2hp_scr", [32, 128, 1024], BF16, kind="Internal").ap()
    inwt_h = nc.dram_tensor("inwt_scr", [16, 128, IC, 128], BF16, kind="Internal").ap()

    ctx = ExitStack()
    with ctx:
        sb = lambda name, shape, dt=F32: ctx.enter_context(nc.sbuf_tensor(name, list(shape), dt))
        X = sb("X", [128, NT, D])
        LNXP = sb("LNXP", [128, 16640], BF16)
        LNX = LNXP[:, 0:NT * D].rearrange("p (t d) -> p t d", t=NT)
        LNXT = sb("LNXT", [128, IC, T], BF16)
        SCRB = sb("SCRB", [128, 16384], BF16)
        SUM = sb("SUM", [128, NT])
        SQ = sb("SQ", [128, NT])
        VAR = sb("VAR", [128, NT])
        RSTD = sb("RSTD", [128, NT])
        BIASC = sb("BIASC", [128, NT])
        TMP16 = sb("TMP16", [128, NT])
        VA = sb("VA", [128, NT])
        VB = sb("VB", [128, NT])
        SCR2 = sb("SCR2", [128, 2048], BF16)
        JUNK = SCR2.rearrange("p (j d) -> p j d", j=2)
        GEXP = sb("GEXP", [128, NT, NE])
        GSEL = sb("GSEL", [128, NT, NE])
        G05 = sb("G05", [128, NT, NE])
        M8 = sb("M8", [128, NT, NE])
        GSUM = sb("GSUM", [128, NT])
        WGMOE = sb("WGMOE", [128, IC, NE], BF16)
        BEROW = SCR2[0:1, 0:2048].rearrange("p (j d) -> p j d", j=2)
        BGMOE = sb("BGMOE", [1, NE], BF16)
        ONESR = sb("ONESR", [1, 128], BF16)
        ONESC = sb("ONESC", [128, 1])
        IDB = sb("IDB", [128, 128], BF16)
        IDF = sb("IDF", [128, 128])
        IOTA = sb("IOTA", [128, NT])
        IOTA16 = sb("IOTA16", [1, 16])
        CONSTS = sb("CONSTS", [128, 8])
        A = {}
        A["SPTN"] = sb("SPTN", [128, IC, 64], BF16)
        A["S2HB"] = sb("S2HB", [1, 64], BF16)
        A["NH"] = sb("NH", [64, 1])
        A["AIMP"] = sb("AIMP", [64, 1])
        INFLF = sb("INFL32", [128, T])
        A["INFL32"] = INFLF[0:64, :]
        A["QTH"] = A["INFL32"][:, 0:D].bitcast(BF16)
        A["KTH"] = A["INFL32"][:, D:T].bitcast(BF16)
        A["Q2"] = INFLF[:, 0:D].bitcast(BF16)
        A["K2"] = INFLF[:, D:T].bitcast(BF16)
        A["INFLB"] = sb("INFLB", [64, T], BF16)
        A["AVS"] = A["INFLB"][:, 0:512]
        A["AVN"] = A["INFLB"][:, 512:1024]
        A["INFLTM"] = sb("INFLTM", [128, NT, 64], BF16)
        A["TOTR"] = sb("TOTR", [64, 1])
        POOLC = sb("POOLC", [128, 2048], BF16)
        A["GATH"] = POOLC[0:64, 0:D]
        A["GATHT"] = POOLC[:, D:D + 512].rearrange("p (ic f) -> p ic f", ic=IC)
        A["ENC"] = POOLC[0:64, D + 512:T]
        A["WQS"] = POOLC[:, 0:1024].rearrange("p (ic f) -> p ic f", ic=IC)
        A["WKS"] = POOLC[:, 1024:2048].rearrange("p (ic f) -> p ic f", ic=IC)
        A["RDENB"] = POOLC[0:64, D + 512:T]
        A["ENCT"] = sb("ENCT", [128, 4, 64], BF16)
        A["ROWBUF"] = sb("ROWBUF", [1, 2, 512], BF16)
        TRB128 = sb("TRANSB", [128, 512], BF16)
        A["TRANSB"] = TRB128[0:64, :]
        A["SQB2"] = TRB128
        A["RDENB2"] = TRB128[0:64, :]
        A["BT16"] = sb("BT16", [64, 512], BF16)
        A["TRANST"] = sb("TRANST", [128, 4, 64], BF16)
        DPOOL = sb("DPOOL", [128, D], BF16)
        A["DEC"] = DPOOL[0:64, :]
        A["SQB"] = DPOOL[0:64, :].rearrange("p (j f) -> p j f", j=2)
        A["RDEN"] = DPOOL[64:65, :].bitcast(F32)
        A["ONESR64"] = sb("ONESR64", [128, 64])
        A["SIGT"] = SCR2[:, 0:1024].rearrange("p (j f) -> p j f", j=2)
        A["ATT"] = SCR2[:, 1024:2048].rearrange("p (j f) -> p j f", j=2)
        A["X2P"] = SCR2[0:1, 1024:2048].rearrange("p (j f) -> p j f", j=2)
        A["WSPL"] = LNXP[:, 0:4096].rearrange("p (ic f) -> p ic f", ic=IC)
        A["WBIGH"] = sb("WBIGH", [128, IC, 512], BF16)
        A["NEGM"] = sb("NEGM", [128, 2])
        A["TMPS"] = sb("TMPS", [1, 16])
        A["BROW"] = A["INFL32"][0:1, 0:1280].bitcast(BF16)
        A["ONES512B"] = sb("ONES512B", [1, 512], BF16)
        A["ONESB"] = sb("ONESB", [128, 1], BF16)
        A["ONESRF"] = sb("ONESRF", [1, 128])
        F = {}
        F["B1HC"] = sb("B1HC", [128, 32])
        F["B1LC"] = sb("B1LC", [128, 4])
        F["GAML"] = sb("GAML", [128, IC])
        F["GAMH"] = sb("GAMH", [128, IC])
        F["RLN"] = sb("RLN", [128, NT])
        F["XRT"] = sb("XRT", [128, NT])
        F["SALL"] = sb("SALL", [128, NT])
        F["W16"] = sb("W16", [128, NT])
        F["SCO"] = sb("SCO", [128, NT])
        F["Z"] = sb("Z", [128, NT])
        F["MASKB"] = sb("MASKB", [128, NT])
        F["TIE"] = sb("TIE", [128, NT])
        F["MASK05"] = sb("MASK05", [128, NT])
        F["RED"] = sb("RED", [128, 1])
        F["L1S"] = sb("L1S", [1, 32])
        F["THB"] = sb("THB", [128, 16])
        F["NAB"] = sb("NAB", [128, 2])
        F["SRTB"] = sb("SRTB", [128, 1])
        F["ZTH"] = sb("ZTH", [128, 1])
        F["EJ"] = sb("EJ", [128, NT], BF16)
        F["IND"] = sb("IND", [128, 16, 16], BF16)
        F["CNT"] = sb("CNT", [128, 16])
        F["CROW"] = sb("CROW", [1, 16])
        F["XPWT"] = sb("XPWT", [128, 16, 32], BF16)
        F["XPBROW"] = sb("XPBROW", [1, 32], BF16)
        F["CONVW"] = sb("CONVW", [128, 16, 4])
        F["INBC"] = sb("INBC", [128, 16])
        F["CBC"] = sb("CBC", [128, 16])

        b_ = Bld(nc, ctx)
        b_.VA, b_.VB = VA, VB
        b_.F = F
        b_.RLN = F["RLN"]
        E = b_.e

        E["v"].memset(ONESR[:], 1.0)
        E["v"].memset(ONESC[:], 1.0)
        E["g"].memset(IDB[:], 0.0)
        tk0 = b_.sig("g", E["g"].memset(IDF[:], 0.0))
        b_.wait("g", tk0)
        E["g"].affine_select(out=IDB[:], in_=IDB[:], compare_op=ALU.not_equal,
                             fill=1.0, base=0, pattern=[[-1, 128]], channel_multiplier=1)
        E["g"].affine_select(out=IDF[:], in_=IDF[:], compare_op=ALU.not_equal,
                             fill=1.0, base=0, pattern=[[-1, 128]], channel_multiplier=1)
        E["g"].iota(IOTA[:], pattern=[[128, NT]], base=0, channel_multiplier=1,
                    allow_small_or_imprecise_dtypes=True)
        tok_const = b_.sig("g", E["g"].iota(IOTA16[:], pattern=[[1, 16]], base=0,
                                            channel_multiplier=0,
                                            allow_small_or_imprecise_dtypes=True))
        for _j, _v in enumerate([LN_EPS, 1e-30]):
            E["v"].memset(CONSTS[:, _j:_j + 1], _v)
            nc.const_aps.register(F32, _v, CONSTS[:, _j:_j + 1])
        E["v"].memset(A["ONES512B"][:], 1.0)
        E["v"].memset(A["ONESB"][:], 1.0)
        E["v"].memset(A["ONESRF"][:], 1.0)
        E["v"].memset(A["ONESR64"][:], 1.0)

        xsrc0 = H["L0_attn_x"][0].rearrange("(t p) d -> p t d", p=128)
        for j in range(4):
            tk_x0 = b_.dma("y", X[:, ts(j, 4), :], xsrc0[:, ts(j, 4), :], sem="xls")

        wsrc = SCRB.rearrange("p (h ic o) -> p h ic o", h=2, ic=IC, o=1024)
        tk_asm_store = None
        tk_wet_store = [None] * NE
        tk_tr_last = None
        for e in range(NE):
            src = wsrc[:, 0, :, :]
            asm = wsrc[:, 1, :, :]
            b_.wait("g", tk_tr_last)
            wesrc = H["L0_ffn_We"][e].rearrange("(oc p) i -> p oc i", p=128)
            tk_lds = []
            for q, sm in enumerate(("pp", "pp2", "pp3", "pp4")):
                tk_lds.append(b_.dma("g", src[:, 2 * q:2 * q + 2, :], wesrc[:, 2 * q:2 * q + 2, :], sem=sm))
            b_.waits("t", tk_lds + [tok_const] + (tk_asm_store or []))
            ev_toks = []
            for icb in range(IC):
                bank = b_.pb[icb % 2]
                b_.wait("t", b_.pb_free[icb % 2])
                mm = None
                for oc in range(IC):
                    mm = E["t"].transpose(bank[:, ts(oc)], src[:, oc, ts(icb)], IDB[:])
                tk_t = b_.sig("t", mm)
                tk_tr_last = tk_t
                b_.wait("v", tk_t)
                cp = E["v"].tensor_copy(out=asm[:, icb, :], in_=bank[:, 0:1024])
                tk_e = b_.sig("v", cp)
                b_.pb_free[icb % 2] = tk_e
                ev_toks.append(tk_e)
            b_.waits("y", ev_toks)
            asm_flat = asm.rearrange("p ic o -> p (ic o)")
            tk_asm_store = [b_.dma("y", wet_h[e][:, 0:4096], asm_flat[:, 0:4096], sem="ppst"),
                            b_.dma("y", wet_h[e][:, 4096:8192], asm_flat[:, 4096:8192], sem="pst2")]
            tk_wet_store[e] = tk_asm_store

        b_.dma("g", WGMOE[:], H["L0_ffn_Wg"].rearrange("(ic p) e -> p ic e", p=128), sem="mw")
        b_.dma("g", BGMOE[:], H["L0_ffn_bg"].rearrange("(u e) -> u e", u=1), sem="mw")

        SPP = LNXP[0:64, 0:2048].bitcast(F32)
        b_.dma("y", SPP, H["L0_attn_splat_positions"][:], sem="m2")
        b_.dma("y", A["NH"][:], H["L0_attn_splat_scales"].rearrange("(s u) -> s u", u=1), sem="m2")
        tk_im = b_.dma("y", A["AIMP"][:], H["L0_attn_splat_importance"].rearrange("(s u) -> s u", u=1), sem="m2")
        b_.wait("s", tk_im)
        b_.wait("s", tk_tr_last)
        tk_s2 = b_.sig("s", E["s"].activation(out=SCRB[0:64, 0:D], in_=SPP, func=AF.Square,
                                              accum_out=A["TOTR"][:]))
        b_.waits("t", [tk_s2, tk0])
        tk_t2 = b_.sig("t", E["t"].transpose(b_.ps[5][0:1, 0:64], A["TOTR"][:], IDF[0:64, 0:64]))
        b_.wait("s", tk_t2)
        tk_s2r = b_.sig("s", E["s"].activation(out=A["S2HB"][:], in_=b_.ps[5][0:1, 0:64],
                                               func=AF.Identity, scale=0.5, bias=0.0))
        for icx in range(IC):
            bank = b_.ps[icx % 2]
            b_.wait("t", b_.ps_free[icx % 2])
            tk_tt = b_.sig("t", E["t"].transpose(bank[:, 0:64], SPP[:, ts(icx)], IDF[0:64, 0:64]))
            b_.wait("s", tk_tt)
            tk_ev = b_.seq("s", E["s"].activation(out=A["SPTN"][:, icx, :], in_=bank[:, 0:64],
                                                  func=AF.Identity, scale=-1.0, bias=0.0))
            b_.ps_free[icx % 2] = tk_ev
        b_.wait("s", tk_im)
        tk_ab = b_.seq("s", E["s"].activation(out=A["NH"][:], in_=A["NH"][:], func=AF.Abs))
        tk_ab2 = b_.sig("s", E["s"].activation(out=A["AIMP"][:], in_=A["AIMP"][:], func=AF.Abs))
        b_.waits("v", [tk_ab, tk_ab2])
        b_.seq("v", E["v"].tensor_scalar_max(A["NH"][:], A["NH"][:], 1e-6))
        b_.seq("v", E["v"].tensor_tensor(out=A["NH"][:], in0=A["NH"][:], in1=A["NH"][:], op=ALU.mult))
        b_.seq("v", E["v"].reciprocal(out=A["NH"][:], in_=A["NH"][:]))
        b_.seq("v", E["v"].tensor_scalar(out=A["NH"][:], in0=A["NH"][:], scalar1=-0.5,
                                         scalar2=None, op0=ALU.mult))
        tk_moe_w = b_.dma("g", A["BT16"][:], H["L0_attn_bt"][:], sem="mw")

        b_.barrier()
        b_.dma("y", F["B1HC"][:], H["L1_ffn_b1_heavy"].rearrange("(hc p) -> p hc", p=128), sem="m2", allow_slow_non_contiguous=True)
        b_.dma("y", F["B1LC"][:], H["L1_ffn_b1_light"].rearrange("(hc p) -> p hc", p=128), sem="m2", allow_slow_non_contiguous=True)
        b_.dma("y", F["GAML"][:], H["L1_ffn_gamma_light"].rearrange("(ic p) -> p ic", p=128), sem="m2", allow_slow_non_contiguous=True)
        tk_gam = b_.dma("y", F["GAMH"][:], H["L1_ffn_gamma_heavy"].rearrange("(ic p) -> p ic", p=128), sem="m2", allow_slow_non_contiguous=True)
        b_.seq("v", E["v"].memset(F["L1S"][0:1, 0:1], 1.0))
        b_.seq("v", E["v"].tensor_scalar(out=F["L1S"][0:1, 1:2], in0=F["L1S"][0:1, 0:1],
                                         scalar1=2.0 ** -30, scalar2=-1.0, op0=ALU.add, op1=ALU.add))
        b_.seq("v", E["v"].tensor_scalar(out=F["L1S"][0:1, 1:2], in0=F["L1S"][0:1, 1:2],
                                         scalar1=0.0, scalar2=None, op0=ALU.is_gt))
        b_.seq("v", E["v"].tensor_scalar(out=F["L1S"][0:1, 1:2], in0=F["L1S"][0:1, 1:2],
                                         scalar1=-1.0, scalar2=1.0, op0=ALU.mult, op1=ALU.add))
        tk_zt = b_.seq("v", E["v"].tensor_scalar(out=F["L1S"][0:1, 1:2], in0=F["L1S"][0:1, 1:2],
                                                 scalar1=2.0 ** -126, scalar2=None, op0=ALU.mult))
        b_.wait("t", tk_zt)
        tk_ztb = b_.sig("t", E["t"].matmul(b_.ps[5][:, 0:1], A["ONESRF"][:], F["L1S"][0:1, 1:2],
                                           start=True, stop=True))
        b_.wait("v", tk_ztb)
        tk_zthc = b_.sig("v", E["v"].tensor_copy(out=F["ZTH"][:], in_=b_.ps[5][:, 0:1]))
        b_.ps_free[5] = tk_zthc
        w1src = H["L1_ffn_w1_heavy"].rearrange("(ic p) hh -> p ic hh", p=128)
        tk_pw1 = None
        tk_pw2 = None
        for hc in range(32):
            tk_pw1 = b_.dma("g", w1pk_h[hc].rearrange("p (ic f) -> p ic f", ic=IC),
                            w1src[:, :, hc * 128:(hc + 1) * 128], sem="pw1",
                            allow_slow_non_contiguous=True)
        for hc in range(32):
            tk_pw2 = b_.dma("g", w2hp_h[hc], H["L1_ffn_w2_heavy"][ts(hc), :], sem="pw2")
        tk_wtb = [None] * 8
        for s in range(64):
            tk_wtb[s // 8] = b_.dma("g", wtb_h[s].rearrange("p (ec f) -> p ec f", ec=4),
                                    H["L0_attn_Wt"][s].rearrange("(ec p) f -> p ec f", p=128),
                                    sem=f"pw{3 + s // 8}")

        b_.dma("y", F["INBC"][:], H["L1_attn_in_b"].rearrange("(c p) -> p c", p=128), sem="m2",
               allow_slow_non_contiguous=True)
        b_.dma("y", F["CBC"][:], H["L1_attn_conv_b"].rearrange("(c p) -> p c", p=128), sem="m2",
               allow_slow_non_contiguous=True)
        b_.dma("g", F["XPBROW"][:], H["L1_attn_xp_b"].rearrange("(u j) -> u j", u=1), sem="m3")
        tk_cw = b_.dma("y", F["CONVW"][:],
                       H["L1_attn_conv_w"].rearrange("(c p) one k -> p c (one k)", p=128),
                       sem="m2", allow_slow_non_contiguous=True)
        xw_stg = SCRB[0:32, 0:2048]
        b_.waits("g", tk_asm_store)
        tk_xw = b_.dma("g", xw_stg, H["L1_attn_xp_w"][:], sem="m3")
        b_.waits("t", [tk_xw, tk_cw])
        for c in range(16):
            bank = b_.pb[c % 2]
            b_.wait("t", b_.pb_free[c % 2])
            tk_t = b_.sig("t", E["t"].transpose(bank[:, 0:32], xw_stg[:, ts(c)], IDB[0:32, 0:32]))
            b_.wait("v", tk_t)
            tk_e = b_.sig("v", E["v"].tensor_copy(out=F["XPWT"][:, c, :], in_=bank[:, 0:32]))
            b_.pb_free[c % 2] = tk_e
        iw_stg = SCRB.rearrange("p (j f) -> p j f", j=8)
        tk_iw = {}
        tk_iwst = {}
        b_.wait("g", tk_e)
        for ocb in range(16):
            sl = ocb % 2
            b_.wait("g", tk_iwst.get(ocb - 2))
            tk_l = b_.dma("g", iw_stg[:, sl, 0:1024], H["L1_attn_in_w"][ts(ocb), :],
                          sem=("wa" if sl == 0 else "wb"))
            asm = iw_stg[:, 2 + sl, 0:1024].rearrange("p (dc f) -> p dc f", dc=IC)
            b_.wait("v", tk_iwst.get(ocb - 2))
            ev = None
            for dc in range(IC):
                bank = b_.pb[dc % 2]
                b_.wait("t", b_.pb_free[dc % 2])
                b_.wait("t", tk_l)
                tk_t = b_.sig("t", E["t"].transpose(bank[:, 0:128], iw_stg[:, sl, dc * 128:(dc + 1) * 128],
                                                    IDB[:]))
                b_.wait("v", tk_t)
                ev = b_.sig("v", E["v"].tensor_copy(out=asm[:, dc, 0:128], in_=bank[:, 0:128]))
                b_.pb_free[dc % 2] = ev
            b_.wait("y", ev)
            tk_iwst[ocb] = b_.dma("y", inwt_h[ocb].rearrange("p dc f -> p (dc f)"),
                                  iw_stg[:, 2 + sl, 0:1024], sem=("ta" if sl == 0 else "tb"))

        b_.barrier()

        for b in range(NB):
            if b > 0:
                xsrc = H["L0_attn_x"][b].rearrange("(t p) d -> p t d", p=128)
                for j in range(4):
                    tk_x = b_.dma("y", X[:, ts(j, 4), :], xsrc[:, ts(j, 4), :], sem="xls")
            else:
                tk_x = tk_x0
            b_.wait("v", tk_x)
            b_.wait("s", tk_x)

            n_done = 0
            if STAGES >= 1:
                emit_ln(b_, X, LNX, LNXT, SUM, SQ, VAR, RSTD, BIASC, TMP16, JUNK,
                        IDB, rms=False)
                if DBG and b == 0:
                    b_.dma("y", dbg_va[:, 0:NT], TMP16[:], sem="m2")
                    b_.dma("y", dbg_va[:, NT:2 * NT], VA[:], sem="m2")
                    b_.dma("y", dbg_va[:, 2 * NT:3 * NT], VB[:], sem="m2")
                    b_.dma("y", dbg_va[:, 3 * NT:4 * NT], VAR[:], sem="m2")
                    b_.dma("y", dbg_lnx[:], LNX[:], sem="m2")
                    b_.dma("y", dbg_lnxt[:], LNXT[:], sem="m2")
                    b_.dma("y", dbg_stat[:, 0:NT], SUM[:], sem="m2")
                    b_.dma("y", dbg_stat[:, NT:2 * NT], SQ[:], sem="m2")
                    b_.dma("y", dbg_stat[:, 2 * NT:3 * NT], RSTD[:], sem="m2")
                    b_.dma("y", dbg_stat[:, 3 * NT:4 * NT], BIASC[:], sem="m2")
                    b_.dma("y", dbg_wet[0], wet_h[0], sem="m2")
                    b_.dma("y", dbg_wet[1], wet_h[7], sem="m2")
                emit_moe(b_, nc, H, X, LNXT, SCRB, wet_h,
                         WGMOE, BEROW, BGMOE, ONESR,
                         GEXP, GSEL, G05, M8, GSUM, tk_moe_w, LNXP, IDF)
                if DBG and b == 0:
                    b_.dma("y", dbg_g05[:], G05[:], sem="m2")
                    b_.dma("y", dbg_gexp[:], GEXP[:], sem="m2")
                n_done = 1

            if STAGES >= 2:
                emit_ln(b_, X, LNX, LNXT, SUM, SQ, VAR, RSTD, BIASC, TMP16, JUNK,
                        IDB, rms=False)
                emit_l0attn(b_, nc, H, X, LNX, LNXP, LNXT, SCRB, A, IDB, ONESR, trd_h, wtb_h, tk_wtb)
                n_done = 2

            if STAGES >= 3:
                emit_ln(b_, X, LNX, LNXT, SUM, SQ, VAR, RSTD, BIASC, TMP16, JUNK,
                        IDB, rms=False)
                emit_moe(b_, nc, H, X, LNXT, SCRB, wet_h,
                         WGMOE, BEROW, BGMOE, ONESR,
                         GEXP, GSEL, G05, M8, GSUM, tk_moe_w, LNXP, IDF)
                n_done = 3

            if STAGES >= 4:
                emit_ln(b_, X, LNX, LNXT, SUM, SQ, VAR, RSTD, BIASC, TMP16, JUNK,
                        IDB, rms=True)
                emit_l1ffn(b_, nc, H, X, LNXP, LNXT, SCRB, A, F, TMP16, IOTA, IOTA16,
                           ONESC, ONESR, IDF, w1pk_h, w2hp_h, tk_pw1, tk_pw2)
                n_done = 4

            if STAGES >= 5:
                emit_ln(b_, X, LNX, LNXT, SUM, SQ, VAR, RSTD, BIASC, TMP16, JUNK,
                        IDB, rms=False)
                emit_l1attn(b_, nc, H, X, LNXP, LNXT, SCRB, A, F, ONESR, ONESC, inwt_h, POOLC, IDB)
                n_done = 5

            if STAGES >= 6:
                emit_ln(b_, X, LNX, LNXT, SUM, SQ, VAR, RSTD, BIASC, TMP16, JUNK,
                        IDB, rms=True)
                emit_l1ffn(b_, nc, H, X, LNXP, LNXT, SCRB, A, F, TMP16, IOTA, IOTA16,
                           ONESC, ONESR, IDF, w1pk_h, w2hp_h, tk_pw1, tk_pw2)
                n_done = 6

            b_.barrier()
            xdst = out_h[b].rearrange("(t p) d -> p t d", p=128)
            for j in range(4):
                tk_st = b_.dma("y", xdst[:, ts(j, 4), :], X[:, ts(j, 4), :], sem="sto")
            b_.wait("y", tk_st)
            b_.barrier()

    return nc


def emit_ln(b_, X, LNX, LNXT, SUM, SQ, VAR, RSTD, BIASC, TMP16, JUNK, IDB, rms=False):
    E = b_.e
    tk_sum = tk_sq = None
    for t in range(NT):
        tk_sum = b_.sig("v", E["v"].reduce_sum(out=SUM[:, t:t + 1], in_=X[:, t, :], axis=AX.X))
        tk_sq = b_.seq("s", E["s"].activation(out=JUNK[:, t % 2, :], in_=X[:, t, :], func=AF.Square,
                                              accum_out=SQ[:, t:t + 1]))
    b_.wait("v", tk_sq)
    b_.wait("v", tk_sum)
    VA, VB = b_.VA, b_.VB
    b_.seq("v", E["v"].tensor_scalar(out=TMP16[:], in0=SUM[:], scalar1=1.0 / D,
                                     scalar2=None, op0=ALU.mult))
    b_.seq("v", E["v"].tensor_tensor(out=VA[:], in0=TMP16[:], in1=TMP16[:], op=ALU.mult))
    b_.seq("v", E["v"].scalar_tensor_tensor(out=VB[:], in0=SQ[:], scalar=1.0 / D,
                                            in1=VA[:], op0=ALU.mult, op1=ALU.subtract))
    eps = 1e-30 if rms else LN_EPS
    tk_v = b_.seq("v", E["v"].tensor_scalar(out=VAR[:], in0=VB[:], scalar1=eps,
                                            scalar2=None, op0=ALU.add))
    b_.wait("s", tk_v)
    tk_sq2 = b_.sig("s", E["s"].activation(out=RSTD[:], in_=VAR[:], func=AF.Sqrt))
    if rms:
        RLN = b_.RLN
        tk_rl = b_.sig("s", E["s"].activation(out=RLN[:], in_=VAR[:], func=AF.Sqrt, bias=LN_EPS))
        b_.wait("v", tk_rl)
        b_.seq("v", E["v"].reciprocal(out=RLN[:], in_=RLN[:]))
    b_.wait("v", tk_sq2)
    tk_r = b_.seq("v", E["v"].reciprocal(out=RSTD[:], in_=RSTD[:]))
    b_.seq("v", E["v"].tensor_tensor(out=BIASC[:], in0=TMP16[:], in1=RSTD[:], op=ALU.mult))
    tk_b = b_.sig("v", E["v"].tensor_scalar(out=BIASC[:], in0=BIASC[:], scalar1=-1.0,
                                            scalar2=None, op0=ALU.mult))
    b_.waits("s", [tk_r, tk_b])
    for t in range(NT):
        tk_n = b_.sig("s", E["s"].activation(out=LNX[:, t, :], in_=X[:, t, :], func=AF.Identity,
                                             bias=BIASC[:, t:t + 1], scale=RSTD[:, t:t + 1]))
        b_.wait("t", tk_n)
        for half in range(2):
            bank = b_.pb[half]
            b_.wait("t", b_.pb_free[half])
            mm = None
            for j in range(4):
                icx = half * 4 + j
                mm = E["t"].transpose(bank[:, ts(j)], LNX[:, t, ts(icx)], IDB[:])
            tk_t = b_.sig("t", mm)
            b_.wait("v", tk_t)
            cp = E["v"].tensor_copy(out=LNXT[:, ts(half, 4), ts(t)], in_=bank[:, 0:512])
            tk_e = b_.sig("v", cp)
            b_.pb_free[half] = tk_e
    b_.barrier()


def emit_moe(b_, nc, H, X, LNXT, SCRB, wet_h, WGMOE, BEROW, BGMOE, ONESR,
             GEXP, GSEL, G05, M8, GSUM, tk_moe_w, LNXP, IDF):
    E = b_.e
    b_.wait("t", tk_moe_w)
    gb = 0
    ev = []
    for t in range(NT):
        bank = b_.ps[gb]
        b_.wait("t", b_.ps_free[gb])
        for icx in range(IC):
            E["t"].matmul(bank[:, 0:NE], LNXT[:, icx, ts(t)], WGMOE[:, icx, :],
                          start=(icx == 0), stop=False)
        mm = E["t"].matmul(bank[:, 0:NE], ONESR[:], BGMOE[:], start=False, stop=True)
        tk_mm = b_.sig("t", mm)
        b_.wait("v", tk_mm)
        tk_ev = b_.sig("v", E["v"].tensor_copy(out=GEXP[:, t, :], in_=bank[:, 0:NE]))
        b_.ps_free[gb] = tk_ev
        ev.append(tk_ev)
        gb = 1 - gb
    b_.waits("s", ev)
    tk_e = b_.sig("s", E["s"].activation(out=GEXP.rearrange("p t e -> p (t e)"),
                                         in_=GEXP.rearrange("p t e -> p (t e)"), func=AF.Exp))
    b_.wait("v", tk_e)
    for t in range(NT):
        mx = E["v"].max(out=M8[:, t, :], in_=GEXP[:, t, :])
    b_.wait("v", b_.sig("v", mx))
    m2 = M8[:, :, 1:2].to_broadcast([128, NT, NE])
    b_.seq("v", E["v"].tensor_tensor(out=GSEL[:], in0=GEXP[:], in1=m2, op=ALU.is_ge))
    b_.seq("v", E["v"].tensor_tensor(out=GSEL[:], in0=GEXP[:], in1=GSEL[:], op=ALU.mult))
    b_.seq("v", E["v"].reduce_sum(out=GSUM.rearrange("p (t u) -> p t u", u=1), in_=GSEL[:], axis=AX.X))
    b_.seq("v", E["v"].reciprocal(out=GSUM[:], in_=GSUM[:]))
    b_.seq("v", E["v"].tensor_scalar(out=GSUM[:], in0=GSUM[:], scalar1=0.5, scalar2=None, op0=ALU.mult))
    tk_g = b_.sig("v", E["v"].tensor_tensor(
        out=G05[:], in0=GSEL[:],
        in1=GSUM.rearrange("p (t u) -> p t u", u=1).to_broadcast([128, NT, NE]), op=ALU.mult))

    wbuf = SCRB.rearrange("p (h ic o) -> p h ic o", h=2, ic=IC, o=1024)
    BE8 = LNXP[0:8, 0:2048].bitcast(F32)
    GTE = LNXP[0:8, 2048:6144].bitcast(F32).rearrange("p (t f) -> p t f", t=NT)
    tk_be = b_.dma("g", BE8, H["L0_ffn_be"][:, :], sem="qa")
    tk_wload = [None, None]
    tk_lastmm = [None, None]
    tk_wload[0] = b_.dma("g", wbuf[:, 0, :, :],
                         wet_h[0].rearrange("p (ic o) -> p ic o", ic=IC), sem="wa")
    tk_gt = {}
    for t in range(NT):
        bank = b_.ps[4 + t % 2]
        b_.wait("t", b_.ps_free[4 + t % 2])
        b_.wait("t", tk_g)
        tk_tr = b_.sig("t", E["t"].transpose(bank[0:8, 0:128], G05[:, t, :], IDF[:]))
        b_.wait("v", tk_tr)
        tk_gt[t] = b_.sig("v", E["v"].tensor_copy(out=GTE[:, t, :], in_=bank[0:8, 0:128]))
        b_.ps_free[4 + t % 2] = tk_gt[t]
    for e in range(NE):
        h = e % 2
        if e + 1 < NE:
            b_.wait("g", tk_lastmm[1 - h])
            sm = "wa" if (1 - h) == 0 else "wb"
            tk_wload[1 - h] = b_.dma("g", wbuf[:, 1 - h, :, :],
                                     wet_h[e + 1].rearrange("p (ic o) -> p ic o", ic=IC), sem=sm)
        b_.wait("t", tk_wload[h])
        b_.wait("t", tk_g)
        we = wbuf[:, h, :, :]
        for t in range(NT):
            bk0, bk1 = (t % 2) * 2, (t % 2) * 2 + 1
            b_.wait("t", b_.ps_free[bk0])
            b_.wait("t", b_.ps_free[bk1])
            for icx in range(IC):
                E["t"].matmul(b_.ps[bk0][:], LNXT[:, icx, ts(t)], we[:, icx, 0:512],
                              start=(icx == 0), stop=(icx == IC - 1))
                mm = E["t"].matmul(b_.ps[bk1][:], LNXT[:, icx, ts(t)], we[:, icx, 512:1024],
                                   start=(icx == 0), stop=(icx == IC - 1))
            tk_mm = b_.sig("t", mm)
            tk_lastmm[h] = tk_mm
            b_.wait("v", tk_mm)
            g = G05[:, t, e:e + 1]
            E["v"].scalar_tensor_tensor(out=X[:, t, 0:512], in0=b_.ps[bk0][:], scalar=g,
                                        in1=X[:, t, 0:512], op0=ALU.mult, op1=ALU.add)
            tk_ev = b_.sig("v", E["v"].scalar_tensor_tensor(
                out=X[:, t, 512:1024], in0=b_.ps[bk1][:], scalar=g,
                in1=X[:, t, 512:1024], op0=ALU.mult, op1=ALU.add))
            b_.ps_free[bk0] = tk_ev
            b_.ps_free[bk1] = tk_ev
    for t in range(NT):
        for half in range(2):
            bank = b_.ps[(2 * t + half) % 4]
            b_.wait("t", b_.ps_free[(2 * t + half) % 4])
            b_.waits("t", [tk_be, tk_gt[t]])
            tk_mm = b_.sig("t", E["t"].matmul(bank[:], GTE[:, t, :], BE8[:, ts(half, 512)],
                                              start=True, stop=True))
            b_.wait("v", tk_mm)
            tk_ev = b_.sig("v", E["v"].tensor_tensor(out=X[:, t, ts(half, 512)], in0=bank[:],
                                                     in1=X[:, t, ts(half, 512)], op=ALU.add))
            b_.ps_free[(2 * t + half) % 4] = tk_ev
    b_.barrier()




def emit_l0attn(b_, nc, H, X, LNX, LNXP, LNXT, SCRB, A, IDB, ONESR, trd_h, wtb_h, tk_wtb):
    E = b_.e
    MIXT = SCRB.rearrange("p (c n) -> p c n", c=IC)
    V = LNXP.rearrange("p (t c) -> p t c", c=1040)
    Vh = LNXP.rearrange("p (t h c) -> p t h c", t=NT, h=16)

    tk_ib = None
    tk_d2mm = {}
    for nb in range(4):
        nbs = slice(nb * 512, (nb + 1) * 512)
        bank4 = b_.ps[4]
        b_.wait("t", b_.ps_free[4])
        tk_mm = {}
        for icx in range(IC):
            sqv = A["SIGT"][:, icx % 2, :]
            if icx >= 2:
                b_.wait("s", tk_mm[icx - 2])
            tk_sq = b_.sig("s", E["s"].activation(out=sqv, in_=LNXT[:, icx, nbs], func=AF.Square))
            b_.wait("t", tk_sq)
            tk_mm[icx] = b_.sig("t", E["t"].matmul(bank4[0:1, :], A["ONESB"][:], sqv,
                                                   start=(icx == 0), stop=(icx == IC - 1)))
        b_.wait("s", tk_mm[IC - 1])
        b_.wait("s", tk_d2mm.get(nb - 2))
        tk_x2 = b_.sig("s", E["s"].activation(out=A["X2P"][0:1, nb % 2, :], in_=bank4[0:1, :],
                                              func=AF.Identity, scale=0.5, bias=0.0))
        b_.ps_free[4] = tk_x2
        bank = b_.ps[nb % 2]
        b_.wait("t", b_.ps_free[nb % 2])
        for icx in range(IC):
            E["t"].matmul(bank[0:64, :], A["SPTN"][:, icx, :], LNXT[:, icx, nbs],
                          start=(icx == 0), stop=False)
        E["t"].matmul(bank[0:64, :], A["S2HB"][:], A["ONES512B"][:], start=False, stop=False)
        b_.wait("t", tk_x2)
        tk_mm = b_.sig("t", E["t"].matmul(bank[0:64, :], ONESR[0:1, 0:64], A["X2P"][0:1, nb % 2, :],
                                          start=False, stop=True))
        tk_d2mm[nb] = tk_mm
        b_.wait("v", tk_mm)
        tk_r = b_.seq("v", E["v"].tensor_scalar(out=A["INFL32"][:, nbs], in0=bank[0:64, :],
                                                scalar1=2.0, scalar2=0.0,
                                                op0=ALU.mult, op1=ALU.max))
        b_.ps_free[nb % 2] = tk_r
        b_.wait("s", tk_r)
        tk_e = b_.sig("s", E["s"].activation(out=A["INFL32"][:, nbs], in_=A["INFL32"][:, nbs],
                                             func=AF.Exp, scale=A["NH"][:]))
        b_.wait("v", tk_e)
        b_.seq("v", E["v"].tensor_scalar(out=A["INFL32"][:, nbs], in0=A["INFL32"][:, nbs],
                                         scalar1=A["AIMP"][:], scalar2=None, op0=ALU.mult))
        tk_ib = b_.seq("v", E["v"].tensor_copy(out=A["INFLB"][:, nbs], in_=A["INFL32"][:, nbs]))
    b_.seq("v", E["v"].reduce_sum(out=A["TOTR"][:], in_=A["INFL32"][:], axis=AX.X))
    b_.seq("v", E["v"].tensor_scalar(out=A["TOTR"][:], in0=A["TOTR"][:], scalar1=1e-8,
                                     scalar2=None, op0=ALU.max))
    tk_tot = b_.seq("v", E["v"].reciprocal(out=A["TOTR"][:], in_=A["TOTR"][:]))
    b_.wait("g", tk_tot)
    b_.dma("g", A["BROW"][0:1, 0:D], H["L0_attn_bg"].rearrange("(u d) -> u d", u=1), sem="m3")
    b_.dma("g", A["BROW"][0:1, D:2 * D], H["L0_attn_bd"].rearrange("(u d) -> u d", u=1), sem="m3")
    tk_brow = b_.dma("g", A["BROW"][0:1, 2 * D:2560], H["L0_attn_be"].rearrange("(u d) -> u d", u=1), sem="m3")
    tk_itm = None
    for g in range(2):
        bank = b_.pb[g]
        b_.wait("t", b_.pb_free[g])
        b_.wait("t", tk_ib)
        mm = None
        for j in range(8):
            t = g * 8 + j
            mm = E["t"].transpose(bank[:, j * 64:(j + 1) * 64], A["INFLB"][:, ts(t)], IDB[0:64, 0:64])
        tk_t = b_.sig("t", mm)
        b_.wait("v", tk_t)
        tk_itm = b_.sig("v", E["v"].tensor_copy(out=A["INFLTM"][:, g * 8:(g + 1) * 8, :],
                                                in_=bank[:, 0:512]))
        b_.pb_free[g] = tk_itm

    tk_g = None
    for dh in range(2):
        bank = b_.ps[2 + dh]
        b_.wait("t", b_.ps_free[2 + dh])
        b_.wait("t", tk_itm)
        for t in range(NT):
            mm = E["t"].matmul(bank[0:64, :], A["INFLTM"][:, t, :], LNX[:, t, ts(dh, 512)],
                               start=(t == 0), stop=(t == NT - 1))
        tk_mm = b_.sig("t", mm)
        b_.waits("v", [tk_mm, tk_tot])
        tk_g = b_.sig("v", E["v"].tensor_scalar(out=A["GATH"][:, ts(dh, 512)], in0=bank[0:64, :],
                                                scalar1=A["TOTR"][:], scalar2=None, op0=ALU.mult))
        b_.ps_free[2 + dh] = tk_g
    bank = b_.pb[0]
    b_.wait("t", b_.pb_free[0])
    b_.wait("t", tk_g)
    for icx in range(IC):
        mm = E["t"].transpose(bank[:, icx * 64:(icx + 1) * 64], A["GATH"][:, ts(icx)], IDB[0:64, 0:64])
    tk_t = b_.sig("t", mm)
    b_.wait("v", tk_t)
    tk_gt = b_.sig("v", E["v"].tensor_copy(out=A["GATHT"].rearrange("p ic f -> p (ic f)"),
                                           in_=bank[:, 0:512]))
    b_.pb_free[0] = tk_gt
    b_.wait("g", tk_g)
    tk_we = b_.dma("g", A["WSPL"][:], H["L0_attn_We"].rearrange("(ic p) f -> p ic f", p=128), sem="m3")
    bank = b_.ps[0]
    b_.wait("t", b_.ps_free[0])
    b_.waits("t", [tk_we, tk_gt])
    for icx in range(IC):
        E["t"].matmul(bank[0:64, :], A["GATHT"][:, icx, :], A["WSPL"][:, icx, :],
                      start=(icx == 0), stop=False)
    tk_mm = b_.sig("t", E["t"].matmul(bank[0:64, :], ONESR[0:1, 0:64], A["BROW"][0:1, 2048:2560],
                                      start=False, stop=True))
    b_.wait("s", tk_mm)
    tk_enc = b_.sig("s", E["s"].activation(out=A["ENC"][:], in_=bank[0:64, :], func=AF.Relu))
    b_.ps_free[0] = tk_enc
    bank = b_.pb[1]
    b_.wait("t", b_.pb_free[1])
    b_.wait("t", tk_enc)
    for ec in range(4):
        mm = E["t"].transpose(bank[:, ec * 64:(ec + 1) * 64], A["ENC"][:, ts(ec)], IDB[0:64, 0:64])
    tk_t = b_.sig("t", mm)
    b_.wait("v", tk_t)
    tk_enct = b_.sig("v", E["v"].tensor_copy(out=A["ENCT"].rearrange("p ic f -> p (ic f)"),
                                             in_=bank[:, 0:256]))
    b_.pb_free[1] = tk_enct

    WT4 = LNXP[:, 0:8192].rearrange("p (h ec f) -> p h ec f", h=4, ec=4)
    wt_sems = ["wa", "wb", "ha", "hb"]
    tk_wt = [None] * 4
    tk_used = [None] * 4
    tk_rowst = {0: None, 1: None}
    b_.wait("g", tk_enc)
    b_.wait("g", tk_g)

    def wt_load(s):
        sl = s % 4
        b_.wait("g", tk_used[sl])
        b_.wait("g", tk_wtb[s // 8])
        tk_wt[sl] = b_.dma("g", WT4[:, sl, :, :],
                           wtb_h[s].rearrange("p (ec f) -> p ec f", ec=4), sem=wt_sems[sl])

    for s in range(4):
        wt_load(s)
    tk_wgh = None
    gate_units = [(dc, nb) for dc in range(IC) for nb in range(4)]
    gi = 0
    tk_gate_mm = None
    tk_sig_last = None

    def gate_unit():
        nonlocal gi, tk_wgh, tk_gate_mm, tk_sig_last
        if gi >= len(gate_units):
            return
        dc, nb = gate_units[gi]
        if gi % 16 == 0:
            ob = gi // 16
            b_.wait("g", tk_gate_mm)
            tk_wgh = b_.dma("g", A["WBIGH"][:],
                            H["L0_attn_Wg"][:, ob * 512:(ob + 1) * 512]
                            .rearrange("(ic p) o -> p ic o", p=128), sem="qa")
        nbs = slice(nb * 512, (nb + 1) * 512)
        bk = b_.ps[2 + gi % 2]
        b_.wait("t", b_.ps_free[2 + gi % 2])
        b_.wait("t", tk_wgh)
        for icx in range(IC):
            E["t"].matmul(bk[:], A["WBIGH"][:, icx, ts(dc % 4)], LNXT[:, icx, nbs],
                          start=(icx == 0), stop=False)
        tk_gate_mm = b_.sig("t", E["t"].matmul(bk[:], A["BROW"][0:1, ts(dc)], A["ONES512B"][:],
                                               start=False, stop=True))
        b_.wait("s", tk_gate_mm)
        tk_sig_last = b_.sig("s", E["s"].activation(out=MIXT[:, dc, nbs], in_=bk[:],
                                                    func=AF.Sigmoid))
        b_.ps_free[2 + gi % 2] = tk_sig_last
        gi += 1

    for s in range(64):
        sl = s % 4
        hh = s % 2
        bank = b_.ps[4 + hh]
        b_.wait("t", b_.ps_free[4 + hh])
        b_.wait("t", tk_wt[sl])
        b_.wait("t", tk_enct)
        mm = None
        for ec in range(4):
            mm = E["t"].matmul(bank[0:1, :], A["ENCT"][:, ec, s:s + 1], WT4[:, sl, ec, :],
                               start=(ec == 0), stop=(ec == 3))
        tk_mm = b_.sig("t", mm)
        tk_used[sl] = tk_mm
        if s + 4 < 64:
            wt_load(s + 4)
        eng = "s" if hh == 0 else "v"
        b_.wait(eng, tk_mm)
        b_.wait(eng, tk_rowst[hh])
        if eng == "s":
            tk_ev = b_.sig("s", E["s"].activation(out=A["ROWBUF"][0:1, hh, :],
                                                  in_=bank[0:1, :], func=AF.Identity))
        else:
            tk_ev = b_.sig("v", E["v"].tensor_copy(out=A["ROWBUF"][0:1, hh, :],
                                                   in_=bank[0:1, :]))
        b_.ps_free[4 + hh] = tk_ev
        b_.wait("y", tk_ev)
        tk_rowst[hh] = b_.dma("y", trd_h[s].rearrange("(u f) -> u f", u=1),
                              A["ROWBUF"][0:1, hh, :], sem=("ta" if hh == 0 else "tb"))
        gate_unit()
    while gi < len(gate_units):
        gate_unit()

    b_.wait("g", tk_rowst[0])
    b_.wait("g", tk_rowst[1])
    tk_sc = b_.dma("g", A["TRANSB"][:], trd_h[:], sem="m4")
    b_.wait("v", tk_sc)
    tk_tr = b_.seq("v", E["v"].tensor_tensor(out=A["TRANSB"][:], in0=A["TRANSB"][:],
                                             in1=A["BT16"][:], op=ALU.add))
    bank = b_.pb[0]
    b_.wait("t", b_.pb_free[0])
    b_.wait("t", tk_tr)
    for ec in range(4):
        mm = E["t"].transpose(bank[:, ec * 64:(ec + 1) * 64], A["TRANSB"][:, ts(ec)], IDB[0:64, 0:64])
    tk_t = b_.sig("t", mm)
    b_.wait("v", tk_t)
    tk_trt = b_.sig("v", E["v"].tensor_copy(out=A["TRANST"].rearrange("p ic f -> p (ic f)"),
                                            in_=bank[:, 0:256]))
    b_.pb_free[0] = tk_trt
    b_.wait("g", tk_used[0])
    b_.wait("g", tk_used[1])
    WD4 = A["WSPL"].rearrange("p ic f -> p (ic f)").rearrange("p (ec f) -> p ec f", ec=4)
    tk_wd = b_.dma("g", WD4[:], H["L0_attn_Wd"].rearrange("(ec p) f -> p ec f", p=128), sem="m3")
    tk_dec = None
    for dh in range(2):
        bank = b_.ps[dh]
        b_.wait("t", b_.ps_free[dh])
        b_.waits("t", [tk_wd, tk_trt])
        for ec in range(4):
            E["t"].matmul(bank[0:64, :], A["TRANST"][:, ec, :], WD4[:, ec, ts(dh, 512)],
                          start=(ec == 0), stop=False)
        tk_mm = b_.sig("t", E["t"].matmul(bank[0:64, :], ONESR[0:1, 0:64],
                                          A["BROW"][0:1, D + dh * 512:D + (dh + 1) * 512],
                                          start=False, stop=True))
        b_.wait("v", tk_mm)
        tk_dec = b_.sig("v", E["v"].tensor_copy(out=A["DEC"][:, ts(dh, 512)], in_=bank[0:64, :]))
        b_.ps_free[dh] = tk_dec

    idx = 0
    tk_mix04 = None
    for dc in range(IC):
        for nb in range(4):
            nbs = slice(nb * 512, (nb + 1) * 512)
            bf_ = b_.ps[2 + idx % 2]
            b_.wait("t", b_.ps_free[2 + idx % 2])
            b_.wait("t", tk_dec)
            tk_fmm = b_.sig("t", E["t"].matmul(bf_[:], A["DEC"][:, ts(dc)], A["INFLB"][:, nbs],
                                               start=True, stop=True))
            b_.waits("v", [tk_fmm, tk_sig_last])
            tk_mix04 = b_.sig("v", E["v"].scalar_tensor_tensor(out=MIXT[:, dc, nbs], in0=bf_[:],
                                                               scalar=0.4, in1=MIXT[:, dc, nbs],
                                                               op0=ALU.mult, op1=ALU.mult))
            b_.ps_free[2 + idx % 2] = tk_mix04
            idx += 1

    tk_vone = b_.sig("v", E["v"].memset(Vh[:, :, :, 64:65], 1.0))
    tk_vev = None
    tk_wv = None
    for ob in range(2):
        b_.wait("g", tk_gate_mm if ob == 0 else tk_vused)
        tk_wv = b_.dma("g", A["WBIGH"][:],
                       H["L0_attn_Wv"][:, ob * 512:(ob + 1) * 512]
                       .rearrange("(ic p) o -> p ic o", p=128), sem="qa")
        for t in range(NT):
            bank = b_.ps[t % 2]
            b_.wait("t", b_.ps_free[t % 2])
            b_.wait("t", tk_wv)
            for icx in range(IC):
                mm = E["t"].matmul(bank[:], LNXT[:, icx, ts(t)], A["WBIGH"][:, icx, :],
                                   start=(icx == 0), stop=(icx == IC - 1))
            tk_mm = b_.sig("t", mm)
            tk_vused = tk_mm
            b_.waits("v", [tk_mm, tk_vone])
            dst = Vh[:, t, ob * 8:(ob + 1) * 8, 0:64]
            tk_vev = b_.sig("v", E["v"].tensor_copy(out=dst, in_=bank[:]))
            b_.ps_free[t % 2] = tk_vev

    Q2 = A["Q2"]
    K2 = A["K2"]
    ATTS = [A["SIGT"][:, 0, :], A["SIGT"][:, 1, :], A["ATT"][:, 0, :], A["ATT"][:, 1, :]]
    SQV = A["SQB2"]
    st = {"qk_used": tk_vused, "negm": None, "lastexp": None, "sqmm": None}

    def emit_qk_dma(h):
        b_.wait("g", st["qk_used"])
        b_.wait("g", tk_gt)
        hs = slice(64 * h, 64 * (h + 1))
        wq = H["L0_attn_Wq"][:, hs].rearrange("(ic p) o -> p ic o", p=128)
        wk = H["L0_attn_Wk"][:, hs].rearrange("(ic p) o -> p ic o", p=128)
        b_.dma("g", A["WQS"][:, :, 0:64], wq, sem="qa")
        st["dq", h] = b_.dma("g", A["WQS"][:, :, 64:128], wq, sem="qa")
        b_.dma("g", A["WKS"][:, :, 0:64], wk, sem="qb")
        st["dk", h] = b_.dma("g", A["WKS"][:, :, 64:128], wk, sem="qb")

    def emit_proj_stab(h, fin_hook=None):
        mxq = A["TMPS"][0:1, 0:1]
        mxk = A["TMPS"][0:1, 1:2]
        tmp = A["TMPS"][0:1, 2:3]
        pending = []

        def pop_stab():
            j, nb, tk_s2 = pending.pop(0)
            mx = mxq if j == 0 else mxk
            pbk = b_.pb[nb % 2][:].bitcast(F32)
            b_.wait("t", b_.pb_free[nb % 2])
            b_.wait("t", tk_s2)
            tk_nm = b_.sig("t", E["t"].matmul(pbk[0:1, 0:512], A["ONESB"][:], ATTS[nb],
                                              start=True, stop=True))
            st["slot_rd", nb] = tk_nm
            b_.wait("v", tk_nm)
            if j == 0 and nb == 0:
                tk_r = b_.seq("v", E["v"].reduce_max(out=mx, in_=pbk[0:1, 0:512], axis=AX.X))
            else:
                tk_r = b_.seq("v", E["v"].reduce_max(out=tmp, in_=pbk[0:1, 0:512], axis=AX.X))
                tk_r = b_.seq("v", E["v"].tensor_tensor(out=mx, in0=(mx if nb > 0 else tmp),
                                                        in1=tmp, op=ALU.max))
            b_.pb_free[nb % 2] = tk_r

        for j, (which, WS, OUT) in enumerate((("q", A["WQS"], Q2), ("k", A["WKS"], K2))):
            for nb in range(4):
                nbs = slice(nb * 512, (nb + 1) * 512)
                bank = b_.ps[5]
                b_.wait("t", b_.ps_free[5])
                b_.wait("t", st["dq", h] if which == "q" else st["dk", h])
                for icx in range(IC):
                    mm = E["t"].matmul(bank[:], WS[:, icx, :], LNXT[:, icx, nbs],
                                       start=(icx == 0), stop=(icx == IC - 1))
                tk_mm = b_.sig("t", mm)
                if which == "k" and nb == 3:
                    st["qk_used"] = tk_mm
                b_.wait("v", tk_mm)
                tk_ev = b_.sig("v", E["v"].tensor_copy(out=OUT[:, nbs], in_=bank[:]))
                b_.ps_free[5] = tk_ev
                b_.wait("v", st.get(("slot_rd", nb)))
                b_.wait("v", tk_ev)
                tk_s2 = b_.seq("v", E["v"].tensor_tensor(out=ATTS[nb], in0=OUT[:, nbs],
                                                         in1=OUT[:, nbs], op=ALU.mult))
                pending.append((j, nb, tk_s2))
                if len(pending) > 2:
                    pop_stab()
                if j == 0 and nb == 1 and fin_hook is not None:
                    fin_hook()
        while pending:
            pop_stab()
        ms = A["TMPS"][0:1, 3:4]
        tk_m = b_.seq("v", E["v"].tensor_tensor(out=ms, in0=mxq, in1=mxk, op=ALU.add))
        tk_m = b_.seq("v", E["v"].tensor_scalar(out=ms, in0=ms, scalar1=-0.03125,
                                                scalar2=None, op0=ALU.mult))
        bank = b_.pb[0][:].bitcast(F32)
        b_.wait("t", b_.pb_free[0])
        b_.wait("t", tk_m)
        tk_bc = b_.sig("t", E["t"].matmul(bank[:, 0:1], A["ONESRF"][:], ms,
                                          start=True, stop=True))
        b_.waits("v", [tk_bc, st["lastexp"]])
        st["negm"] = b_.sig("v", E["v"].tensor_copy(out=A["NEGM"][:, 0:1], in_=bank[:, 0:1]))
        b_.pb_free[0] = st["negm"]

    def emit_fin(h, qq):
        qqs = slice(qq * 512, (qq + 1) * 512)
        b_.seq("v", E["v"].reciprocal(out=A["RDEN"][:], in_=A["RDEN"][:]))
        tk_rd = b_.seq("v", E["v"].tensor_scalar(out=A["RDEN"][:], in0=A["RDEN"][:],
                                                 scalar1=0.6, scalar2=None, op0=ALU.mult))
        bankb = b_.pb[1][:].bitcast(F32)
        b_.wait("t", b_.pb_free[1])
        b_.wait("t", tk_rd)
        tk_bc = b_.sig("t", E["t"].matmul(bankb[0:64, 0:512], A["ONESR64"][64:65, :], A["RDEN"][:],
                                          start=True, stop=True))
        b_.wait("v", tk_bc)
        tk_rb = b_.seq("v", E["v"].tensor_copy(out=A["RDENB2"][:], in_=bankb[0:64, 0:512]))
        tk_avs = b_.seq("v", E["v"].tensor_tensor(out=A["AVS"][:], in0=A["AVN"][:],
                                                  in1=A["RDENB2"][:], op=ALU.mult))
        b_.wait("v", tk_mix04)
        if h % 2 == 0:
            tk_acc = b_.sig("v", E["v"].tensor_tensor(out=MIXT[0:64, h // 2, qqs],
                                                      in0=A["AVS"][:],
                                                      in1=MIXT[0:64, h // 2, qqs],
                                                      op=ALU.add))
            b_.pb_free[1] = tk_rb
        else:
            b_.wait("t", tk_avs)
            tk_cp = b_.sig("t", E["t"].matmul(bankb[64:128, 0:512], IDB[0:64, 0:64], A["AVS"][:],
                                              start=True, stop=True, tile_position=(0, 64)))
            b_.wait("v", tk_cp)
            tk_acc = b_.sig("v", E["v"].tensor_tensor(out=MIXT[64:128, h // 2, qqs],
                                                      in0=bankb[64:128, 0:512],
                                                      in1=MIXT[64:128, h // 2, qqs],
                                                      op=ALU.add))
            b_.pb_free[1] = tk_acc
        st["acc"] = tk_acc
        st["avs_done"] = tk_avs

    def pop_fin():
        if st.get("fin") is not None:
            fh, fqq = st["fin"]
            st["fin"] = None
            emit_fin(fh, fqq)

    emit_qk_dma(0)
    st["fin"] = None
    st["acc"] = None
    for h in range(16):
        emit_proj_stab(h, fin_hook=pop_fin)
        if h + 1 < 16:
            emit_qk_dma(h + 1)
        tk_negm = st["negm"]
        for qq in range(4):
            qqs = slice(qq * 512, (qq + 1) * 512)
            av = b_.ps[2 + qq % 2]
            b_.wait("t", b_.ps_free[2 + qq % 2])
            tk_av = {}
            tk_sc = {}
            SCB = (0, 1, 4)

            def emit_scores(kt):
                sb_ = b_.ps[SCB[kt % 3]]
                b_.wait("t", b_.ps_free[SCB[kt % 3]])
                tk_sc[kt] = b_.sig("t", E["t"].matmul(sb_[:], K2[:, ts(kt)], Q2[:, qqs],
                                                      start=True, stop=True))

            def emit_exp(j):
                b_.waits("s", [tk_sc[j], tk_negm])
                if j >= 4:
                    b_.wait("s", tk_av[j - 4])
                tk_e = b_.sig("s", E["s"].activation(out=ATTS[j % 4], in_=b_.ps[SCB[j % 3]][:],
                                                     func=AF.Exp, scale=0.0625,
                                                     bias=A["NEGM"][:, 0:1]))
                st["lastexp"] = tk_e
                tk_exp[j] = tk_e
                b_.ps_free[SCB[j % 3]] = tk_e

            tk_exp = {}
            emit_scores(0)
            emit_scores(1)
            emit_scores(2)
            emit_exp(0)
            for kt in range(NT):
                if kt + 3 < NT:
                    emit_scores(kt + 3)
                if kt + 1 < NT:
                    emit_exp(kt + 1)
                b_.wait("t", tk_exp[kt])
                tk_av[kt] = b_.sig("t", E["t"].matmul(av[0:65, :], V[:, kt, 65 * h:65 * h + 65],
                                                      ATTS[kt % 4],
                                                      start=(kt == 0), stop=(kt == NT - 1)))
                if kt == 5:
                    pop_fin()
            b_.wait("v", tk_av[NT - 1])
            b_.wait("v", st.get("avs_done"))
            b_.seq("v", E["v"].tensor_copy(out=A["RDEN"][:], in_=av[64:65, :]))
            tk_an = b_.seq("v", E["v"].tensor_copy(out=A["AVN"][:], in_=av[0:64, :]))
            b_.ps_free[2 + qq % 2] = tk_an
            st["fin"] = (h, qq)
    pop_fin()
    tk_heads = st["acc"]

    tk_wo_used = tk_vused
    for dhf in range(2):
        b_.wait("g", tk_wo_used)
        tk_wo = b_.dma("g", A["WBIGH"][:],
                       H["L0_attn_Wo"][:, dhf * 512:(dhf + 1) * 512]
                       .rearrange("(ic p) o -> p ic o", p=128), sem="m4")
        for t in range(NT):
            bank = b_.ps[t % 2]
            b_.wait("t", b_.ps_free[t % 2])
            b_.waits("t", [tk_wo, tk_heads])
            for dc in range(IC):
                mm = E["t"].matmul(bank[:], MIXT[:, dc, ts(t)], A["WBIGH"][:, dc, :],
                                   start=(dc == 0), stop=(dc == IC - 1))
            tk_mm = b_.sig("t", mm)
            tk_wo_used = tk_mm
            b_.wait("v", tk_mm)
            tk_x = b_.sig("v", E["v"].tensor_tensor(out=X[:, t, ts(dhf, 512)], in0=bank[:],
                                                    in1=X[:, t, ts(dhf, 512)], op=ALU.add))
            b_.ps_free[t % 2] = tk_x
    b_.barrier()




C_ROUTE = 0.1 * math.log(8.0)


def emit_l1ffn(b_, nc, H, X, LNXP, LNXT, SCRB, A, F, TMP16, IOTA, IOTA16, ONESC, ONESR, IDF, w1pk_h, w2hp_h, tk_pw1, tk_pw2):
    E = b_.e
    eps_i = 10.0
    W1PKs = [LNXP[:, 0:1024].rearrange("p (ic f) -> p ic f", ic=IC),
             LNXP[:, 1024:2048].rearrange("p (ic f) -> p ic f", ic=IC),
             LNXP[:, 12288:13312].rearrange("p (ic f) -> p ic f", ic=IC),
             LNXP[:, 13312:14336].rearrange("p (ic f) -> p ic f", ic=IC)]
    W2BF = LNXP[:, 2048:4096].rearrange("p (j f) -> p j f", j=4)
    W1L = LNXP[:, 4096:8192].rearrange("p (ic f) -> p ic f", ic=IC)
    W2L = LNXP[:, 8192:12288].rearrange("p (hc f) -> p hc f", hc=4)
    RTM = LNXP[:, 12288:14336].bitcast(F32)
    B2L = LNXP[0:1, 14336:15360]
    B2H = LNXP[0:1, 15360:16384]
    H1T = SCRB.rearrange("p (hc f) -> p hc f", hc=4)
    HHT = SCRB.rearrange("p (hc f) -> p hc f", hc=32)
    L1S = F["L1S"]

    def bcast128(src11, dst, tag):
        bank = b_.ps[5]
        b_.wait("t", b_.ps_free[5])
        b_.wait("t", tag)
        tk = b_.sig("t", E["t"].matmul(bank[:, 0:1], A["ONESRF"][:], src11, start=True, stop=True))
        b_.wait("v", tk)
        tk2 = b_.seq("v", E["v"].tensor_copy(out=dst, in_=bank[:, 0:1]))
        b_.ps_free[5] = tk2
        return tk2

    def psum_row(src, dst11, red, tag_v):
        tk = b_.seq("v", E["v"].reduce_sum(out=red, in_=src, axis=AX.X))
        bank = b_.ps[5]
        b_.wait("t", b_.ps_free[5])
        b_.wait("t", tk)
        tk2 = b_.sig("t", E["t"].matmul(bank[0:1, 0:1], ONESC[:], red, start=True, stop=True))
        b_.wait("v", tk2)
        tk3 = b_.seq("v", E["v"].tensor_copy(out=dst11, in_=bank[0:1, 0:1]))
        b_.ps_free[5] = tk3
        return tk3

    w1_sems = ["wa", "wb", "ha", "hb"]
    w2_sems = ["qa", "qb", "hg", "hh"]
    st1 = {"n": 0, "rd": [None] * 4, "ld": [None] * 4}
    st2 = {"n": 0, "rd": [None] * 4, "ld": [None] * 4}

    def issue1():
        if st1["n"] >= 128:
            return
        k = st1["n"]
        sl = k % 4
        b_.wait("g", st1["rd"][sl])
        st1["ld"][sl] = b_.dma("g", W1PKs[sl][:],
                               w1pk_h[k % 32].rearrange("p (ic f) -> p ic f", ic=IC),
                               sem=w1_sems[sl])
        st1["n"] += 1

    def issue2():
        if st2["n"] >= 256:
            return
        j = st2["n"]
        hc2 = j % 32
        half2 = (j // 32) % 2
        sl = j % 4
        b_.wait("g", st2["rd"][sl])
        st2["ld"][sl] = b_.dma("g", W2BF[:, sl, :], w2hp_h[hc2][:, ts(half2, 512)], sem=w2_sems[sl])
        st2["n"] += 1

    tk_rt = b_.dma("y", RTM[0:1, :], H["L1_ffn_routing_token"].rearrange("(u d) -> u d", u=1), sem="m2")
    b_.dma("g", B2L, H["L1_ffn_b2_light"].rearrange("(u d) -> u d", u=1), sem="m3")
    tk_b2 = b_.dma("g", B2H, H["L1_ffn_b2_heavy"].rearrange("(u d) -> u d", u=1), sem="m3")
    b_.wait("g", tk_pw1)
    b_.wait("g", tk_pw2)
    issue1()
    issue1()
    for _ in range(4):
        issue2()
    tk_w2l = b_.dma("g", W2L[:], H["L1_ffn_w2_light"].rearrange("(hc p) o -> p hc o", p=128), sem="m4")
    tk_wl = b_.dma("g", W1L[:], H["L1_ffn_w1_light"].rearrange("(ic p) o -> p ic o", p=128), sem="m4")
    b_.wait("v", tk_wl)
    tk_gl = None
    for icx in range(IC):
        tk_gl = b_.seq("v", E["v"].tensor_scalar(out=W1L[:, icx, :], in0=W1L[:, icx, :],
                                                 scalar1=F["GAML"][:, icx:icx + 1],
                                                 scalar2=None, op0=ALU.mult))
    b_.wait("t", tk_rt)
    for half in range(2):
        bank = b_.ps[4]
        b_.wait("t", b_.ps_free[4])
        tk = b_.sig("t", E["t"].matmul(bank[:], A["ONESRF"][:], RTM[0:1, ts(half, 512)],
                                       start=True, stop=True))
        b_.wait("v", tk)
        tk_rtm = b_.seq("v", E["v"].tensor_copy(out=RTM[:, ts(half, 512)], in_=bank[:]))
        b_.ps_free[4] = tk_rtm
    tk_s1 = b_.seq("v", E["v"].reduce_sum(out=L1S[0:1, 2:3], in_=RTM[0:1, :], axis=AX.X))
    tk_srt = bcast128(L1S[0:1, 2:3], F["SRTB"][:], tk_s1)
    tk_x = None
    for t in range(NT):
        tk_x = b_.seq("v", E["v"].scalar_tensor_tensor(out=SCRB[:, 0:2048].bitcast(F32),
                                                       in0=X[:, t, :], scalar=1.0,
                                                       in1=RTM[:], op0=ALU.bypass, op1=ALU.mult,
                                                       accum_out=F["XRT"][:, t:t + 1]))
    b_.seq("v", E["v"].tensor_tensor(out=F["MASKB"][:], in0=TMP16[:],
                                     in1=F["SRTB"][:].to_broadcast([128, NT]), op=ALU.mult))
    b_.seq("v", E["v"].tensor_tensor(out=F["SALL"][:], in0=F["XRT"][:], in1=F["MASKB"][:], op=ALU.subtract))
    tk_s = b_.seq("v", E["v"].tensor_tensor(out=F["SALL"][:], in0=F["SALL"][:], in1=F["RLN"][:], op=ALU.mult))
    tk1 = b_.seq("v", E["v"].reduce_max(out=F["RED"][:], in_=F["SALL"][:], axis=AX.X))
    bank = b_.ps[4]
    b_.wait("t", b_.ps_free[4])
    b_.wait("t", tk1)
    tkt = b_.sig("t", E["t"].transpose(bank[0:1, 0:128], F["RED"][:], IDF[:]))
    b_.wait("v", tkt)
    tk2 = b_.seq("v", E["v"].reduce_max(out=L1S[0:1, 3:4], in_=bank[0:1, 0:128], axis=AX.X))
    b_.ps_free[4] = tk2
    b_.wait("g", tk_x)
    issue1()
    issue1()

    cell = {"h1": None, "mm1last": None}

    def u_mm1(oc, nb):
        bank_ = b_.ps[nb]
        b_.wait("t", b_.ps_free[nb])
        b_.wait("t", tk_gl)
        for icx in range(IC):
            mm = E["t"].matmul(bank_[:], W1L[:, icx, ts(oc)], LNXT[:, icx, slice(nb * 512, nb * 512 + 512)],
                               start=(icx == 0), stop=(icx == IC - 1))
        tk_mm = b_.sig("t", mm)
        cell["mm1last"] = tk_mm
        b_.wait("v", tk_mm)
        tk_h1 = b_.sig("v", E["v"].tensor_copy(out=H1T[:, oc, slice(2048 + nb * 512, 2048 + nb * 512 + 512)],
                                               in_=bank_[:]))
        cell["pre", oc, nb] = tk_h1
        b_.ps_free[nb] = tk_h1

    def u_mm2(t, half):
        bank_ = b_.ps[(2 * t + half) % 4]
        b_.wait("t", b_.ps_free[(2 * t + half) % 4])
        b_.waits("t", [tk_wl, cell["h1"]])
        for hc in range(4):
            E["t"].matmul(bank_[:], H1T[:, hc, ts(t)], W2L[:, hc, ts(half, 512)],
                          start=(hc == 0), stop=False)
        b_.wait("t", tk_b2)
        tk_mm = b_.sig("t", E["t"].matmul(bank_[:], ONESR[0:1, :], B2L[0:1, ts(half, 512)],
                                          start=False, stop=True))
        b_.wait("v", tk_mm)
        tk_xl = b_.sig("v", E["v"].scalar_tensor_tensor(out=X[:, t, ts(half, 512)], in0=bank_[:],
                                                        scalar=0.5, in1=X[:, t, ts(half, 512)],
                                                        op0=ALU.mult, op1=ALU.add))
        b_.ps_free[(2 * t + half) % 4] = tk_xl

    fillers = [(lambda oc=oc, nb=nb: u_mm1(oc, nb)) for oc in range(4) for nb in range(4)]
    fillers_mm2 = [(lambda t=t, half=half: u_mm2(t, half)) for t in range(NT) for half in range(2)]

    def fill(n=1):
        for _ in range(n):
            if fillers:
                fillers.pop(0)()
    b_.seq("v", E["v"].memset(L1S[0:1, 0:1], 0.0))
    tk_a = None
    for it in range(50):
        b_.seq("v", E["v"].tensor_scalar(out=L1S[0:1, 5:6], in0=L1S[0:1, 0:1], scalar1=-1.0,
                                         scalar2=None, op0=ALU.mult))
        b_.seq("v", E["v"].tensor_tensor(out=L1S[0:1, 6:7], in0=L1S[0:1, 3:4],
                                         in1=L1S[0:1, 5:6], op=ALU.min))
        tk_nab = b_.seq("v", E["v"].tensor_scalar(out=L1S[0:1, 7:8], in0=L1S[0:1, 6:7],
                                                  scalar1=-eps_i, scalar2=None, op0=ALU.mult))
        bank = b_.ps[5]
        b_.wait("t", b_.ps_free[5])
        b_.wait("t", tk_nab)
        tkb = b_.sig("t", E["t"].matmul(bank[:, 0:2], A["ONESRF"][:], L1S[0:1, 5:8:2],
                                        start=True, stop=True))
        b_.wait("v", tkb)
        tk_nb = b_.seq("v", E["v"].tensor_copy(out=F["NAB"][:], in_=bank[:, 0:2]))
        b_.ps_free[5] = tk_nb
        tk_u = b_.seq("v", E["v"].tensor_scalar(out=F["W16"][:], in0=F["SALL"][:],
                                                scalar1=F["NAB"][:, 0:1], scalar2=None, op0=ALU.min))
        b_.wait("s", tk_u)
        tk_e = b_.sig("s", E["s"].activation(out=F["EJ"][:], in_=F["W16"][:], func=AF.Exp,
                                             scale=eps_i, bias=F["NAB"][:, 1:2],
                                             accum_out=F["RED"][:]))
        bank = b_.ps[5]
        b_.wait("t", b_.ps_free[5])
        b_.wait("t", tk_e)
        tks = b_.sig("t", E["t"].matmul(bank[0:1, 0:1], ONESC[:], F["RED"][:], start=True, stop=True))
        b_.wait("s", tks)
        tk_ln = b_.sig("s", E["s"].activation(out=L1S[0:1, 8:9], in_=bank[0:1, 0:1], func=AF.Ln))
        b_.ps_free[5] = tk_ln
        b_.wait("v", tk_ln)
        b_.seq("v", E["v"].tensor_scalar(out=L1S[0:1, 8:9], in0=L1S[0:1, 8:9], scalar1=-0.1,
                                         scalar2=C_ROUTE, op0=ALU.mult, op1=ALU.add))
        tk_a = b_.seq("v", E["v"].tensor_tensor(out=L1S[0:1, 0:1], in0=L1S[0:1, 8:9],
                                                in1=L1S[0:1, 6:7], op=ALU.subtract))
        fill(1)
    tk_ab = bcast128(L1S[0:1, 0:1], F["NAB"][:, 0:1], tk_a)
    b_.seq("v", E["v"].tensor_scalar(out=F["W16"][:], in0=F["SALL"][:], scalar1=F["NAB"][:, 0:1],
                                     scalar2=0.0, op0=ALU.add, op1=ALU.min))
    tk_w = b_.seq("v", E["v"].tensor_scalar(out=F["W16"][:], in0=F["W16"][:], scalar1=10.0,
                                            scalar2=None, op0=ALU.mult))
    b_.wait("s", tk_w)
    tk_sc = b_.sig("s", E["s"].activation(out=F["SCO"][:], in_=F["W16"][:], func=AF.Exp))
    b_.wait("v", tk_sc)
    for oc in range(4):
        for nb in range(4):
            b_.waits("s", [cell["pre", oc, nb], tk_x])
            cell["h1"] = b_.sig("s", E["s"].activation(
                out=H1T[:, oc, slice(nb * 512, nb * 512 + 512)],
                in_=H1T[:, oc, slice(2048 + nb * 512, 2048 + nb * 512 + 512)],
                func=AF.Gelu, bias=F["B1LC"][:, oc:oc + 1]))
    fillers.extend(fillers_mm2)
    b_.wait("v", cell["mm1last"])
    tk_gsc = None
    for icx in range(IC):
        tk_gsc = b_.seq("v", E["v"].tensor_scalar(out=LNXT[:, icx, :], in0=LNXT[:, icx, :],
                                                  scalar1=F["GAMH"][:, icx:icx + 1],
                                                  scalar2=None, op0=ALU.mult))
    b_.seq("v", E["v"].tensor_scalar(out=F["Z"][:], in0=F["SCO"][:], scalar1=F["ZTH"][:, 0:1],
                                     scalar2=None, op0=ALU.is_lt))
    b_.seq("v", E["v"].tensor_scalar(out=F["MASKB"][:], in0=F["SCO"][:], scalar1=0.0,
                                     scalar2=None, op0=ALU.is_equal))
    tk_z = b_.seq("v", E["v"].tensor_tensor(out=F["Z"][:], in0=F["Z"][:], in1=F["MASKB"][:],
                                            op=ALU.logical_or))
    tk_nz = psum_row(F["Z"][:], L1S[0:1, 9:10], F["RED"][:], tk_z)
    b_.seq("v", E["v"].tensor_scalar(out=L1S[0:1, 9:10], in0=L1S[0:1, 9:10], scalar1=-1.0,
                                     scalar2=2048.0, op0=ALU.mult, op1=ALU.add))
    tk_gg = b_.seq("v", E["v"].tensor_scalar(out=L1S[0:1, 10:11], in0=L1S[0:1, 9:10],
                                             scalar1=1024.0, scalar2=None, op0=ALU.is_ge))
    b_.seq("v", E["v"].memset(L1S[0:1, 11:12], -21000.0))
    b_.seq("v", E["v"].memset(L1S[0:1, 12:13], 1.0))
    for lev in range(8):
        b_.seq("v", E["v"].tensor_tensor(out=L1S[0:1, 13:14], in0=L1S[0:1, 12:13],
                                         in1=L1S[0:1, 11:12], op=ALU.subtract))
        b_.seq("v", E["v"].tensor_scalar(out=L1S[0:1, 13:14], in0=L1S[0:1, 13:14],
                                         scalar1=1.0 / 16, scalar2=None, op0=ALU.mult))
        tk_tr = b_.seq("v", E["v"].tensor_scalar(out=F["CROW"][:], in0=IOTA16[:],
                                                 scalar1=L1S[0:1, 13:14], scalar2=L1S[0:1, 11:12],
                                                 op0=ALU.mult, op1=ALU.add))
        bank = b_.ps[5]
        b_.wait("t", b_.ps_free[5])
        b_.wait("t", tk_tr)
        tkb = b_.sig("t", E["t"].matmul(bank[:, 0:16], A["ONESRF"][:], F["CROW"][:],
                                        start=True, stop=True))
        b_.wait("v", tkb)
        tk_th = b_.seq("v", E["v"].tensor_copy(out=F["THB"][:], in_=bank[:, 0:16]))
        b_.ps_free[5] = tk_th
        b_.seq("v", E["v"].tensor_tensor(
            out=F["IND"][:],
            in0=F["W16"].rearrange("p (u t) -> p u t", u=1).to_broadcast([128, 16, NT]),
            in1=F["THB"].rearrange("p (j u) -> p j u", u=1).to_broadcast([128, 16, NT]),
            op=ALU.is_ge))
        tk_c = b_.seq("v", E["v"].reduce_sum(out=F["CNT"].rearrange("p (j u) -> p j u", u=1),
                                             in_=F["IND"][:], axis=AX.X))
        bank = b_.ps[5]
        b_.wait("t", b_.ps_free[5])
        b_.wait("t", tk_c)
        tks = b_.sig("t", E["t"].matmul(bank[0:1, 0:16], ONESC[:], F["CNT"][:], start=True, stop=True))
        b_.wait("v", tks)
        tk_cr = b_.seq("v", E["v"].tensor_copy(out=F["CROW"][:], in_=bank[0:1, 0:16]))
        b_.ps_free[5] = tk_cr
        b_.seq("v", E["v"].tensor_scalar(out=F["CROW"][:], in0=F["CROW"][:], scalar1=1024.0,
                                         scalar2=None, op0=ALU.is_ge))
        b_.seq("v", E["v"].reduce_sum(out=L1S[0:1, 14:15], in_=F["CROW"][:], axis=AX.X))
        b_.seq("v", E["v"].tensor_scalar(out=L1S[0:1, 14:15], in0=L1S[0:1, 14:15], scalar1=-1.0,
                                         scalar2=None, op0=ALU.add))
        b_.seq("v", E["v"].tensor_tensor(out=L1S[0:1, 15:16], in0=L1S[0:1, 14:15],
                                         in1=L1S[0:1, 13:14], op=ALU.mult))
        b_.seq("v", E["v"].tensor_tensor(out=L1S[0:1, 11:12], in0=L1S[0:1, 11:12],
                                         in1=L1S[0:1, 15:16], op=ALU.add))
        tk_lo = b_.seq("v", E["v"].tensor_tensor(out=L1S[0:1, 12:13], in0=L1S[0:1, 11:12],
                                                 in1=L1S[0:1, 13:14], op=ALU.add))
        fill(2)
    tk_tb = bcast128(L1S[0:1, 11:12], F["NAB"][:, 1:2], tk_lo)
    b_.seq("v", E["v"].tensor_scalar(out=F["MASKB"][:], in0=F["W16"][:], scalar1=F["NAB"][:, 1:2],
                                     scalar2=None, op0=ALU.is_gt))
    b_.seq("v", E["v"].tensor_scalar(out=F["TIE"][:], in0=F["W16"][:], scalar1=F["NAB"][:, 1:2],
                                     scalar2=None, op0=ALU.is_ge))
    b_.seq("v", E["v"].tensor_tensor(out=F["TIE"][:], in0=F["TIE"][:], in1=F["MASKB"][:],
                                     op=ALU.subtract))
    tk_gb = bcast128(L1S[0:1, 10:11], F["NAB"][:, 0:1], tk_gg)
    b_.seq("v", E["v"].tensor_scalar(out=F["SCO"][:], in0=F["Z"][:], scalar1=-1.0, scalar2=1.0,
                                     op0=ALU.mult, op1=ALU.add))
    b_.seq("v", E["v"].tensor_tensor(out=F["MASKB"][:], in0=F["MASKB"][:], in1=F["SCO"][:],
                                     op=ALU.subtract))
    b_.seq("v", E["v"].scalar_tensor_tensor(out=F["MASKB"][:], in0=F["MASKB"][:],
                                            scalar=F["NAB"][:, 0:1], in1=F["SCO"][:],
                                            op0=ALU.mult, op1=ALU.add))
    b_.seq("v", E["v"].tensor_tensor(out=F["TIE"][:], in0=F["TIE"][:], in1=F["Z"][:],
                                     op=ALU.subtract))
    tk_tie = b_.seq("v", E["v"].scalar_tensor_tensor(out=F["TIE"][:], in0=F["TIE"][:],
                                                     scalar=F["NAB"][:, 0:1], in1=F["Z"][:],
                                                     op0=ALU.mult, op1=ALU.add))
    tk_cb = psum_row(F["MASKB"][:], L1S[0:1, 16:17], F["RED"][:], tk_tie)
    tk_R = b_.seq("v", E["v"].tensor_scalar(out=L1S[0:1, 16:17], in0=L1S[0:1, 16:17], scalar1=-1.0,
                                            scalar2=1024.0, op0=ALU.mult, op1=ALU.add))
    b_.seq("v", E["v"].memset(L1S[0:1, 17:18], -1.0))
    b_.seq("v", E["v"].memset(L1S[0:1, 18:19], 256.0))
    for lev in range(3):
        tk_tr = b_.seq("v", E["v"].tensor_scalar(out=F["CROW"][:], in0=IOTA16[:],
                                                 scalar1=L1S[0:1, 18:19], scalar2=L1S[0:1, 17:18],
                                                 op0=ALU.mult, op1=ALU.add))
        bank = b_.ps[5]
        b_.wait("t", b_.ps_free[5])
        b_.wait("t", tk_tr)
        tkb = b_.sig("t", E["t"].matmul(bank[:, 0:16], A["ONESRF"][:], F["CROW"][:],
                                        start=True, stop=True))
        b_.wait("v", tkb)
        tk_th = b_.seq("v", E["v"].tensor_copy(out=F["THB"][:], in_=bank[:, 0:16]))
        b_.ps_free[5] = tk_th
        b_.seq("v", E["v"].tensor_tensor(
            out=F["IND"][:],
            in0=IOTA.rearrange("p (u t) -> p u t", u=1).to_broadcast([128, 16, NT]),
            in1=F["THB"].rearrange("p (j u) -> p j u", u=1).to_broadcast([128, 16, NT]),
            op=ALU.is_le))
        b_.seq("v", E["v"].tensor_tensor(
            out=F["IND"][:], in0=F["IND"][:],
            in1=F["TIE"].rearrange("p (u t) -> p u t", u=1).to_broadcast([128, 16, NT]),
            op=ALU.mult))
        tk_c = b_.seq("v", E["v"].reduce_sum(out=F["CNT"].rearrange("p (j u) -> p j u", u=1),
                                             in_=F["IND"][:], axis=AX.X))
        bank = b_.ps[5]
        b_.wait("t", b_.ps_free[5])
        b_.wait("t", tk_c)
        tks = b_.sig("t", E["t"].matmul(bank[0:1, 0:16], ONESC[:], F["CNT"][:], start=True, stop=True))
        b_.wait("v", tks)
        tk_cr = b_.seq("v", E["v"].tensor_copy(out=F["CROW"][:], in_=bank[0:1, 0:16]))
        b_.ps_free[5] = tk_cr
        b_.seq("v", E["v"].tensor_scalar(out=F["CROW"][:], in0=F["CROW"][:],
                                         scalar1=L1S[0:1, 16:17], scalar2=None, op0=ALU.is_le))
        b_.seq("v", E["v"].reduce_sum(out=L1S[0:1, 14:15], in_=F["CROW"][:], axis=AX.X))
        b_.seq("v", E["v"].tensor_scalar(out=L1S[0:1, 14:15], in0=L1S[0:1, 14:15], scalar1=-1.0,
                                         scalar2=None, op0=ALU.add))
        b_.seq("v", E["v"].tensor_tensor(out=L1S[0:1, 15:16], in0=L1S[0:1, 14:15],
                                         in1=L1S[0:1, 18:19], op=ALU.mult))
        b_.seq("v", E["v"].tensor_tensor(out=L1S[0:1, 17:18], in0=L1S[0:1, 17:18],
                                         in1=L1S[0:1, 15:16], op=ALU.add))
        tk_st = b_.seq("v", E["v"].tensor_scalar(out=L1S[0:1, 18:19], in0=L1S[0:1, 18:19],
                                                 scalar1=1.0 / 16, scalar2=None, op0=ALU.mult))
        fill(2)
    tk_ib = bcast128(L1S[0:1, 17:18], F["NAB"][:, 1:2], tk_st)
    b_.seq("v", E["v"].tensor_scalar(out=F["SCO"][:], in0=IOTA[:], scalar1=F["NAB"][:, 1:2],
                                     scalar2=None, op0=ALU.is_le))
    b_.seq("v", E["v"].tensor_tensor(out=F["SCO"][:], in0=F["SCO"][:], in1=F["TIE"][:], op=ALU.mult))
    b_.seq("v", E["v"].tensor_tensor(out=F["MASK05"][:], in0=F["MASKB"][:], in1=F["SCO"][:], op=ALU.add))
    tk_mask = b_.seq("v", E["v"].tensor_scalar(out=F["MASK05"][:], in0=F["MASK05"][:], scalar1=0.5,
                                               scalar2=None, op0=ALU.mult))

    fill(len(fillers))

    for grp in range(4):
        gs = slice(grp * 512, (grp + 1) * 512)
        tk_hht = {}
        for hc in range(32):
            k = grp * 32 + hc
            sl = k % 4
            bank = b_.ps[4 + hc % 2]
            b_.wait("t", b_.ps_free[4 + hc % 2])
            b_.wait("t", st1["ld"][sl])
            b_.wait("t", tk_gsc)
            for icx in range(IC):
                mm = E["t"].matmul(bank[:], W1PKs[sl][:, icx, :], LNXT[:, icx, gs],
                                   start=(icx == 0), stop=(icx == IC - 1))
            tk_mm = b_.sig("t", mm)
            st1["rd"][sl] = tk_mm
            issue1()
            b_.wait("s", tk_mm)
            tk_h = b_.sig("s", E["s"].activation(out=HHT[:, hc, :], in_=bank[:], func=AF.Gelu,
                                                 bias=F["B1HC"][:, hc:hc + 1]))
            tk_hht[hc] = tk_h
            b_.ps_free[4 + hc % 2] = tk_h
        for half in range(2):
            for hc in range(32):
                j = grp * 64 + half * 32 + hc
                sl = j % 4
                b_.wait("t", st2["ld"][sl])
                b_.wait("t", tk_hht.get(31))
                for tt in range(4):
                    bank = b_.ps[tt]
                    if hc == 0:
                        b_.wait("t", b_.ps_free[tt])
                    mm = E["t"].matmul(bank[:], HHT[:, hc, ts(tt)], W2BF[:, sl, 0:512],
                                       start=(hc == 0), stop=False)
                st2["rd"][sl] = b_.sig("t", mm)
                issue2()
            for tt in range(4):
                t = grp * 4 + tt
                bank = b_.ps[tt]
                b_.wait("t", tk_b2)
                tk_mm = b_.sig("t", E["t"].matmul(bank[:], ONESR[0:1, :], B2H[0:1, ts(half, 512)],
                                                  start=False, stop=True))
                b_.waits("v", [tk_mm, tk_mask])
                tk_xh = b_.sig("v", E["v"].scalar_tensor_tensor(out=X[:, t, ts(half, 512)], in0=bank[:],
                                                                scalar=F["MASK05"][:, t:t + 1],
                                                                in1=X[:, t, ts(half, 512)],
                                                                op0=ALU.mult, op1=ALU.add))
                b_.ps_free[tt] = tk_xh
    b_.barrier()




def emit_l1attn(b_, nc, H, X, LNXP, LNXT, SCRB, A, F, ONESR, ONESC, inwt_h, POOLC, IDB):
    E = b_.e
    SH = SCRB.rearrange("p (c f) -> p c f", c=16)
    IWB = LNXP[:, 0:2048].rearrange("p (j dc f) -> p j dc f", j=2, dc=IC)
    CV = LNXP[:, 2048:4096].bitcast(F32)
    HALO = LNXP[:, 4096:4224].rearrange("p (c j f) -> p c j f", c=16, j=2)
    XPT1 = LNXP[0:16, 4224:5248].bitcast(F32)
    XPT2 = LNXP[0:16, 5248:6272].bitcast(F32)
    AT = LNXP[0:16, 6272:10368].bitcast(F32)
    BT = LNXP[0:16, 10368:14464].bitcast(F32)
    ROWS = LNXP[0:1, 14464:15488].bitcast(F32)
    YSB = LNXP[0:16, 10368:12416]
    OWTP = POOLC[0:16, 0:1024]
    OBP = POOLC[0:1, 1024:2048]

    ow_stg = CV.bitcast(BF16)[:, 0:128].rearrange("p (ic j) -> p ic j", ic=IC)
    b_.dma("g", ow_stg, H["L1_attn_out_w"].rearrange("(ic p) j -> p ic j", p=128),
                   sem="m3", allow_slow_non_contiguous=True)
    tk_ow = b_.dma("g", OBP, H["L1_attn_out_b"].rearrange("(u d) -> u d", u=1), sem="m3")
    b_.wait("t", tk_ow)
    tk_owt = None
    for icx in range(IC):
        bank = b_.pb[icx % 2]
        b_.wait("t", b_.pb_free[icx % 2])
        tk_t = b_.sig("t", E["t"].transpose(bank[0:16, 0:128], ow_stg[:, icx, :], IDB[:]))
        b_.wait("v", tk_t)
        tk_owt = b_.sig("v", E["v"].tensor_copy(out=OWTP[0:16, ts(icx)], in_=bank[0:16, 0:128]))
        b_.pb_free[icx % 2] = tk_owt

    tk_halo0 = b_.sig("v", E["v"].memset(HALO[:, :, 0, 0:3], 0.0))
    DGB = A["SIGT"].rearrange("p j f -> p (j f)").rearrange("p (j k f) -> p j k f", j=2, k=4)
    CPAD = A["ATT"].rearrange("p j f -> p (j f)")[:, 0:515]
    st_dg = {}
    tk_scan_dep = {}
    tk_used = [None, None]
    tk_shread = None
    tk_ab_guard = None
    tk_bc_prev = None
    for blk in range(2):
        bs = slice(blk * 1024, (blk + 1) * 1024)
        tk_iwl = [None, None]
        b_.wait("g", tk_used[0])
        tk_iwl[0] = b_.dma("g", IWB[:, 0, :, :], inwt_h[0], sem="wa")
        b_.wait("s", tk_shread)
        tk_sh = {}
        for c in range(16):
            sl = c % 2
            if c + 1 < 16:
                b_.wait("g", tk_used[1 - sl])
                tk_iwl[1 - sl] = b_.dma("g", IWB[:, 1 - sl, :, :], inwt_h[c + 1],
                                        sem=("wa" if (1 - sl) == 0 else "wb"))
            for nb2 in range(2):
                bank = b_.ps[nb2]
                b_.wait("t", b_.ps_free[nb2])
                b_.wait("t", tk_iwl[sl])
                for dc in range(IC):
                    mm = E["t"].matmul(bank[:], IWB[:, sl, dc, :],
                                       LNXT[:, dc, slice(blk * 1024 + nb2 * 512, blk * 1024 + nb2 * 512 + 512)],
                                       start=(dc == 0), stop=(dc == IC - 1))
                tk_mm = b_.sig("t", mm)
                if nb2 == 1:
                    tk_used[sl] = tk_mm
                b_.wait("s", tk_mm)
                tk_raw = b_.sig("s", E["s"].activation(out=SH[:, c, nb2 * 512:(nb2 + 1) * 512],
                                                       in_=bank[:], func=AF.Identity,
                                                       bias=F["INBC"][:, c:c + 1]))
                b_.ps_free[nb2] = tk_raw
            raw = SH[:, c, :]
            hal = HALO[:, c, blk % 2, 0:3]
            b_.wait("v", tk_raw)
            b_.wait("v", tk_halo0)
            b_.wait("v", st_dg.get(c % 2))
            tk_dg = None
            for k in range(4):
                tk_dg = b_.seq("v", E["v"].tensor_scalar(out=DGB[:, c % 2, k, :], in0=IDB[:],
                                                         scalar1=F["CONVW"][:, c, k:k + 1],
                                                         scalar2=None, op0=ALU.mult))
            tk_hn = b_.seq("v", E["v"].tensor_copy(out=HALO[:, c, (blk + 1) % 2, 0:3],
                                                   in_=raw[:, 1021:1024]))
            b_.seq("v", E["v"].tensor_copy(out=CPAD[:, 0:3], in_=hal))
            tk_pad = b_.seq("v", E["v"].tensor_copy(out=CPAD[:, 3:515], in_=raw[:, 0:512]))
            tk_cmm = {}
            for nb2 in range(2):
                bank = b_.ps[2 if nb2 == 0 else 5]
                b_.wait("t", b_.ps_free[2 if nb2 == 0 else 5])
                b_.wait("t", tk_dg)
                b_.wait("t", tk_raw)
                b_.wait("t", tk_pad)
                for k in range(4):
                    if nb2 == 1:
                        mm = E["t"].matmul(bank[:], DGB[:, c % 2, k, :], raw[:, 509 + k:1021 + k],
                                           start=(k == 0), stop=(k == 3))
                    else:
                        mm = E["t"].matmul(bank[:], DGB[:, c % 2, k, :], CPAD[:, k:512 + k],
                                           start=(k == 0), stop=(k == 3))
                tk_cmm[nb2] = b_.sig("t", mm)
            st_dg[c % 2] = tk_cmm[1]
            b_.waits("s", [tk_cmm[0], tk_cmm[1], tk_hn])
            E["s"].activation(out=SH[:, c, 0:512], in_=b_.ps[2][:], func=AF.Silu,
                              bias=F["CBC"][:, c:c + 1])
            tk_sil = b_.sig("s", E["s"].activation(out=SH[:, c, 512:1024], in_=b_.ps[5][:],
                                                   func=AF.Silu, bias=F["CBC"][:, c:c + 1]))
            b_.ps_free[2] = tk_sil
            b_.ps_free[5] = tk_sil
            tk_sh[c] = tk_sil
            b_.wait("v", tk_sil)
        for nb2 in range(2):
            nsl = slice(nb2 * 512, (nb2 + 1) * 512)
            bank = b_.ps[2]
            bank5 = b_.ps[5]
            b_.wait("t", b_.ps_free[2])
            b_.wait("t", b_.ps_free[5])
            b_.wait("t", tk_sh[15])
            for c in range(16):
                E["t"].matmul(bank[0:16, :], F["XPWT"][:, c, 0:16], SH[:, c, nsl],
                              start=(c == 0), stop=False)
            E["t"].matmul(bank[0:16, :], F["XPBROW"][0:1, 0:16], A["ONES512B"][:],
                          start=False, stop=True)
            for c in range(16):
                E["t"].matmul(bank5[0:16, :], F["XPWT"][:, c, 16:32], SH[:, c, nsl],
                              start=(c == 0), stop=False)
            tk_mm = b_.sig("t", E["t"].matmul(bank5[0:16, :], F["XPBROW"][0:1, 16:32], A["ONES512B"][:],
                                              start=False, stop=True))
            tk_shread = tk_mm
            b_.wait("v", tk_mm)
            b_.wait("v", tk_ab_guard)
            tk_xp = b_.seq("v", E["v"].tensor_copy(out=XPT1[:], in_=bank[0:16, :]))
            tk_xp = b_.seq("v", E["v"].tensor_copy(out=XPT2[:], in_=bank5[0:16, :]))
            b_.ps_free[2] = tk_xp
            b_.ps_free[5] = tk_xp
            bank = b_.ps[3]
            b_.wait("t", b_.ps_free[3])
            for c in range(16):
                mm = E["t"].matmul(bank[0:1, :], A["ONESB"][:], SH[:, c, nsl],
                                   start=(c == 0), stop=(c == 15))
            tk_rs = b_.sig("t", mm)
            tk_shread = tk_rs
            b_.wait("v", tk_rs)
            b_.wait("v", tk_bc_prev)
            tk_ro = b_.seq("v", E["v"].tensor_copy(out=ROWS[:], in_=bank[0:1, :]))
            b_.ps_free[3] = tk_ro
            bank = b_.ps[4]
            b_.wait("t", b_.ps_free[4])
            b_.wait("t", tk_ro)
            tk_bc = b_.sig("t", E["t"].matmul(bank[0:16, :], A["ONESRF"][0:1, 0:16], ROWS[:],
                                              start=True, stop=True))
            tk_bc_prev = tk_bc
            b_.wait("s", tk_xp)
            ab = slice(blk * 1024 + nb2 * 512, blk * 1024 + nb2 * 512 + 512)
            b_.seq("s", E["s"].activation(out=AT[:, ab], in_=XPT1[:], func=AF.Exp))
            tk_a = b_.seq("s", E["s"].activation(out=AT[:, ab], in_=AT[:, ab], func=AF.Exp, scale=-1.0))
            tk_b1 = b_.sig("s", E["s"].activation(out=BT[:, ab], in_=XPT2[:], func=AF.Silu))
            b_.waits("v", [tk_b1, tk_bc])
            tk_b2 = b_.seq("v", E["v"].tensor_tensor(out=BT[:, ab], in0=BT[:, ab],
                                                     in1=bank[0:16, :], op=ALU.mult))
            b_.ps_free[4] = tk_b2
            tk_ab_guard = tk_a
            tk_scan_dep[(blk, nb2)] = (tk_a, tk_b2)
    b_.wait("s", tk_scan_dep[(1, 1)][0])
    b_.wait("v", tk_scan_dep[(1, 1)][0])
    tk_scan = b_.seq("v", E["v"].tensor_tensor_scan(out=AT[:, 0:2048],
                                                    data0=AT[:, 0:2048], data1=BT[:, 0:2048],
                                                    initial=0.0, op0=ALU.mult, op1=ALU.add))
    tk_ys = b_.seq("v", E["v"].tensor_copy(out=YSB[:], in_=AT[:, 0:2048]))
    for t in range(NT):
        for half in range(2):
            bank = b_.ps[(2 * t + half) % 2]
            b_.wait("t", b_.ps_free[(2 * t + half) % 2])
            b_.waits("t", [tk_ys, tk_owt])
            E["t"].matmul(bank[:], YSB[0:16, ts(t)], OWTP[0:16, ts(half, 512)],
                          start=True, stop=False)
            tk_mm = b_.sig("t", E["t"].matmul(bank[:], ONESR[0:1, :], OBP[0:1, ts(half, 512)],
                                              start=False, stop=True))
            b_.wait("v", tk_mm)
            tk_x = b_.sig("v", E["v"].tensor_tensor(out=X[:, t, ts(half, 512)], in0=bank[:],
                                                    in1=X[:, t, ts(half, 512)], op=ALU.add))
            b_.ps_free[(2 * t + half) % 2] = tk_x
    b_.barrier()



_NC_CACHE = {}


def kernel(**inputs):
    key = STAGES
    if key not in _NC_CACHE:
        _NC_CACHE[key] = build()
    nc = _NC_CACHE[key]
    x = np.ascontiguousarray(np.asarray(inputs["L0_attn_x"], dtype=np.float32))
    in_maps = []
    for c in range(8):
        m = {k: np.ascontiguousarray(np.asarray(v, dtype=np.float32))
             for k, v in inputs.items() if k != "L0_attn_x"}
        m["L0_attn_x"] = x[2 * c:2 * c + 2]
        in_maps.append(m)
    res = run_bass_kernel_spmd(nc, in_maps, core_ids=list(range(8)))
    out = np.concatenate([np.asarray(r["out"], dtype=np.float32)
                          for r in res.results], axis=0)
    return out.reshape(16, T, D)


# version 19
# speedup vs baseline: 1.0550x; 1.0550x over previous
import math
import numpy as np
import concourse.bass as bass
import concourse.mybir as mybir
from concourse.bass_utils import run_bass_kernel_spmd
from contextlib import ExitStack

F32 = mybir.dt.float32
BF16 = mybir.dt.bfloat16
AF = mybir.ActivationFunctionType
ALU = mybir.AluOpType
AX = mybir.AxisListType

D = 1024
T = 2048
NT = 16
IC = 8
NB = 2
NE = 8
LN_EPS = 1e-5

STAGES = 6
RACE_CHECK = False
DBG = 0
L1RAW = 0

ENGINES = "tvsgy"


class Bld:

    def __init__(self, nc, ctx):
        self.nc = nc
        self.e = {
            "t": nc.tensor, "v": nc.vector, "s": nc.scalar,
            "g": nc.gpsimd, "y": nc.sync,
        }
        self.sem = {k: ctx.enter_context(nc.semaphore(f"sp_{k}")) for k in ENGINES}
        dkeys = ["pp", "ppst", "wa", "wb", "xls", "sto", "mw", "m2", "m3", "m4", "qa", "qb", "ta", "tb",
                 "ha", "hb", "hg", "hh", "pw1", "pw2", "pp2", "pp3", "pp4", "pst2"]
        self.dsem = {k: ctx.enter_context(nc.semaphore(f"sd_{k}")) for k in dkeys}
        self.cnt = {k: 0 for k in ENGINES}
        self.dcnt = {k: 0 for k in dkeys}
        self.waited = {}
        self.ps = [ctx.enter_context(nc.psum_tensor(f"ps{i}", [128, 512], F32))
                   for i in range(6)]
        self.pb = [ctx.enter_context(nc.psum_tensor(f"pb{i}", [128, 1024], BF16))
                   for i in range(2)]
        self.ps_free = [None] * 6
        self.pb_free = [None] * 2

    def sig(self, k, inst):
        self.cnt[k] += 1
        inst.then_inc(self.sem[k], 1)
        return ("p", k, self.cnt[k])

    def seq(self, k, inst):
        tk = self.sig(k, inst)
        self.wait(k, tk)
        return tk

    def tok(self, k):
        return ("p", k, self.cnt[k])

    def dma(self, k, out, in_, sem="mw", **kw):
        inst = self.e[k].dma_start(out=out, in_=in_, **kw)
        self.dcnt[sem] += 16
        inst.then_inc(self.dsem[sem], 16)
        return ("d", sem, self.dcnt[sem])

    ORDERED = {"ppst"}

    def wait(self, ck, tok):
        if tok is None:
            return
        kind, pk, v = tok
        key = (ck, kind, pk)
        if self.waited.get(key, 0) >= v:
            return
        if kind == "d" and pk not in self.ORDERED:
            assert v == self.dcnt[pk], (pk, v, self.dcnt[pk])
        sem = self.sem[pk] if kind == "p" else self.dsem[pk]
        self.e[ck].wait_ge(sem, v)
        self.waited[key] = v

    def waits(self, ck, toks):
        for t in toks:
            self.wait(ck, t)

    def barrier(self):
        for dk, v in self.dcnt.items():
            if v:
                self.wait("y", ("d", dk, v))
        toks = {}
        for k in ENGINES:
            self.e[k].drain()
            toks[k] = self.sig(k, self.e[k].nop())
        for ck in ENGINES:
            for pk in ENGINES:
                self.wait(ck, toks[pk])
        return toks


def ts(i, n=128):
    return slice(i * n, (i + 1) * n)


def build():
    nc = bass.Bass("TRN2", target_bir_lowering=False,
                   detect_race_conditions=RACE_CHECK)

    H = {}
    def di(name, shape):
        H[name] = nc.dram_tensor(name, list(shape), F32, kind="ExternalInput").ap()
    di("L0_attn_x", (NB, T, D))
    for n, s in [
        ("L0_attn_Wq", (D, D)), ("L0_attn_Wk", (D, D)), ("L0_attn_Wv", (D, D)),
        ("L0_attn_Wo", (D, D)), ("L0_attn_splat_positions", (64, D)),
        ("L0_attn_splat_scales", (64,)), ("L0_attn_splat_importance", (64,)),
        ("L0_attn_We", (D, 512)), ("L0_attn_be", (512,)),
        ("L0_attn_Wt", (64, 512, 512)), ("L0_attn_bt", (64, 512)),
        ("L0_attn_Wd", (512, D)), ("L0_attn_bd", (D,)),
        ("L0_attn_Wg", (D, D)), ("L0_attn_bg", (D,)),
        ("L0_ffn_Wg", (D, NE)), ("L0_ffn_bg", (NE,)),
        ("L0_ffn_We", (NE, D, D)), ("L0_ffn_be", (NE, D)),
        ("L1_attn_in_w", (2 * D, D)), ("L1_attn_in_b", (2 * D,)),
        ("L1_attn_conv_w", (2 * D, 1, 4)), ("L1_attn_conv_b", (2 * D,)),
        ("L1_attn_xp_w", (32, 2 * D)), ("L1_attn_xp_b", (32,)),
        ("L1_attn_out_w", (D, 16)), ("L1_attn_out_b", (D,)),
        ("L1_ffn_routing_token", (D,)), ("L1_ffn_gamma_light", (D,)),
        ("L1_ffn_w1_light", (D, 512)), ("L1_ffn_b1_light", (512,)),
        ("L1_ffn_w2_light", (512, D)), ("L1_ffn_b2_light", (D,)),
        ("L1_ffn_gamma_heavy", (D,)), ("L1_ffn_w1_heavy", (D, 4096)),
        ("L1_ffn_b1_heavy", (4096,)), ("L1_ffn_w2_heavy", (4096, D)),
        ("L1_ffn_b2_heavy", (D,)),
    ]:
        di(n, s)
    out_h = nc.dram_tensor("out", [NB, T, D], F32, kind="ExternalOutput").ap()
    if DBG:
        dbg_lnx = nc.dram_tensor("dbg_lnx", [128, NT, D], BF16, kind="ExternalOutput").ap()
        dbg_lnxt = nc.dram_tensor("dbg_lnxt", [128, IC, T], BF16, kind="ExternalOutput").ap()
        dbg_g05 = nc.dram_tensor("dbg_g05", [128, NT, NE], F32, kind="ExternalOutput").ap()
        dbg_gexp = nc.dram_tensor("dbg_gexp", [128, NT, NE], F32, kind="ExternalOutput").ap()
        dbg_stat = nc.dram_tensor("dbg_stat", [128, 4 * NT], F32, kind="ExternalOutput").ap()
        dbg_va = nc.dram_tensor("dbg_va", [128, 4 * NT], F32, kind="ExternalOutput").ap()
        dbg_wet = nc.dram_tensor("dbg_wet", [2, 128, IC * 1024], BF16, kind="ExternalOutput").ap()

    wet_h = nc.dram_tensor("wet_scr", [NE, 128, IC * 1024], BF16, kind="Internal").ap()
    trd_h = nc.dram_tensor("trd_scr", [64, 512], BF16, kind="Internal").ap()
    w1hp_h = nc.dram_tensor("w1hp_scr", [IC, 128, 4096], BF16, kind="Internal").ap()
    w1pk_h = nc.dram_tensor("w1pk_scr", [32, 128, 1024], BF16, kind="Internal").ap()
    w2hp_h = nc.dram_tensor("w2hp_scr", [32, 128, 1024], BF16, kind="Internal").ap()
    inwt_h = nc.dram_tensor("inwt_scr", [16, 128, IC, 128], BF16, kind="Internal").ap()

    ctx = ExitStack()
    with ctx:
        sb = lambda name, shape, dt=F32: ctx.enter_context(nc.sbuf_tensor(name, list(shape), dt))
        X = sb("X", [128, NT, D])
        LNXP = sb("LNXP", [128, 16640], BF16)
        LNX = LNXP[:, 0:NT * D].rearrange("p (t d) -> p t d", t=NT)
        LNXT = sb("LNXT", [128, IC, T], BF16)
        SCRB = sb("SCRB", [128, 16384], BF16)
        SUM = sb("SUM", [128, NT])
        SQ = sb("SQ", [128, NT])
        VAR = sb("VAR", [128, NT])
        RSTD = sb("RSTD", [128, NT])
        BIASC = sb("BIASC", [128, NT])
        TMP16 = sb("TMP16", [128, NT])
        VA = sb("VA", [128, NT])
        VB = sb("VB", [128, NT])
        SCR2 = sb("SCR2", [128, 2048], BF16)
        JUNK = SCR2.rearrange("p (j d) -> p j d", j=2)
        GEXP = sb("GEXP", [128, NT, NE])
        GSEL = sb("GSEL", [128, NT, NE])
        G05 = sb("G05", [128, NT, NE])
        M8 = sb("M8", [128, NT, NE])
        GSUM = sb("GSUM", [128, NT])
        WGMOE = sb("WGMOE", [128, IC, NE], BF16)
        BEROW = SCR2[0:1, 0:2048].rearrange("p (j d) -> p j d", j=2)
        BGMOE = sb("BGMOE", [1, NE], BF16)
        ONESR = sb("ONESR", [1, 128], BF16)
        ONESC = sb("ONESC", [128, 1])
        IDB = sb("IDB", [128, 128], BF16)
        IDF = sb("IDF", [128, 128])
        IOTA = sb("IOTA", [128, NT])
        IOTA16 = sb("IOTA16", [1, 16])
        CONSTS = sb("CONSTS", [128, 8])
        A = {}
        A["SPTN"] = sb("SPTN", [128, IC, 64], BF16)
        A["S2HB"] = sb("S2HB", [1, 64], BF16)
        A["NH"] = sb("NH", [64, 1])
        A["AIMP"] = sb("AIMP", [64, 1])
        INFLF = sb("INFL32", [128, T])
        A["INFL32"] = INFLF[0:64, :]
        A["QTH"] = A["INFL32"][:, 0:D].bitcast(BF16)
        A["KTH"] = A["INFL32"][:, D:T].bitcast(BF16)
        A["Q2"] = INFLF[:, 0:D].bitcast(BF16)
        A["K2"] = INFLF[:, D:T].bitcast(BF16)
        A["INFLB"] = sb("INFLB", [64, T], BF16)
        A["AVS"] = A["INFLB"][:, 0:512]
        A["AVN"] = A["INFLB"][:, 512:1024]
        A["INFLTM"] = sb("INFLTM", [128, NT, 64], BF16)
        A["TOTR"] = sb("TOTR", [64, 1])
        POOLC = sb("POOLC", [128, 2048], BF16)
        A["GATH"] = POOLC[0:64, 0:D]
        A["GATHT"] = POOLC[:, D:D + 512].rearrange("p (ic f) -> p ic f", ic=IC)
        A["ENC"] = POOLC[0:64, D + 512:T]
        A["WQS"] = POOLC[:, 0:1024].rearrange("p (ic f) -> p ic f", ic=IC)
        A["WKS"] = POOLC[:, 1024:2048].rearrange("p (ic f) -> p ic f", ic=IC)
        A["RDENB"] = POOLC[0:64, D + 512:T]
        A["ENCT"] = sb("ENCT", [128, 4, 64], BF16)
        A["ROWBUF"] = sb("ROWBUF", [1, 2, 512], BF16)
        TRB128 = sb("TRANSB", [128, 512], BF16)
        A["TRANSB"] = TRB128[0:64, :]
        A["SQB2"] = TRB128
        A["RDENB2"] = TRB128[0:64, :]
        A["BT16"] = sb("BT16", [64, 512], BF16)
        A["TRANST"] = sb("TRANST", [128, 4, 64], BF16)
        DPOOL = sb("DPOOL", [128, D], BF16)
        A["DEC"] = DPOOL[0:64, :]
        A["SQB"] = DPOOL[0:64, :].rearrange("p (j f) -> p j f", j=2)
        A["RDEN"] = DPOOL[64:65, :].bitcast(F32)
        A["ONESR64"] = sb("ONESR64", [128, 64])
        A["SIGT"] = SCR2[:, 0:1024].rearrange("p (j f) -> p j f", j=2)
        A["ATT"] = SCR2[:, 1024:2048].rearrange("p (j f) -> p j f", j=2)
        A["X2P"] = SCR2[0:1, 1024:2048].rearrange("p (j f) -> p j f", j=2)
        A["WSPL"] = LNXP[:, 0:4096].rearrange("p (ic f) -> p ic f", ic=IC)
        A["WBIGH"] = sb("WBIGH", [128, IC, 512], BF16)
        A["NEGM"] = sb("NEGM", [128, 2])
        A["TMPS"] = sb("TMPS", [1, 16])
        A["BROW"] = A["INFL32"][0:1, 0:1280].bitcast(BF16)
        A["ONES512B"] = sb("ONES512B", [1, 512], BF16)
        A["ONESB"] = sb("ONESB", [128, 1], BF16)
        A["ONESRF"] = sb("ONESRF", [1, 128])
        F = {}
        F["B1HC"] = sb("B1HC", [128, 32])
        F["B1LC"] = sb("B1LC", [128, 4])
        F["GAML"] = sb("GAML", [128, IC])
        F["GAMH"] = sb("GAMH", [128, IC])
        F["RLN"] = sb("RLN", [128, NT])
        F["XRT"] = sb("XRT", [128, NT])
        F["SALL"] = sb("SALL", [128, NT])
        F["W16"] = sb("W16", [128, NT])
        F["SCO"] = sb("SCO", [128, NT])
        F["Z"] = sb("Z", [128, NT])
        F["MASKB"] = sb("MASKB", [128, NT])
        F["TIE"] = sb("TIE", [128, NT])
        F["MASK05"] = sb("MASK05", [128, NT])
        F["RED"] = sb("RED", [128, 1])
        F["L1S"] = sb("L1S", [1, 32])
        F["THB"] = sb("THB", [128, 16])
        F["NAB"] = sb("NAB", [128, 2])
        F["SRTB"] = sb("SRTB", [128, 1])
        F["ZTH"] = sb("ZTH", [128, 1])
        F["EJ"] = sb("EJ", [128, NT], BF16)
        F["IND"] = sb("IND", [128, 16, 16], BF16)
        F["CNT"] = sb("CNT", [128, 16])
        F["CROW"] = sb("CROW", [1, 16])
        F["XPWT"] = sb("XPWT", [128, 16, 32], BF16)
        F["XPBROW"] = sb("XPBROW", [1, 32], BF16)
        F["CONVW"] = sb("CONVW", [128, 16, 4])
        F["INBC"] = sb("INBC", [128, 16])
        F["CBC"] = sb("CBC", [128, 16])

        b_ = Bld(nc, ctx)
        b_.VA, b_.VB = VA, VB
        b_.F = F
        b_.RLN = F["RLN"]
        E = b_.e

        E["v"].memset(ONESR[:], 1.0)
        E["v"].memset(ONESC[:], 1.0)
        E["g"].memset(IDB[:], 0.0)
        tk0 = b_.sig("g", E["g"].memset(IDF[:], 0.0))
        b_.wait("g", tk0)
        E["g"].affine_select(out=IDB[:], in_=IDB[:], compare_op=ALU.not_equal,
                             fill=1.0, base=0, pattern=[[-1, 128]], channel_multiplier=1)
        E["g"].affine_select(out=IDF[:], in_=IDF[:], compare_op=ALU.not_equal,
                             fill=1.0, base=0, pattern=[[-1, 128]], channel_multiplier=1)
        E["g"].iota(IOTA[:], pattern=[[128, NT]], base=0, channel_multiplier=1,
                    allow_small_or_imprecise_dtypes=True)
        tok_const = b_.sig("g", E["g"].iota(IOTA16[:], pattern=[[1, 16]], base=0,
                                            channel_multiplier=0,
                                            allow_small_or_imprecise_dtypes=True))
        for _j, _v in enumerate([LN_EPS, 1e-30]):
            E["v"].memset(CONSTS[:, _j:_j + 1], _v)
            nc.const_aps.register(F32, _v, CONSTS[:, _j:_j + 1])
        E["v"].memset(A["ONES512B"][:], 1.0)
        E["v"].memset(A["ONESB"][:], 1.0)
        E["v"].memset(A["ONESRF"][:], 1.0)
        E["v"].memset(A["ONESR64"][:], 1.0)

        xsrc0 = H["L0_attn_x"][0].rearrange("(t p) d -> p t d", p=128)
        for j in range(4):
            tk_x0 = b_.dma("y", X[:, ts(j, 4), :], xsrc0[:, ts(j, 4), :], sem="xls")

        wsrc = SCRB.rearrange("p (h ic o) -> p h ic o", h=2, ic=IC, o=1024)
        tk_asm_store = None
        tk_wet_store = [None] * NE
        tk_tr_last = None
        for e in range(NE):
            src = wsrc[:, 0, :, :]
            asm = wsrc[:, 1, :, :]
            b_.wait("g", tk_tr_last)
            wesrc = H["L0_ffn_We"][e].rearrange("(oc p) i -> p oc i", p=128)
            tk_lds = []
            for q, sm in enumerate(("pp", "pp2", "pp3", "pp4")):
                tk_lds.append(b_.dma("g", src[:, 2 * q:2 * q + 2, :], wesrc[:, 2 * q:2 * q + 2, :], sem=sm))
            b_.waits("t", tk_lds + [tok_const] + (tk_asm_store or []))
            ev_toks = []
            for icb in range(IC):
                bank = b_.pb[icb % 2]
                b_.wait("t", b_.pb_free[icb % 2])
                mm = None
                for oc in range(IC):
                    mm = E["t"].transpose(bank[:, ts(oc)], src[:, oc, ts(icb)], IDB[:])
                tk_t = b_.sig("t", mm)
                tk_tr_last = tk_t
                b_.wait("v", tk_t)
                cp = E["v"].tensor_copy(out=asm[:, icb, :], in_=bank[:, 0:1024])
                tk_e = b_.sig("v", cp)
                b_.pb_free[icb % 2] = tk_e
                ev_toks.append(tk_e)
            b_.waits("y", ev_toks)
            asm_flat = asm.rearrange("p ic o -> p (ic o)")
            tk_asm_store = [b_.dma("y", wet_h[e][:, 0:4096], asm_flat[:, 0:4096], sem="ppst"),
                            b_.dma("y", wet_h[e][:, 4096:8192], asm_flat[:, 4096:8192], sem="pst2")]
            tk_wet_store[e] = tk_asm_store

        b_.dma("g", WGMOE[:], H["L0_ffn_Wg"].rearrange("(ic p) e -> p ic e", p=128), sem="mw")
        b_.dma("g", BGMOE[:], H["L0_ffn_bg"].rearrange("(u e) -> u e", u=1), sem="mw")

        SPP = LNXP[0:64, 0:2048].bitcast(F32)
        b_.dma("y", SPP, H["L0_attn_splat_positions"][:], sem="m2")
        b_.dma("y", A["NH"][:], H["L0_attn_splat_scales"].rearrange("(s u) -> s u", u=1), sem="m2")
        tk_im = b_.dma("y", A["AIMP"][:], H["L0_attn_splat_importance"].rearrange("(s u) -> s u", u=1), sem="m2")
        b_.wait("s", tk_im)
        b_.wait("s", tk_tr_last)
        tk_s2 = b_.sig("s", E["s"].activation(out=SCRB[0:64, 0:D], in_=SPP, func=AF.Square,
                                              accum_out=A["TOTR"][:]))
        b_.waits("t", [tk_s2, tk0])
        tk_t2 = b_.sig("t", E["t"].transpose(b_.ps[5][0:1, 0:64], A["TOTR"][:], IDF[0:64, 0:64]))
        b_.wait("s", tk_t2)
        tk_s2r = b_.sig("s", E["s"].activation(out=A["S2HB"][:], in_=b_.ps[5][0:1, 0:64],
                                               func=AF.Identity, scale=0.5, bias=0.0))
        for icx in range(IC):
            bank = b_.ps[icx % 2]
            b_.wait("t", b_.ps_free[icx % 2])
            tk_tt = b_.sig("t", E["t"].transpose(bank[:, 0:64], SPP[:, ts(icx)], IDF[0:64, 0:64]))
            b_.wait("s", tk_tt)
            tk_ev = b_.seq("s", E["s"].activation(out=A["SPTN"][:, icx, :], in_=bank[:, 0:64],
                                                  func=AF.Identity, scale=-1.0, bias=0.0))
            b_.ps_free[icx % 2] = tk_ev
        b_.wait("s", tk_im)
        tk_ab = b_.seq("s", E["s"].activation(out=A["NH"][:], in_=A["NH"][:], func=AF.Abs))
        tk_ab2 = b_.sig("s", E["s"].activation(out=A["AIMP"][:], in_=A["AIMP"][:], func=AF.Abs))
        b_.waits("v", [tk_ab, tk_ab2])
        b_.seq("v", E["v"].tensor_scalar_max(A["NH"][:], A["NH"][:], 1e-6))
        b_.seq("v", E["v"].tensor_tensor(out=A["NH"][:], in0=A["NH"][:], in1=A["NH"][:], op=ALU.mult))
        b_.seq("v", E["v"].reciprocal(out=A["NH"][:], in_=A["NH"][:]))
        b_.seq("v", E["v"].tensor_scalar(out=A["NH"][:], in0=A["NH"][:], scalar1=-0.5,
                                         scalar2=None, op0=ALU.mult))
        tk_moe_w = b_.dma("g", A["BT16"][:], H["L0_attn_bt"][:], sem="mw")

        b_.barrier()
        b_.dma("y", F["B1HC"][:], H["L1_ffn_b1_heavy"].rearrange("(hc p) -> p hc", p=128), sem="m2", allow_slow_non_contiguous=True)
        b_.dma("y", F["B1LC"][:], H["L1_ffn_b1_light"].rearrange("(hc p) -> p hc", p=128), sem="m2", allow_slow_non_contiguous=True)
        b_.dma("y", F["GAML"][:], H["L1_ffn_gamma_light"].rearrange("(ic p) -> p ic", p=128), sem="m2", allow_slow_non_contiguous=True)
        tk_gam = b_.dma("y", F["GAMH"][:], H["L1_ffn_gamma_heavy"].rearrange("(ic p) -> p ic", p=128), sem="m2", allow_slow_non_contiguous=True)
        b_.seq("v", E["v"].memset(F["L1S"][0:1, 0:1], 1.0))
        b_.seq("v", E["v"].tensor_scalar(out=F["L1S"][0:1, 1:2], in0=F["L1S"][0:1, 0:1],
                                         scalar1=2.0 ** -30, scalar2=-1.0, op0=ALU.add, op1=ALU.add))
        b_.seq("v", E["v"].tensor_scalar(out=F["L1S"][0:1, 1:2], in0=F["L1S"][0:1, 1:2],
                                         scalar1=0.0, scalar2=None, op0=ALU.is_gt))
        b_.seq("v", E["v"].tensor_scalar(out=F["L1S"][0:1, 1:2], in0=F["L1S"][0:1, 1:2],
                                         scalar1=-1.0, scalar2=1.0, op0=ALU.mult, op1=ALU.add))
        tk_zt = b_.seq("v", E["v"].tensor_scalar(out=F["L1S"][0:1, 1:2], in0=F["L1S"][0:1, 1:2],
                                                 scalar1=2.0 ** -126, scalar2=None, op0=ALU.mult))
        b_.wait("t", tk_zt)
        tk_ztb = b_.sig("t", E["t"].matmul(b_.ps[5][:, 0:1], A["ONESRF"][:], F["L1S"][0:1, 1:2],
                                           start=True, stop=True))
        b_.wait("v", tk_ztb)
        tk_zthc = b_.sig("v", E["v"].tensor_copy(out=F["ZTH"][:], in_=b_.ps[5][:, 0:1]))
        b_.ps_free[5] = tk_zthc
        w1src = H["L1_ffn_w1_heavy"].rearrange("(ic p) hh -> p ic hh", p=128)
        tk_pw1 = None
        tk_pw2 = None
        for hc in range(32):
            tk_pw1 = b_.dma("g", w1pk_h[hc].rearrange("p (ic f) -> p ic f", ic=IC),
                            w1src[:, :, hc * 128:(hc + 1) * 128], sem="pw1",
                            allow_slow_non_contiguous=True)
        for hc in range(32):
            tk_pw2 = b_.dma("g", w2hp_h[hc], H["L1_ffn_w2_heavy"][ts(hc), :], sem="pw2")

        b_.dma("y", F["INBC"][:], H["L1_attn_in_b"].rearrange("(c p) -> p c", p=128), sem="m2",
               allow_slow_non_contiguous=True)
        b_.dma("y", F["CBC"][:], H["L1_attn_conv_b"].rearrange("(c p) -> p c", p=128), sem="m2",
               allow_slow_non_contiguous=True)
        b_.dma("g", F["XPBROW"][:], H["L1_attn_xp_b"].rearrange("(u j) -> u j", u=1), sem="m3")
        tk_cw = b_.dma("y", F["CONVW"][:],
                       H["L1_attn_conv_w"].rearrange("(c p) one k -> p c (one k)", p=128),
                       sem="m2", allow_slow_non_contiguous=True)
        xw_stg = SCRB[0:32, 0:2048]
        b_.waits("g", tk_asm_store)
        tk_xw = b_.dma("g", xw_stg, H["L1_attn_xp_w"][:], sem="m3")
        b_.waits("t", [tk_xw, tk_cw])
        for c in range(16):
            bank = b_.pb[c % 2]
            b_.wait("t", b_.pb_free[c % 2])
            tk_t = b_.sig("t", E["t"].transpose(bank[:, 0:32], xw_stg[:, ts(c)], IDB[0:32, 0:32]))
            b_.wait("v", tk_t)
            tk_e = b_.sig("v", E["v"].tensor_copy(out=F["XPWT"][:, c, :], in_=bank[:, 0:32]))
            b_.pb_free[c % 2] = tk_e
        iw_stg = SCRB.rearrange("p (j f) -> p j f", j=8)
        tk_iw = {}
        tk_iwst = {}
        b_.wait("g", tk_e)
        for ocb in range(16):
            sl = ocb % 2
            b_.wait("g", tk_iwst.get(ocb - 2))
            tk_l = b_.dma("g", iw_stg[:, sl, 0:1024], H["L1_attn_in_w"][ts(ocb), :],
                          sem=("wa" if sl == 0 else "wb"))
            asm = iw_stg[:, 2 + sl, 0:1024].rearrange("p (dc f) -> p dc f", dc=IC)
            b_.wait("v", tk_iwst.get(ocb - 2))
            ev = None
            for dc in range(IC):
                bank = b_.pb[dc % 2]
                b_.wait("t", b_.pb_free[dc % 2])
                b_.wait("t", tk_l)
                tk_t = b_.sig("t", E["t"].transpose(bank[:, 0:128], iw_stg[:, sl, dc * 128:(dc + 1) * 128],
                                                    IDB[:]))
                b_.wait("v", tk_t)
                ev = b_.sig("v", E["v"].tensor_copy(out=asm[:, dc, 0:128], in_=bank[:, 0:128]))
                b_.pb_free[dc % 2] = ev
            b_.wait("y", ev)
            tk_iwst[ocb] = b_.dma("y", inwt_h[ocb].rearrange("p dc f -> p (dc f)"),
                                  iw_stg[:, 2 + sl, 0:1024], sem=("ta" if sl == 0 else "tb"))

        b_.barrier()

        for b in range(NB):
            if b > 0:
                xsrc = H["L0_attn_x"][b].rearrange("(t p) d -> p t d", p=128)
                for j in range(4):
                    tk_x = b_.dma("y", X[:, ts(j, 4), :], xsrc[:, ts(j, 4), :], sem="xls")
            else:
                tk_x = tk_x0
            b_.wait("v", tk_x)
            b_.wait("s", tk_x)

            n_done = 0
            if STAGES >= 1:
                emit_ln(b_, X, LNX, LNXT, SUM, SQ, VAR, RSTD, BIASC, TMP16, JUNK,
                        IDB, rms=False)
                if DBG and b == 0:
                    b_.dma("y", dbg_va[:, 0:NT], TMP16[:], sem="m2")
                    b_.dma("y", dbg_va[:, NT:2 * NT], VA[:], sem="m2")
                    b_.dma("y", dbg_va[:, 2 * NT:3 * NT], VB[:], sem="m2")
                    b_.dma("y", dbg_va[:, 3 * NT:4 * NT], VAR[:], sem="m2")
                    b_.dma("y", dbg_lnx[:], LNX[:], sem="m2")
                    b_.dma("y", dbg_lnxt[:], LNXT[:], sem="m2")
                    b_.dma("y", dbg_stat[:, 0:NT], SUM[:], sem="m2")
                    b_.dma("y", dbg_stat[:, NT:2 * NT], SQ[:], sem="m2")
                    b_.dma("y", dbg_stat[:, 2 * NT:3 * NT], RSTD[:], sem="m2")
                    b_.dma("y", dbg_stat[:, 3 * NT:4 * NT], BIASC[:], sem="m2")
                    b_.dma("y", dbg_wet[0], wet_h[0], sem="m2")
                    b_.dma("y", dbg_wet[1], wet_h[7], sem="m2")
                emit_moe(b_, nc, H, X, LNXT, SCRB, wet_h,
                         WGMOE, BEROW, BGMOE, ONESR,
                         GEXP, GSEL, G05, M8, GSUM, tk_moe_w, LNXP, IDF)
                if DBG and b == 0:
                    b_.dma("y", dbg_g05[:], G05[:], sem="m2")
                    b_.dma("y", dbg_gexp[:], GEXP[:], sem="m2")
                n_done = 1

            if STAGES >= 2:
                emit_ln(b_, X, LNX, LNXT, SUM, SQ, VAR, RSTD, BIASC, TMP16, JUNK,
                        IDB, rms=False)
                emit_l0attn(b_, nc, H, X, LNX, LNXP, LNXT, SCRB, A, IDB, ONESR, trd_h)
                n_done = 2

            if STAGES >= 3:
                emit_ln(b_, X, LNX, LNXT, SUM, SQ, VAR, RSTD, BIASC, TMP16, JUNK,
                        IDB, rms=False)
                emit_moe(b_, nc, H, X, LNXT, SCRB, wet_h,
                         WGMOE, BEROW, BGMOE, ONESR,
                         GEXP, GSEL, G05, M8, GSUM, tk_moe_w, LNXP, IDF)
                n_done = 3

            if STAGES >= 4:
                emit_ln(b_, X, LNX, LNXT, SUM, SQ, VAR, RSTD, BIASC, TMP16, JUNK,
                        IDB, rms=True)
                emit_l1ffn(b_, nc, H, X, LNXP, LNXT, SCRB, A, F, TMP16, IOTA, IOTA16,
                           ONESC, ONESR, IDF, w1pk_h, w2hp_h, tk_pw1, tk_pw2)
                n_done = 4

            if STAGES >= 5:
                emit_ln(b_, X, LNX, LNXT, SUM, SQ, VAR, RSTD, BIASC, TMP16, JUNK,
                        IDB, rms=False)
                emit_l1attn(b_, nc, H, X, LNXP, LNXT, SCRB, A, F, ONESR, ONESC, inwt_h, POOLC, IDB)
                n_done = 5

            if STAGES >= 6:
                emit_ln(b_, X, LNX, LNXT, SUM, SQ, VAR, RSTD, BIASC, TMP16, JUNK,
                        IDB, rms=True)
                emit_l1ffn(b_, nc, H, X, LNXP, LNXT, SCRB, A, F, TMP16, IOTA, IOTA16,
                           ONESC, ONESR, IDF, w1pk_h, w2hp_h, tk_pw1, tk_pw2)
                n_done = 6

            b_.barrier()
            xdst = out_h[b].rearrange("(t p) d -> p t d", p=128)
            for j in range(4):
                tk_st = b_.dma("y", xdst[:, ts(j, 4), :], X[:, ts(j, 4), :], sem="sto")
            b_.wait("y", tk_st)
            b_.barrier()

    return nc


def emit_ln(b_, X, LNX, LNXT, SUM, SQ, VAR, RSTD, BIASC, TMP16, JUNK, IDB, rms=False):
    E = b_.e
    tk_sum = tk_sq = None
    for t in range(NT):
        tk_sum = b_.sig("v", E["v"].reduce_sum(out=SUM[:, t:t + 1], in_=X[:, t, :], axis=AX.X))
        tk_sq = b_.seq("s", E["s"].activation(out=JUNK[:, t % 2, :], in_=X[:, t, :], func=AF.Square,
                                              accum_out=SQ[:, t:t + 1]))
    b_.wait("v", tk_sq)
    b_.wait("v", tk_sum)
    VA, VB = b_.VA, b_.VB
    b_.seq("v", E["v"].tensor_scalar(out=TMP16[:], in0=SUM[:], scalar1=1.0 / D,
                                     scalar2=None, op0=ALU.mult))
    b_.seq("v", E["v"].tensor_tensor(out=VA[:], in0=TMP16[:], in1=TMP16[:], op=ALU.mult))
    b_.seq("v", E["v"].scalar_tensor_tensor(out=VB[:], in0=SQ[:], scalar=1.0 / D,
                                            in1=VA[:], op0=ALU.mult, op1=ALU.subtract))
    eps = 1e-30 if rms else LN_EPS
    tk_v = b_.seq("v", E["v"].tensor_scalar(out=VAR[:], in0=VB[:], scalar1=eps,
                                            scalar2=None, op0=ALU.add))
    b_.wait("s", tk_v)
    tk_sq2 = b_.sig("s", E["s"].activation(out=RSTD[:], in_=VAR[:], func=AF.Sqrt))
    if rms:
        RLN = b_.RLN
        tk_rl = b_.sig("s", E["s"].activation(out=RLN[:], in_=VAR[:], func=AF.Sqrt, bias=LN_EPS))
        b_.wait("v", tk_rl)
        b_.seq("v", E["v"].reciprocal(out=RLN[:], in_=RLN[:]))
    b_.wait("v", tk_sq2)
    tk_r = b_.seq("v", E["v"].reciprocal(out=RSTD[:], in_=RSTD[:]))
    b_.seq("v", E["v"].tensor_tensor(out=BIASC[:], in0=TMP16[:], in1=RSTD[:], op=ALU.mult))
    tk_b = b_.sig("v", E["v"].tensor_scalar(out=BIASC[:], in0=BIASC[:], scalar1=-1.0,
                                            scalar2=None, op0=ALU.mult))
    b_.waits("s", [tk_r, tk_b])
    for t in range(NT):
        tk_n = b_.sig("s", E["s"].activation(out=LNX[:, t, :], in_=X[:, t, :], func=AF.Identity,
                                             bias=BIASC[:, t:t + 1], scale=RSTD[:, t:t + 1]))
        b_.wait("t", tk_n)
        for half in range(2):
            bank = b_.pb[half]
            b_.wait("t", b_.pb_free[half])
            mm = None
            for j in range(4):
                icx = half * 4 + j
                mm = E["t"].transpose(bank[:, ts(j)], LNX[:, t, ts(icx)], IDB[:])
            tk_t = b_.sig("t", mm)
            b_.wait("v", tk_t)
            cp = E["v"].tensor_copy(out=LNXT[:, ts(half, 4), ts(t)], in_=bank[:, 0:512])
            tk_e = b_.sig("v", cp)
            b_.pb_free[half] = tk_e
    b_.barrier()


def emit_moe(b_, nc, H, X, LNXT, SCRB, wet_h, WGMOE, BEROW, BGMOE, ONESR,
             GEXP, GSEL, G05, M8, GSUM, tk_moe_w, LNXP, IDF):
    E = b_.e
    b_.wait("t", tk_moe_w)
    gb = 0
    ev = []
    for t in range(NT):
        bank = b_.ps[gb]
        b_.wait("t", b_.ps_free[gb])
        for icx in range(IC):
            E["t"].matmul(bank[:, 0:NE], LNXT[:, icx, ts(t)], WGMOE[:, icx, :],
                          start=(icx == 0), stop=False)
        mm = E["t"].matmul(bank[:, 0:NE], ONESR[:], BGMOE[:], start=False, stop=True)
        tk_mm = b_.sig("t", mm)
        b_.wait("v", tk_mm)
        tk_ev = b_.sig("v", E["v"].tensor_copy(out=GEXP[:, t, :], in_=bank[:, 0:NE]))
        b_.ps_free[gb] = tk_ev
        ev.append(tk_ev)
        gb = 1 - gb
    b_.waits("s", ev)
    tk_e = b_.sig("s", E["s"].activation(out=GEXP.rearrange("p t e -> p (t e)"),
                                         in_=GEXP.rearrange("p t e -> p (t e)"), func=AF.Exp))
    b_.wait("v", tk_e)
    for t in range(NT):
        mx = E["v"].max(out=M8[:, t, :], in_=GEXP[:, t, :])
    b_.wait("v", b_.sig("v", mx))
    m2 = M8[:, :, 1:2].to_broadcast([128, NT, NE])
    b_.seq("v", E["v"].tensor_tensor(out=GSEL[:], in0=GEXP[:], in1=m2, op=ALU.is_ge))
    b_.seq("v", E["v"].tensor_tensor(out=GSEL[:], in0=GEXP[:], in1=GSEL[:], op=ALU.mult))
    b_.seq("v", E["v"].reduce_sum(out=GSUM.rearrange("p (t u) -> p t u", u=1), in_=GSEL[:], axis=AX.X))
    b_.seq("v", E["v"].reciprocal(out=GSUM[:], in_=GSUM[:]))
    b_.seq("v", E["v"].tensor_scalar(out=GSUM[:], in0=GSUM[:], scalar1=0.5, scalar2=None, op0=ALU.mult))
    tk_g = b_.sig("v", E["v"].tensor_tensor(
        out=G05[:], in0=GSEL[:],
        in1=GSUM.rearrange("p (t u) -> p t u", u=1).to_broadcast([128, NT, NE]), op=ALU.mult))

    wbuf = SCRB.rearrange("p (h ic o) -> p h ic o", h=2, ic=IC, o=1024)
    BE8 = LNXP[0:8, 0:2048].bitcast(F32)
    GTE = LNXP[0:8, 2048:6144].bitcast(F32).rearrange("p (t f) -> p t f", t=NT)
    tk_be = b_.dma("g", BE8, H["L0_ffn_be"][:, :], sem="qa")
    tk_wload = [None, None]
    tk_lastmm = [None, None]
    tk_wload[0] = b_.dma("g", wbuf[:, 0, :, :],
                         wet_h[0].rearrange("p (ic o) -> p ic o", ic=IC), sem="wa")
    tk_gt = {}
    for t in range(NT):
        bank = b_.ps[4 + t % 2]
        b_.wait("t", b_.ps_free[4 + t % 2])
        b_.wait("t", tk_g)
        tk_tr = b_.sig("t", E["t"].transpose(bank[0:8, 0:128], G05[:, t, :], IDF[:]))
        b_.wait("v", tk_tr)
        tk_gt[t] = b_.sig("v", E["v"].tensor_copy(out=GTE[:, t, :], in_=bank[0:8, 0:128]))
        b_.ps_free[4 + t % 2] = tk_gt[t]
    for e in range(NE):
        h = e % 2
        if e + 1 < NE:
            b_.wait("g", tk_lastmm[1 - h])
            sm = "wa" if (1 - h) == 0 else "wb"
            tk_wload[1 - h] = b_.dma("g", wbuf[:, 1 - h, :, :],
                                     wet_h[e + 1].rearrange("p (ic o) -> p ic o", ic=IC), sem=sm)
        b_.wait("t", tk_wload[h])
        b_.wait("t", tk_g)
        we = wbuf[:, h, :, :]
        for t in range(NT):
            bk0, bk1 = (t % 2) * 2, (t % 2) * 2 + 1
            b_.wait("t", b_.ps_free[bk0])
            b_.wait("t", b_.ps_free[bk1])
            for icx in range(IC):
                E["t"].matmul(b_.ps[bk0][:], LNXT[:, icx, ts(t)], we[:, icx, 0:512],
                              start=(icx == 0), stop=(icx == IC - 1))
                mm = E["t"].matmul(b_.ps[bk1][:], LNXT[:, icx, ts(t)], we[:, icx, 512:1024],
                                   start=(icx == 0), stop=(icx == IC - 1))
            tk_mm = b_.sig("t", mm)
            tk_lastmm[h] = tk_mm
            b_.wait("v", tk_mm)
            g = G05[:, t, e:e + 1]
            E["v"].scalar_tensor_tensor(out=X[:, t, 0:512], in0=b_.ps[bk0][:], scalar=g,
                                        in1=X[:, t, 0:512], op0=ALU.mult, op1=ALU.add)
            tk_ev = b_.sig("v", E["v"].scalar_tensor_tensor(
                out=X[:, t, 512:1024], in0=b_.ps[bk1][:], scalar=g,
                in1=X[:, t, 512:1024], op0=ALU.mult, op1=ALU.add))
            b_.ps_free[bk0] = tk_ev
            b_.ps_free[bk1] = tk_ev
    for t in range(NT):
        for half in range(2):
            bank = b_.ps[(2 * t + half) % 4]
            b_.wait("t", b_.ps_free[(2 * t + half) % 4])
            b_.waits("t", [tk_be, tk_gt[t]])
            tk_mm = b_.sig("t", E["t"].matmul(bank[:], GTE[:, t, :], BE8[:, ts(half, 512)],
                                              start=True, stop=True))
            b_.wait("v", tk_mm)
            tk_ev = b_.sig("v", E["v"].tensor_tensor(out=X[:, t, ts(half, 512)], in0=bank[:],
                                                     in1=X[:, t, ts(half, 512)], op=ALU.add))
            b_.ps_free[(2 * t + half) % 4] = tk_ev
    b_.barrier()




def emit_l0attn(b_, nc, H, X, LNX, LNXP, LNXT, SCRB, A, IDB, ONESR, trd_h):
    E = b_.e
    MIXT = SCRB.rearrange("p (c n) -> p c n", c=IC)
    V = LNXP.rearrange("p (t c) -> p t c", c=1040)
    Vh = LNXP.rearrange("p (t h c) -> p t h c", t=NT, h=16)

    tk_ib = None
    tk_d2mm = {}
    for nb in range(4):
        nbs = slice(nb * 512, (nb + 1) * 512)
        bank4 = b_.ps[4]
        b_.wait("t", b_.ps_free[4])
        tk_mm = {}
        for icx in range(IC):
            sqv = A["SIGT"][:, icx % 2, :]
            if icx >= 2:
                b_.wait("s", tk_mm[icx - 2])
            tk_sq = b_.sig("s", E["s"].activation(out=sqv, in_=LNXT[:, icx, nbs], func=AF.Square))
            b_.wait("t", tk_sq)
            tk_mm[icx] = b_.sig("t", E["t"].matmul(bank4[0:1, :], A["ONESB"][:], sqv,
                                                   start=(icx == 0), stop=(icx == IC - 1)))
        b_.wait("s", tk_mm[IC - 1])
        b_.wait("s", tk_d2mm.get(nb - 2))
        tk_x2 = b_.sig("s", E["s"].activation(out=A["X2P"][0:1, nb % 2, :], in_=bank4[0:1, :],
                                              func=AF.Identity, scale=0.5, bias=0.0))
        b_.ps_free[4] = tk_x2
        bank = b_.ps[nb % 2]
        b_.wait("t", b_.ps_free[nb % 2])
        for icx in range(IC):
            E["t"].matmul(bank[0:64, :], A["SPTN"][:, icx, :], LNXT[:, icx, nbs],
                          start=(icx == 0), stop=False)
        E["t"].matmul(bank[0:64, :], A["S2HB"][:], A["ONES512B"][:], start=False, stop=False)
        b_.wait("t", tk_x2)
        tk_mm = b_.sig("t", E["t"].matmul(bank[0:64, :], ONESR[0:1, 0:64], A["X2P"][0:1, nb % 2, :],
                                          start=False, stop=True))
        tk_d2mm[nb] = tk_mm
        b_.wait("v", tk_mm)
        tk_r = b_.seq("v", E["v"].tensor_scalar(out=A["INFL32"][:, nbs], in0=bank[0:64, :],
                                                scalar1=2.0, scalar2=0.0,
                                                op0=ALU.mult, op1=ALU.max))
        b_.ps_free[nb % 2] = tk_r
        b_.wait("s", tk_r)
        tk_e = b_.sig("s", E["s"].activation(out=A["INFL32"][:, nbs], in_=A["INFL32"][:, nbs],
                                             func=AF.Exp, scale=A["NH"][:]))
        b_.wait("v", tk_e)
        b_.seq("v", E["v"].tensor_scalar(out=A["INFL32"][:, nbs], in0=A["INFL32"][:, nbs],
                                         scalar1=A["AIMP"][:], scalar2=None, op0=ALU.mult))
        tk_ib = b_.seq("v", E["v"].tensor_copy(out=A["INFLB"][:, nbs], in_=A["INFL32"][:, nbs]))
    b_.seq("v", E["v"].reduce_sum(out=A["TOTR"][:], in_=A["INFL32"][:], axis=AX.X))
    b_.seq("v", E["v"].tensor_scalar(out=A["TOTR"][:], in0=A["TOTR"][:], scalar1=1e-8,
                                     scalar2=None, op0=ALU.max))
    tk_tot = b_.seq("v", E["v"].reciprocal(out=A["TOTR"][:], in_=A["TOTR"][:]))
    b_.wait("g", tk_tot)
    b_.dma("g", A["BROW"][0:1, 0:D], H["L0_attn_bg"].rearrange("(u d) -> u d", u=1), sem="m3")
    b_.dma("g", A["BROW"][0:1, D:2 * D], H["L0_attn_bd"].rearrange("(u d) -> u d", u=1), sem="m3")
    tk_brow = b_.dma("g", A["BROW"][0:1, 2 * D:2560], H["L0_attn_be"].rearrange("(u d) -> u d", u=1), sem="m3")
    tk_itm = None
    for g in range(2):
        bank = b_.pb[g]
        b_.wait("t", b_.pb_free[g])
        b_.wait("t", tk_ib)
        mm = None
        for j in range(8):
            t = g * 8 + j
            mm = E["t"].transpose(bank[:, j * 64:(j + 1) * 64], A["INFLB"][:, ts(t)], IDB[0:64, 0:64])
        tk_t = b_.sig("t", mm)
        b_.wait("v", tk_t)
        tk_itm = b_.sig("v", E["v"].tensor_copy(out=A["INFLTM"][:, g * 8:(g + 1) * 8, :],
                                                in_=bank[:, 0:512]))
        b_.pb_free[g] = tk_itm

    tk_g = None
    for dh in range(2):
        bank = b_.ps[2 + dh]
        b_.wait("t", b_.ps_free[2 + dh])
        b_.wait("t", tk_itm)
        for t in range(NT):
            mm = E["t"].matmul(bank[0:64, :], A["INFLTM"][:, t, :], LNX[:, t, ts(dh, 512)],
                               start=(t == 0), stop=(t == NT - 1))
        tk_mm = b_.sig("t", mm)
        b_.waits("v", [tk_mm, tk_tot])
        tk_g = b_.sig("v", E["v"].tensor_scalar(out=A["GATH"][:, ts(dh, 512)], in0=bank[0:64, :],
                                                scalar1=A["TOTR"][:], scalar2=None, op0=ALU.mult))
        b_.ps_free[2 + dh] = tk_g
    bank = b_.pb[0]
    b_.wait("t", b_.pb_free[0])
    b_.wait("t", tk_g)
    for icx in range(IC):
        mm = E["t"].transpose(bank[:, icx * 64:(icx + 1) * 64], A["GATH"][:, ts(icx)], IDB[0:64, 0:64])
    tk_t = b_.sig("t", mm)
    b_.wait("v", tk_t)
    tk_gt = b_.sig("v", E["v"].tensor_copy(out=A["GATHT"].rearrange("p ic f -> p (ic f)"),
                                           in_=bank[:, 0:512]))
    b_.pb_free[0] = tk_gt
    b_.wait("g", tk_g)
    tk_we = b_.dma("g", A["WSPL"][:], H["L0_attn_We"].rearrange("(ic p) f -> p ic f", p=128), sem="m3")
    bank = b_.ps[0]
    b_.wait("t", b_.ps_free[0])
    b_.waits("t", [tk_we, tk_gt])
    for icx in range(IC):
        E["t"].matmul(bank[0:64, :], A["GATHT"][:, icx, :], A["WSPL"][:, icx, :],
                      start=(icx == 0), stop=False)
    tk_mm = b_.sig("t", E["t"].matmul(bank[0:64, :], ONESR[0:1, 0:64], A["BROW"][0:1, 2048:2560],
                                      start=False, stop=True))
    b_.wait("s", tk_mm)
    tk_enc = b_.sig("s", E["s"].activation(out=A["ENC"][:], in_=bank[0:64, :], func=AF.Relu))
    b_.ps_free[0] = tk_enc
    bank = b_.pb[1]
    b_.wait("t", b_.pb_free[1])
    b_.wait("t", tk_enc)
    for ec in range(4):
        mm = E["t"].transpose(bank[:, ec * 64:(ec + 1) * 64], A["ENC"][:, ts(ec)], IDB[0:64, 0:64])
    tk_t = b_.sig("t", mm)
    b_.wait("v", tk_t)
    tk_enct = b_.sig("v", E["v"].tensor_copy(out=A["ENCT"].rearrange("p ic f -> p (ic f)"),
                                             in_=bank[:, 0:256]))
    b_.pb_free[1] = tk_enct

    WT4 = LNXP[:, 0:8192].rearrange("p (h ec f) -> p h ec f", h=4, ec=4)
    wt_sems = ["wa", "wb", "ha", "hb"]
    tk_wt = [None] * 4
    tk_used = [None] * 4
    tk_rowst = {0: None, 1: None}
    b_.wait("g", tk_enc)
    b_.wait("g", tk_g)

    def wt_load(s):
        sl = s % 4
        b_.wait("g", tk_used[sl])
        tk_wt[sl] = b_.dma("g", WT4[:, sl, :, :],
                           H["L0_attn_Wt"][s].rearrange("(ec p) f -> p ec f", p=128),
                           sem=wt_sems[sl])

    for s in range(4):
        wt_load(s)
    tk_wgh = None
    gate_units = [(dc, nb) for dc in range(IC) for nb in range(4)]
    gi = 0
    tk_gate_mm = None
    tk_sig_last = None

    def gate_unit():
        nonlocal gi, tk_wgh, tk_gate_mm, tk_sig_last
        if gi >= len(gate_units):
            return
        dc, nb = gate_units[gi]
        if gi % 16 == 0:
            ob = gi // 16
            b_.wait("g", tk_gate_mm)
            tk_wgh = b_.dma("g", A["WBIGH"][:],
                            H["L0_attn_Wg"][:, ob * 512:(ob + 1) * 512]
                            .rearrange("(ic p) o -> p ic o", p=128), sem="qa")
        nbs = slice(nb * 512, (nb + 1) * 512)
        bk = b_.ps[2 + gi % 2]
        b_.wait("t", b_.ps_free[2 + gi % 2])
        b_.wait("t", tk_wgh)
        for icx in range(IC):
            E["t"].matmul(bk[:], A["WBIGH"][:, icx, ts(dc % 4)], LNXT[:, icx, nbs],
                          start=(icx == 0), stop=False)
        tk_gate_mm = b_.sig("t", E["t"].matmul(bk[:], A["BROW"][0:1, ts(dc)], A["ONES512B"][:],
                                               start=False, stop=True))
        b_.wait("s", tk_gate_mm)
        tk_sig_last = b_.sig("s", E["s"].activation(out=MIXT[:, dc, nbs], in_=bk[:],
                                                    func=AF.Sigmoid))
        b_.ps_free[2 + gi % 2] = tk_sig_last
        gi += 1

    for s in range(64):
        sl = s % 4
        hh = s % 2
        bank = b_.ps[4 + hh]
        b_.wait("t", b_.ps_free[4 + hh])
        b_.wait("t", tk_wt[sl])
        b_.wait("t", tk_enct)
        mm = None
        for ec in range(4):
            mm = E["t"].matmul(bank[0:1, :], A["ENCT"][:, ec, s:s + 1], WT4[:, sl, ec, :],
                               start=(ec == 0), stop=(ec == 3))
        tk_mm = b_.sig("t", mm)
        tk_used[sl] = tk_mm
        if s + 4 < 64:
            wt_load(s + 4)
        eng = "s" if hh == 0 else "v"
        b_.wait(eng, tk_mm)
        b_.wait(eng, tk_rowst[hh])
        if eng == "s":
            tk_ev = b_.sig("s", E["s"].activation(out=A["ROWBUF"][0:1, hh, :],
                                                  in_=bank[0:1, :], func=AF.Identity))
        else:
            tk_ev = b_.sig("v", E["v"].tensor_copy(out=A["ROWBUF"][0:1, hh, :],
                                                   in_=bank[0:1, :]))
        b_.ps_free[4 + hh] = tk_ev
        b_.wait("y", tk_ev)
        tk_rowst[hh] = b_.dma("y", trd_h[s].rearrange("(u f) -> u f", u=1),
                              A["ROWBUF"][0:1, hh, :], sem=("ta" if hh == 0 else "tb"))
        gate_unit()
    while gi < len(gate_units):
        gate_unit()

    b_.wait("g", tk_rowst[0])
    b_.wait("g", tk_rowst[1])
    tk_sc = b_.dma("g", A["TRANSB"][:], trd_h[:], sem="m4")
    b_.wait("v", tk_sc)
    tk_tr = b_.seq("v", E["v"].tensor_tensor(out=A["TRANSB"][:], in0=A["TRANSB"][:],
                                             in1=A["BT16"][:], op=ALU.add))
    bank = b_.pb[0]
    b_.wait("t", b_.pb_free[0])
    b_.wait("t", tk_tr)
    for ec in range(4):
        mm = E["t"].transpose(bank[:, ec * 64:(ec + 1) * 64], A["TRANSB"][:, ts(ec)], IDB[0:64, 0:64])
    tk_t = b_.sig("t", mm)
    b_.wait("v", tk_t)
    tk_trt = b_.sig("v", E["v"].tensor_copy(out=A["TRANST"].rearrange("p ic f -> p (ic f)"),
                                            in_=bank[:, 0:256]))
    b_.pb_free[0] = tk_trt
    b_.wait("g", tk_used[0])
    b_.wait("g", tk_used[1])
    WD4 = A["WSPL"].rearrange("p ic f -> p (ic f)").rearrange("p (ec f) -> p ec f", ec=4)
    tk_wd = b_.dma("g", WD4[:], H["L0_attn_Wd"].rearrange("(ec p) f -> p ec f", p=128), sem="m3")
    tk_dec = None
    for dh in range(2):
        bank = b_.ps[dh]
        b_.wait("t", b_.ps_free[dh])
        b_.waits("t", [tk_wd, tk_trt])
        for ec in range(4):
            E["t"].matmul(bank[0:64, :], A["TRANST"][:, ec, :], WD4[:, ec, ts(dh, 512)],
                          start=(ec == 0), stop=False)
        tk_mm = b_.sig("t", E["t"].matmul(bank[0:64, :], ONESR[0:1, 0:64],
                                          A["BROW"][0:1, D + dh * 512:D + (dh + 1) * 512],
                                          start=False, stop=True))
        b_.wait("v", tk_mm)
        tk_dec = b_.sig("v", E["v"].tensor_copy(out=A["DEC"][:, ts(dh, 512)], in_=bank[0:64, :]))
        b_.ps_free[dh] = tk_dec

    idx = 0
    tk_mix04 = None
    for dc in range(IC):
        for nb in range(4):
            nbs = slice(nb * 512, (nb + 1) * 512)
            bf_ = b_.ps[2 + idx % 2]
            b_.wait("t", b_.ps_free[2 + idx % 2])
            b_.wait("t", tk_dec)
            tk_fmm = b_.sig("t", E["t"].matmul(bf_[:], A["DEC"][:, ts(dc)], A["INFLB"][:, nbs],
                                               start=True, stop=True))
            b_.waits("v", [tk_fmm, tk_sig_last])
            tk_mix04 = b_.sig("v", E["v"].scalar_tensor_tensor(out=MIXT[:, dc, nbs], in0=bf_[:],
                                                               scalar=0.4, in1=MIXT[:, dc, nbs],
                                                               op0=ALU.mult, op1=ALU.mult))
            b_.ps_free[2 + idx % 2] = tk_mix04
            idx += 1

    tk_vone = b_.sig("v", E["v"].memset(Vh[:, :, :, 64:65], 1.0))
    tk_vev = None
    tk_wv = None
    for ob in range(2):
        b_.wait("g", tk_gate_mm if ob == 0 else tk_vused)
        tk_wv = b_.dma("g", A["WBIGH"][:],
                       H["L0_attn_Wv"][:, ob * 512:(ob + 1) * 512]
                       .rearrange("(ic p) o -> p ic o", p=128), sem="qa")
        for t in range(NT):
            bank = b_.ps[t % 2]
            b_.wait("t", b_.ps_free[t % 2])
            b_.wait("t", tk_wv)
            for icx in range(IC):
                mm = E["t"].matmul(bank[:], LNXT[:, icx, ts(t)], A["WBIGH"][:, icx, :],
                                   start=(icx == 0), stop=(icx == IC - 1))
            tk_mm = b_.sig("t", mm)
            tk_vused = tk_mm
            b_.waits("v", [tk_mm, tk_vone])
            dst = Vh[:, t, ob * 8:(ob + 1) * 8, 0:64]
            tk_vev = b_.sig("v", E["v"].tensor_copy(out=dst, in_=bank[:]))
            b_.ps_free[t % 2] = tk_vev

    Q2 = A["Q2"]
    K2 = A["K2"]
    ATTS = [A["SIGT"][:, 0, :], A["SIGT"][:, 1, :], A["ATT"][:, 0, :], A["ATT"][:, 1, :]]
    SQV = A["SQB2"]
    st = {"qk_used": tk_vused, "negm": None, "lastexp": None, "sqmm": None}

    def emit_qk_dma(h):
        b_.wait("g", st["qk_used"])
        b_.wait("g", tk_gt)
        hs = slice(64 * h, 64 * (h + 1))
        wq = H["L0_attn_Wq"][:, hs].rearrange("(ic p) o -> p ic o", p=128)
        wk = H["L0_attn_Wk"][:, hs].rearrange("(ic p) o -> p ic o", p=128)
        b_.dma("g", A["WQS"][:, :, 0:64], wq, sem="qa")
        st["dq", h] = b_.dma("g", A["WQS"][:, :, 64:128], wq, sem="qa")
        b_.dma("g", A["WKS"][:, :, 0:64], wk, sem="qb")
        st["dk", h] = b_.dma("g", A["WKS"][:, :, 64:128], wk, sem="qb")

    def emit_proj_stab(h, fin_hook=None):
        mxq = A["TMPS"][0:1, 0:1]
        mxk = A["TMPS"][0:1, 1:2]
        tmp = A["TMPS"][0:1, 2:3]
        pending = []

        def pop_stab():
            j, nb, tk_s2 = pending.pop(0)
            mx = mxq if j == 0 else mxk
            pbk = b_.pb[nb % 2][:].bitcast(F32)
            b_.wait("t", b_.pb_free[nb % 2])
            b_.wait("t", tk_s2)
            tk_nm = b_.sig("t", E["t"].matmul(pbk[0:1, 0:512], A["ONESB"][:], ATTS[nb],
                                              start=True, stop=True))
            st["slot_rd", nb] = tk_nm
            b_.wait("v", tk_nm)
            if j == 0 and nb == 0:
                tk_r = b_.seq("v", E["v"].reduce_max(out=mx, in_=pbk[0:1, 0:512], axis=AX.X))
            else:
                tk_r = b_.seq("v", E["v"].reduce_max(out=tmp, in_=pbk[0:1, 0:512], axis=AX.X))
                tk_r = b_.seq("v", E["v"].tensor_tensor(out=mx, in0=(mx if nb > 0 else tmp),
                                                        in1=tmp, op=ALU.max))
            b_.pb_free[nb % 2] = tk_r

        for j, (which, WS, OUT) in enumerate((("q", A["WQS"], Q2), ("k", A["WKS"], K2))):
            for nb in range(4):
                nbs = slice(nb * 512, (nb + 1) * 512)
                bank = b_.ps[5]
                b_.wait("t", b_.ps_free[5])
                b_.wait("t", st["dq", h] if which == "q" else st["dk", h])
                for icx in range(IC):
                    mm = E["t"].matmul(bank[:], WS[:, icx, :], LNXT[:, icx, nbs],
                                       start=(icx == 0), stop=(icx == IC - 1))
                tk_mm = b_.sig("t", mm)
                if which == "k" and nb == 3:
                    st["qk_used"] = tk_mm
                b_.wait("v", tk_mm)
                tk_ev = b_.sig("v", E["v"].tensor_copy(out=OUT[:, nbs], in_=bank[:]))
                b_.ps_free[5] = tk_ev
                b_.wait("v", st.get(("slot_rd", nb)))
                b_.wait("v", tk_ev)
                tk_s2 = b_.seq("v", E["v"].tensor_tensor(out=ATTS[nb], in0=OUT[:, nbs],
                                                         in1=OUT[:, nbs], op=ALU.mult))
                pending.append((j, nb, tk_s2))
                if len(pending) > 2:
                    pop_stab()
                if j == 0 and nb == 1 and fin_hook is not None:
                    fin_hook()
        while pending:
            pop_stab()
        ms = A["TMPS"][0:1, 3:4]
        tk_m = b_.seq("v", E["v"].tensor_tensor(out=ms, in0=mxq, in1=mxk, op=ALU.add))
        tk_m = b_.seq("v", E["v"].tensor_scalar(out=ms, in0=ms, scalar1=-0.03125,
                                                scalar2=None, op0=ALU.mult))
        bank = b_.pb[0][:].bitcast(F32)
        b_.wait("t", b_.pb_free[0])
        b_.wait("t", tk_m)
        tk_bc = b_.sig("t", E["t"].matmul(bank[:, 0:1], A["ONESRF"][:], ms,
                                          start=True, stop=True))
        b_.waits("v", [tk_bc, st["lastexp"]])
        st["negm"] = b_.sig("v", E["v"].tensor_copy(out=A["NEGM"][:, 0:1], in_=bank[:, 0:1]))
        b_.pb_free[0] = st["negm"]

    def emit_fin(h, qq):
        qqs = slice(qq * 512, (qq + 1) * 512)
        b_.seq("v", E["v"].reciprocal(out=A["RDEN"][:], in_=A["RDEN"][:]))
        tk_rd = b_.seq("v", E["v"].tensor_scalar(out=A["RDEN"][:], in0=A["RDEN"][:],
                                                 scalar1=0.6, scalar2=None, op0=ALU.mult))
        bankb = b_.pb[1][:].bitcast(F32)
        b_.wait("t", b_.pb_free[1])
        b_.wait("t", tk_rd)
        tk_bc = b_.sig("t", E["t"].matmul(bankb[0:64, 0:512], A["ONESR64"][64:65, :], A["RDEN"][:],
                                          start=True, stop=True))
        b_.wait("v", tk_bc)
        tk_rb = b_.seq("v", E["v"].tensor_copy(out=A["RDENB2"][:], in_=bankb[0:64, 0:512]))
        tk_avs = b_.seq("v", E["v"].tensor_tensor(out=A["AVS"][:], in0=A["AVN"][:],
                                                  in1=A["RDENB2"][:], op=ALU.mult))
        b_.wait("v", tk_mix04)
        if h % 2 == 0:
            tk_acc = b_.sig("v", E["v"].tensor_tensor(out=MIXT[0:64, h // 2, qqs],
                                                      in0=A["AVS"][:],
                                                      in1=MIXT[0:64, h // 2, qqs],
                                                      op=ALU.add))
            b_.pb_free[1] = tk_rb
        else:
            b_.wait("t", tk_avs)
            tk_cp = b_.sig("t", E["t"].matmul(bankb[64:128, 0:512], IDB[0:64, 0:64], A["AVS"][:],
                                              start=True, stop=True, tile_position=(0, 64)))
            b_.wait("v", tk_cp)
            tk_acc = b_.sig("v", E["v"].tensor_tensor(out=MIXT[64:128, h // 2, qqs],
                                                      in0=bankb[64:128, 0:512],
                                                      in1=MIXT[64:128, h // 2, qqs],
                                                      op=ALU.add))
            b_.pb_free[1] = tk_acc
        st["acc"] = tk_acc
        st["avs_done"] = tk_avs

    def pop_fin():
        if st.get("fin") is not None:
            fh, fqq = st["fin"]
            st["fin"] = None
            emit_fin(fh, fqq)

    emit_qk_dma(0)
    st["fin"] = None
    st["acc"] = None
    for h in range(16):
        emit_proj_stab(h, fin_hook=pop_fin)
        if h + 1 < 16:
            emit_qk_dma(h + 1)
        tk_negm = st["negm"]
        for qq in range(4):
            qqs = slice(qq * 512, (qq + 1) * 512)
            av = b_.ps[2 + qq % 2]
            b_.wait("t", b_.ps_free[2 + qq % 2])
            tk_av = {}
            tk_sc = {}
            SCB = (0, 1, 4)

            def emit_scores(kt):
                sb_ = b_.ps[SCB[kt % 3]]
                b_.wait("t", b_.ps_free[SCB[kt % 3]])
                tk_sc[kt] = b_.sig("t", E["t"].matmul(sb_[:], K2[:, ts(kt)], Q2[:, qqs],
                                                      start=True, stop=True))

            def emit_exp(j):
                b_.waits("s", [tk_sc[j], tk_negm])
                if j >= 4:
                    b_.wait("s", tk_av[j - 4])
                tk_e = b_.sig("s", E["s"].activation(out=ATTS[j % 4], in_=b_.ps[SCB[j % 3]][:],
                                                     func=AF.Exp, scale=0.0625,
                                                     bias=A["NEGM"][:, 0:1]))
                st["lastexp"] = tk_e
                tk_exp[j] = tk_e
                b_.ps_free[SCB[j % 3]] = tk_e

            tk_exp = {}
            emit_scores(0)
            emit_scores(1)
            emit_scores(2)
            emit_exp(0)
            for kt in range(NT):
                if kt + 3 < NT:
                    emit_scores(kt + 3)
                if kt + 1 < NT:
                    emit_exp(kt + 1)
                b_.wait("t", tk_exp[kt])
                tk_av[kt] = b_.sig("t", E["t"].matmul(av[0:65, :], V[:, kt, 65 * h:65 * h + 65],
                                                      ATTS[kt % 4],
                                                      start=(kt == 0), stop=(kt == NT - 1)))
                if kt == 5:
                    pop_fin()
            b_.wait("v", tk_av[NT - 1])
            b_.wait("v", st.get("avs_done"))
            b_.seq("v", E["v"].tensor_copy(out=A["RDEN"][:], in_=av[64:65, :]))
            tk_an = b_.seq("v", E["v"].tensor_copy(out=A["AVN"][:], in_=av[0:64, :]))
            b_.ps_free[2 + qq % 2] = tk_an
            st["fin"] = (h, qq)
    pop_fin()
    tk_heads = st["acc"]

    tk_wo_used = tk_vused
    for dhf in range(2):
        b_.wait("g", tk_wo_used)
        tk_wo = b_.dma("g", A["WBIGH"][:],
                       H["L0_attn_Wo"][:, dhf * 512:(dhf + 1) * 512]
                       .rearrange("(ic p) o -> p ic o", p=128), sem="m4")
        for t in range(NT):
            bank = b_.ps[t % 2]
            b_.wait("t", b_.ps_free[t % 2])
            b_.waits("t", [tk_wo, tk_heads])
            for dc in range(IC):
                mm = E["t"].matmul(bank[:], MIXT[:, dc, ts(t)], A["WBIGH"][:, dc, :],
                                   start=(dc == 0), stop=(dc == IC - 1))
            tk_mm = b_.sig("t", mm)
            tk_wo_used = tk_mm
            b_.wait("v", tk_mm)
            tk_x = b_.sig("v", E["v"].tensor_tensor(out=X[:, t, ts(dhf, 512)], in0=bank[:],
                                                    in1=X[:, t, ts(dhf, 512)], op=ALU.add))
            b_.ps_free[t % 2] = tk_x
    b_.barrier()




C_ROUTE = 0.1 * math.log(8.0)


def emit_l1ffn(b_, nc, H, X, LNXP, LNXT, SCRB, A, F, TMP16, IOTA, IOTA16, ONESC, ONESR, IDF, w1pk_h, w2hp_h, tk_pw1, tk_pw2):
    E = b_.e
    eps_i = 10.0
    W1PKs = [LNXP[:, 0:1024].rearrange("p (ic f) -> p ic f", ic=IC),
             LNXP[:, 1024:2048].rearrange("p (ic f) -> p ic f", ic=IC),
             LNXP[:, 12288:13312].rearrange("p (ic f) -> p ic f", ic=IC),
             LNXP[:, 13312:14336].rearrange("p (ic f) -> p ic f", ic=IC)]
    W2BF = LNXP[:, 2048:4096].rearrange("p (j f) -> p j f", j=4)
    W1L = LNXP[:, 4096:8192].rearrange("p (ic f) -> p ic f", ic=IC)
    W2L = LNXP[:, 8192:12288].rearrange("p (hc f) -> p hc f", hc=4)
    RTM = LNXP[:, 12288:14336].bitcast(F32)
    B2L = LNXP[0:1, 14336:15360]
    B2H = LNXP[0:1, 15360:16384]
    H1T = SCRB.rearrange("p (hc f) -> p hc f", hc=4)
    HHT = SCRB.rearrange("p (hc f) -> p hc f", hc=32)
    L1S = F["L1S"]

    def bcast128(src11, dst, tag):
        bank = b_.ps[5]
        b_.wait("t", b_.ps_free[5])
        b_.wait("t", tag)
        tk = b_.sig("t", E["t"].matmul(bank[:, 0:1], A["ONESRF"][:], src11, start=True, stop=True))
        b_.wait("v", tk)
        tk2 = b_.seq("v", E["v"].tensor_copy(out=dst, in_=bank[:, 0:1]))
        b_.ps_free[5] = tk2
        return tk2

    def psum_row(src, dst11, red, tag_v):
        tk = b_.seq("v", E["v"].reduce_sum(out=red, in_=src, axis=AX.X))
        bank = b_.ps[5]
        b_.wait("t", b_.ps_free[5])
        b_.wait("t", tk)
        tk2 = b_.sig("t", E["t"].matmul(bank[0:1, 0:1], ONESC[:], red, start=True, stop=True))
        b_.wait("v", tk2)
        tk3 = b_.seq("v", E["v"].tensor_copy(out=dst11, in_=bank[0:1, 0:1]))
        b_.ps_free[5] = tk3
        return tk3

    w1_sems = ["wa", "wb", "ha", "hb"]
    w2_sems = ["qa", "qb", "hg", "hh"]
    st1 = {"n": 0, "rd": [None] * 4, "ld": [None] * 4}
    st2 = {"n": 0, "rd": [None] * 4, "ld": [None] * 4}

    def issue1():
        if st1["n"] >= 128:
            return
        k = st1["n"]
        sl = k % 4
        b_.wait("g", st1["rd"][sl])
        st1["ld"][sl] = b_.dma("g", W1PKs[sl][:],
                               w1pk_h[k % 32].rearrange("p (ic f) -> p ic f", ic=IC),
                               sem=w1_sems[sl])
        st1["n"] += 1

    def issue2():
        if st2["n"] >= 256:
            return
        j = st2["n"]
        hc2 = j % 32
        half2 = (j // 32) % 2
        sl = j % 4
        b_.wait("g", st2["rd"][sl])
        st2["ld"][sl] = b_.dma("g", W2BF[:, sl, :], w2hp_h[hc2][:, ts(half2, 512)], sem=w2_sems[sl])
        st2["n"] += 1

    tk_rt = b_.dma("y", RTM[0:1, :], H["L1_ffn_routing_token"].rearrange("(u d) -> u d", u=1), sem="m2")
    b_.dma("g", B2L, H["L1_ffn_b2_light"].rearrange("(u d) -> u d", u=1), sem="m3")
    tk_b2 = b_.dma("g", B2H, H["L1_ffn_b2_heavy"].rearrange("(u d) -> u d", u=1), sem="m3")
    b_.wait("g", tk_pw1)
    b_.wait("g", tk_pw2)
    issue1()
    issue1()
    for _ in range(4):
        issue2()
    tk_w2l = b_.dma("g", W2L[:], H["L1_ffn_w2_light"].rearrange("(hc p) o -> p hc o", p=128), sem="m4")
    tk_wl = b_.dma("g", W1L[:], H["L1_ffn_w1_light"].rearrange("(ic p) o -> p ic o", p=128), sem="m4")
    b_.wait("v", tk_wl)
    tk_gl = None
    for icx in range(IC):
        tk_gl = b_.seq("v", E["v"].tensor_scalar(out=W1L[:, icx, :], in0=W1L[:, icx, :],
                                                 scalar1=F["GAML"][:, icx:icx + 1],
                                                 scalar2=None, op0=ALU.mult))
    b_.wait("t", tk_rt)
    for half in range(2):
        bank = b_.ps[4]
        b_.wait("t", b_.ps_free[4])
        tk = b_.sig("t", E["t"].matmul(bank[:], A["ONESRF"][:], RTM[0:1, ts(half, 512)],
                                       start=True, stop=True))
        b_.wait("v", tk)
        tk_rtm = b_.seq("v", E["v"].tensor_copy(out=RTM[:, ts(half, 512)], in_=bank[:]))
        b_.ps_free[4] = tk_rtm
    tk_s1 = b_.seq("v", E["v"].reduce_sum(out=L1S[0:1, 2:3], in_=RTM[0:1, :], axis=AX.X))
    tk_srt = bcast128(L1S[0:1, 2:3], F["SRTB"][:], tk_s1)
    tk_x = None
    for t in range(NT):
        tk_x = b_.seq("v", E["v"].scalar_tensor_tensor(out=SCRB[:, 0:2048].bitcast(F32),
                                                       in0=X[:, t, :], scalar=1.0,
                                                       in1=RTM[:], op0=ALU.bypass, op1=ALU.mult,
                                                       accum_out=F["XRT"][:, t:t + 1]))
    b_.seq("v", E["v"].tensor_tensor(out=F["MASKB"][:], in0=TMP16[:],
                                     in1=F["SRTB"][:].to_broadcast([128, NT]), op=ALU.mult))
    b_.seq("v", E["v"].tensor_tensor(out=F["SALL"][:], in0=F["XRT"][:], in1=F["MASKB"][:], op=ALU.subtract))
    tk_s = b_.seq("v", E["v"].tensor_tensor(out=F["SALL"][:], in0=F["SALL"][:], in1=F["RLN"][:], op=ALU.mult))
    tk1 = b_.seq("v", E["v"].reduce_max(out=F["RED"][:], in_=F["SALL"][:], axis=AX.X))
    bank = b_.ps[4]
    b_.wait("t", b_.ps_free[4])
    b_.wait("t", tk1)
    tkt = b_.sig("t", E["t"].transpose(bank[0:1, 0:128], F["RED"][:], IDF[:]))
    b_.wait("v", tkt)
    tk2 = b_.seq("v", E["v"].reduce_max(out=L1S[0:1, 3:4], in_=bank[0:1, 0:128], axis=AX.X))
    b_.ps_free[4] = tk2
    b_.wait("g", tk_x)
    issue1()
    issue1()

    cell = {"h1": None, "mm1last": None}

    def u_mm1(oc, nb):
        bank_ = b_.ps[nb]
        b_.wait("t", b_.ps_free[nb])
        b_.wait("t", tk_gl)
        for icx in range(IC):
            mm = E["t"].matmul(bank_[:], W1L[:, icx, ts(oc)], LNXT[:, icx, slice(nb * 512, nb * 512 + 512)],
                               start=(icx == 0), stop=(icx == IC - 1))
        tk_mm = b_.sig("t", mm)
        cell["mm1last"] = tk_mm
        b_.wait("v", tk_mm)
        tk_h1 = b_.sig("v", E["v"].tensor_copy(out=H1T[:, oc, slice(2048 + nb * 512, 2048 + nb * 512 + 512)],
                                               in_=bank_[:]))
        cell["pre", oc, nb] = tk_h1
        b_.ps_free[nb] = tk_h1

    def u_mm2(t, half):
        bank_ = b_.ps[(2 * t + half) % 4]
        b_.wait("t", b_.ps_free[(2 * t + half) % 4])
        b_.waits("t", [tk_wl, cell["h1"]])
        for hc in range(4):
            E["t"].matmul(bank_[:], H1T[:, hc, ts(t)], W2L[:, hc, ts(half, 512)],
                          start=(hc == 0), stop=False)
        b_.wait("t", tk_b2)
        tk_mm = b_.sig("t", E["t"].matmul(bank_[:], ONESR[0:1, :], B2L[0:1, ts(half, 512)],
                                          start=False, stop=True))
        b_.wait("v", tk_mm)
        tk_xl = b_.sig("v", E["v"].scalar_tensor_tensor(out=X[:, t, ts(half, 512)], in0=bank_[:],
                                                        scalar=0.5, in1=X[:, t, ts(half, 512)],
                                                        op0=ALU.mult, op1=ALU.add))
        b_.ps_free[(2 * t + half) % 4] = tk_xl

    fillers = [(lambda oc=oc, nb=nb: u_mm1(oc, nb)) for oc in range(4) for nb in range(4)]
    fillers_mm2 = [(lambda t=t, half=half: u_mm2(t, half)) for t in range(NT) for half in range(2)]

    def fill(n=1):
        for _ in range(n):
            if fillers:
                fillers.pop(0)()
    b_.seq("v", E["v"].memset(L1S[0:1, 0:1], 0.0))
    tk_a = None
    for it in range(50):
        b_.seq("v", E["v"].tensor_scalar(out=L1S[0:1, 5:6], in0=L1S[0:1, 0:1], scalar1=-1.0,
                                         scalar2=None, op0=ALU.mult))
        b_.seq("v", E["v"].tensor_tensor(out=L1S[0:1, 6:7], in0=L1S[0:1, 3:4],
                                         in1=L1S[0:1, 5:6], op=ALU.min))
        tk_nab = b_.seq("v", E["v"].tensor_scalar(out=L1S[0:1, 7:8], in0=L1S[0:1, 6:7],
                                                  scalar1=-eps_i, scalar2=None, op0=ALU.mult))
        bank = b_.ps[5]
        b_.wait("t", b_.ps_free[5])
        b_.wait("t", tk_nab)
        tkb = b_.sig("t", E["t"].matmul(bank[:, 0:2], A["ONESRF"][:], L1S[0:1, 5:8:2],
                                        start=True, stop=True))
        b_.wait("v", tkb)
        tk_nb = b_.seq("v", E["v"].tensor_copy(out=F["NAB"][:], in_=bank[:, 0:2]))
        b_.ps_free[5] = tk_nb
        tk_u = b_.seq("v", E["v"].tensor_scalar(out=F["W16"][:], in0=F["SALL"][:],
                                                scalar1=F["NAB"][:, 0:1], scalar2=None, op0=ALU.min))
        b_.wait("s", tk_u)
        tk_e = b_.sig("s", E["s"].activation(out=F["EJ"][:], in_=F["W16"][:], func=AF.Exp,
                                             scale=eps_i, bias=F["NAB"][:, 1:2],
                                             accum_out=F["RED"][:]))
        bank = b_.ps[5]
        b_.wait("t", b_.ps_free[5])
        b_.wait("t", tk_e)
        tks = b_.sig("t", E["t"].matmul(bank[0:1, 0:1], ONESC[:], F["RED"][:], start=True, stop=True))
        b_.wait("s", tks)
        tk_ln = b_.sig("s", E["s"].activation(out=L1S[0:1, 8:9], in_=bank[0:1, 0:1], func=AF.Ln))
        b_.ps_free[5] = tk_ln
        b_.wait("v", tk_ln)
        b_.seq("v", E["v"].tensor_scalar(out=L1S[0:1, 8:9], in0=L1S[0:1, 8:9], scalar1=-0.1,
                                         scalar2=C_ROUTE, op0=ALU.mult, op1=ALU.add))
        tk_a = b_.seq("v", E["v"].tensor_tensor(out=L1S[0:1, 0:1], in0=L1S[0:1, 8:9],
                                                in1=L1S[0:1, 6:7], op=ALU.subtract))
        fill(1)
    tk_ab = bcast128(L1S[0:1, 0:1], F["NAB"][:, 0:1], tk_a)
    b_.seq("v", E["v"].tensor_scalar(out=F["W16"][:], in0=F["SALL"][:], scalar1=F["NAB"][:, 0:1],
                                     scalar2=0.0, op0=ALU.add, op1=ALU.min))
    tk_w = b_.seq("v", E["v"].tensor_scalar(out=F["W16"][:], in0=F["W16"][:], scalar1=10.0,
                                            scalar2=None, op0=ALU.mult))
    b_.wait("s", tk_w)
    tk_sc = b_.sig("s", E["s"].activation(out=F["SCO"][:], in_=F["W16"][:], func=AF.Exp))
    b_.wait("v", tk_sc)
    for oc in range(4):
        for nb in range(4):
            b_.waits("s", [cell["pre", oc, nb], tk_x])
            cell["h1"] = b_.sig("s", E["s"].activation(
                out=H1T[:, oc, slice(nb * 512, nb * 512 + 512)],
                in_=H1T[:, oc, slice(2048 + nb * 512, 2048 + nb * 512 + 512)],
                func=AF.Gelu, bias=F["B1LC"][:, oc:oc + 1]))
    fillers.extend(fillers_mm2)
    b_.wait("v", cell["mm1last"])
    tk_gsc = None
    for icx in range(IC):
        tk_gsc = b_.seq("v", E["v"].tensor_scalar(out=LNXT[:, icx, :], in0=LNXT[:, icx, :],
                                                  scalar1=F["GAMH"][:, icx:icx + 1],
                                                  scalar2=None, op0=ALU.mult))
    b_.seq("v", E["v"].tensor_scalar(out=F["Z"][:], in0=F["SCO"][:], scalar1=F["ZTH"][:, 0:1],
                                     scalar2=None, op0=ALU.is_lt))
    b_.seq("v", E["v"].tensor_scalar(out=F["MASKB"][:], in0=F["SCO"][:], scalar1=0.0,
                                     scalar2=None, op0=ALU.is_equal))
    tk_z = b_.seq("v", E["v"].tensor_tensor(out=F["Z"][:], in0=F["Z"][:], in1=F["MASKB"][:],
                                            op=ALU.logical_or))
    tk_nz = psum_row(F["Z"][:], L1S[0:1, 9:10], F["RED"][:], tk_z)
    b_.seq("v", E["v"].tensor_scalar(out=L1S[0:1, 9:10], in0=L1S[0:1, 9:10], scalar1=-1.0,
                                     scalar2=2048.0, op0=ALU.mult, op1=ALU.add))
    tk_gg = b_.seq("v", E["v"].tensor_scalar(out=L1S[0:1, 10:11], in0=L1S[0:1, 9:10],
                                             scalar1=1024.0, scalar2=None, op0=ALU.is_ge))
    b_.seq("v", E["v"].memset(L1S[0:1, 11:12], -21000.0))
    b_.seq("v", E["v"].memset(L1S[0:1, 12:13], 1.0))
    for lev in range(8):
        b_.seq("v", E["v"].tensor_tensor(out=L1S[0:1, 13:14], in0=L1S[0:1, 12:13],
                                         in1=L1S[0:1, 11:12], op=ALU.subtract))
        b_.seq("v", E["v"].tensor_scalar(out=L1S[0:1, 13:14], in0=L1S[0:1, 13:14],
                                         scalar1=1.0 / 16, scalar2=None, op0=ALU.mult))
        tk_tr = b_.seq("v", E["v"].tensor_scalar(out=F["CROW"][:], in0=IOTA16[:],
                                                 scalar1=L1S[0:1, 13:14], scalar2=L1S[0:1, 11:12],
                                                 op0=ALU.mult, op1=ALU.add))
        bank = b_.ps[5]
        b_.wait("t", b_.ps_free[5])
        b_.wait("t", tk_tr)
        tkb = b_.sig("t", E["t"].matmul(bank[:, 0:16], A["ONESRF"][:], F["CROW"][:],
                                        start=True, stop=True))
        b_.wait("v", tkb)
        tk_th = b_.seq("v", E["v"].tensor_copy(out=F["THB"][:], in_=bank[:, 0:16]))
        b_.ps_free[5] = tk_th
        b_.seq("v", E["v"].tensor_tensor(
            out=F["IND"][:],
            in0=F["W16"].rearrange("p (u t) -> p u t", u=1).to_broadcast([128, 16, NT]),
            in1=F["THB"].rearrange("p (j u) -> p j u", u=1).to_broadcast([128, 16, NT]),
            op=ALU.is_ge))
        tk_c = b_.seq("v", E["v"].reduce_sum(out=F["CNT"].rearrange("p (j u) -> p j u", u=1),
                                             in_=F["IND"][:], axis=AX.X))
        bank = b_.ps[5]
        b_.wait("t", b_.ps_free[5])
        b_.wait("t", tk_c)
        tks = b_.sig("t", E["t"].matmul(bank[0:1, 0:16], ONESC[:], F["CNT"][:], start=True, stop=True))
        b_.wait("v", tks)
        tk_cr = b_.seq("v", E["v"].tensor_copy(out=F["CROW"][:], in_=bank[0:1, 0:16]))
        b_.ps_free[5] = tk_cr
        b_.seq("v", E["v"].tensor_scalar(out=F["CROW"][:], in0=F["CROW"][:], scalar1=1024.0,
                                         scalar2=None, op0=ALU.is_ge))
        b_.seq("v", E["v"].reduce_sum(out=L1S[0:1, 14:15], in_=F["CROW"][:], axis=AX.X))
        b_.seq("v", E["v"].tensor_scalar(out=L1S[0:1, 14:15], in0=L1S[0:1, 14:15], scalar1=-1.0,
                                         scalar2=None, op0=ALU.add))
        b_.seq("v", E["v"].tensor_tensor(out=L1S[0:1, 15:16], in0=L1S[0:1, 14:15],
                                         in1=L1S[0:1, 13:14], op=ALU.mult))
        b_.seq("v", E["v"].tensor_tensor(out=L1S[0:1, 11:12], in0=L1S[0:1, 11:12],
                                         in1=L1S[0:1, 15:16], op=ALU.add))
        tk_lo = b_.seq("v", E["v"].tensor_tensor(out=L1S[0:1, 12:13], in0=L1S[0:1, 11:12],
                                                 in1=L1S[0:1, 13:14], op=ALU.add))
        fill(2)
    tk_tb = bcast128(L1S[0:1, 11:12], F["NAB"][:, 1:2], tk_lo)
    b_.seq("v", E["v"].tensor_scalar(out=F["MASKB"][:], in0=F["W16"][:], scalar1=F["NAB"][:, 1:2],
                                     scalar2=None, op0=ALU.is_gt))
    b_.seq("v", E["v"].tensor_scalar(out=F["TIE"][:], in0=F["W16"][:], scalar1=F["NAB"][:, 1:2],
                                     scalar2=None, op0=ALU.is_ge))
    b_.seq("v", E["v"].tensor_tensor(out=F["TIE"][:], in0=F["TIE"][:], in1=F["MASKB"][:],
                                     op=ALU.subtract))
    tk_gb = bcast128(L1S[0:1, 10:11], F["NAB"][:, 0:1], tk_gg)
    b_.seq("v", E["v"].tensor_scalar(out=F["SCO"][:], in0=F["Z"][:], scalar1=-1.0, scalar2=1.0,
                                     op0=ALU.mult, op1=ALU.add))
    b_.seq("v", E["v"].tensor_tensor(out=F["MASKB"][:], in0=F["MASKB"][:], in1=F["SCO"][:],
                                     op=ALU.subtract))
    b_.seq("v", E["v"].scalar_tensor_tensor(out=F["MASKB"][:], in0=F["MASKB"][:],
                                            scalar=F["NAB"][:, 0:1], in1=F["SCO"][:],
                                            op0=ALU.mult, op1=ALU.add))
    b_.seq("v", E["v"].tensor_tensor(out=F["TIE"][:], in0=F["TIE"][:], in1=F["Z"][:],
                                     op=ALU.subtract))
    tk_tie = b_.seq("v", E["v"].scalar_tensor_tensor(out=F["TIE"][:], in0=F["TIE"][:],
                                                     scalar=F["NAB"][:, 0:1], in1=F["Z"][:],
                                                     op0=ALU.mult, op1=ALU.add))
    tk_cb = psum_row(F["MASKB"][:], L1S[0:1, 16:17], F["RED"][:], tk_tie)
    tk_R = b_.seq("v", E["v"].tensor_scalar(out=L1S[0:1, 16:17], in0=L1S[0:1, 16:17], scalar1=-1.0,
                                            scalar2=1024.0, op0=ALU.mult, op1=ALU.add))
    b_.seq("v", E["v"].memset(L1S[0:1, 17:18], -1.0))
    b_.seq("v", E["v"].memset(L1S[0:1, 18:19], 256.0))
    for lev in range(3):
        tk_tr = b_.seq("v", E["v"].tensor_scalar(out=F["CROW"][:], in0=IOTA16[:],
                                                 scalar1=L1S[0:1, 18:19], scalar2=L1S[0:1, 17:18],
                                                 op0=ALU.mult, op1=ALU.add))
        bank = b_.ps[5]
        b_.wait("t", b_.ps_free[5])
        b_.wait("t", tk_tr)
        tkb = b_.sig("t", E["t"].matmul(bank[:, 0:16], A["ONESRF"][:], F["CROW"][:],
                                        start=True, stop=True))
        b_.wait("v", tkb)
        tk_th = b_.seq("v", E["v"].tensor_copy(out=F["THB"][:], in_=bank[:, 0:16]))
        b_.ps_free[5] = tk_th
        b_.seq("v", E["v"].tensor_tensor(
            out=F["IND"][:],
            in0=IOTA.rearrange("p (u t) -> p u t", u=1).to_broadcast([128, 16, NT]),
            in1=F["THB"].rearrange("p (j u) -> p j u", u=1).to_broadcast([128, 16, NT]),
            op=ALU.is_le))
        b_.seq("v", E["v"].tensor_tensor(
            out=F["IND"][:], in0=F["IND"][:],
            in1=F["TIE"].rearrange("p (u t) -> p u t", u=1).to_broadcast([128, 16, NT]),
            op=ALU.mult))
        tk_c = b_.seq("v", E["v"].reduce_sum(out=F["CNT"].rearrange("p (j u) -> p j u", u=1),
                                             in_=F["IND"][:], axis=AX.X))
        bank = b_.ps[5]
        b_.wait("t", b_.ps_free[5])
        b_.wait("t", tk_c)
        tks = b_.sig("t", E["t"].matmul(bank[0:1, 0:16], ONESC[:], F["CNT"][:], start=True, stop=True))
        b_.wait("v", tks)
        tk_cr = b_.seq("v", E["v"].tensor_copy(out=F["CROW"][:], in_=bank[0:1, 0:16]))
        b_.ps_free[5] = tk_cr
        b_.seq("v", E["v"].tensor_scalar(out=F["CROW"][:], in0=F["CROW"][:],
                                         scalar1=L1S[0:1, 16:17], scalar2=None, op0=ALU.is_le))
        b_.seq("v", E["v"].reduce_sum(out=L1S[0:1, 14:15], in_=F["CROW"][:], axis=AX.X))
        b_.seq("v", E["v"].tensor_scalar(out=L1S[0:1, 14:15], in0=L1S[0:1, 14:15], scalar1=-1.0,
                                         scalar2=None, op0=ALU.add))
        b_.seq("v", E["v"].tensor_tensor(out=L1S[0:1, 15:16], in0=L1S[0:1, 14:15],
                                         in1=L1S[0:1, 18:19], op=ALU.mult))
        b_.seq("v", E["v"].tensor_tensor(out=L1S[0:1, 17:18], in0=L1S[0:1, 17:18],
                                         in1=L1S[0:1, 15:16], op=ALU.add))
        tk_st = b_.seq("v", E["v"].tensor_scalar(out=L1S[0:1, 18:19], in0=L1S[0:1, 18:19],
                                                 scalar1=1.0 / 16, scalar2=None, op0=ALU.mult))
        fill(2)
    tk_ib = bcast128(L1S[0:1, 17:18], F["NAB"][:, 1:2], tk_st)
    b_.seq("v", E["v"].tensor_scalar(out=F["SCO"][:], in0=IOTA[:], scalar1=F["NAB"][:, 1:2],
                                     scalar2=None, op0=ALU.is_le))
    b_.seq("v", E["v"].tensor_tensor(out=F["SCO"][:], in0=F["SCO"][:], in1=F["TIE"][:], op=ALU.mult))
    b_.seq("v", E["v"].tensor_tensor(out=F["MASK05"][:], in0=F["MASKB"][:], in1=F["SCO"][:], op=ALU.add))
    tk_mask = b_.seq("v", E["v"].tensor_scalar(out=F["MASK05"][:], in0=F["MASK05"][:], scalar1=0.5,
                                               scalar2=None, op0=ALU.mult))

    fill(len(fillers))

    for grp in range(4):
        gs = slice(grp * 512, (grp + 1) * 512)
        tk_hht = {}
        for hc in range(32):
            k = grp * 32 + hc
            sl = k % 4
            bank = b_.ps[4 + hc % 2]
            b_.wait("t", b_.ps_free[4 + hc % 2])
            b_.wait("t", st1["ld"][sl])
            b_.wait("t", tk_gsc)
            for icx in range(IC):
                mm = E["t"].matmul(bank[:], W1PKs[sl][:, icx, :], LNXT[:, icx, gs],
                                   start=(icx == 0), stop=(icx == IC - 1))
            tk_mm = b_.sig("t", mm)
            st1["rd"][sl] = tk_mm
            issue1()
            b_.wait("s", tk_mm)
            tk_h = b_.sig("s", E["s"].activation(out=HHT[:, hc, :], in_=bank[:], func=AF.Gelu,
                                                 bias=F["B1HC"][:, hc:hc + 1]))
            tk_hht[hc] = tk_h
            b_.ps_free[4 + hc % 2] = tk_h
        for half in range(2):
            for hc in range(32):
                j = grp * 64 + half * 32 + hc
                sl = j % 4
                b_.wait("t", st2["ld"][sl])
                b_.wait("t", tk_hht.get(31))
                for tt in range(4):
                    bank = b_.ps[tt]
                    if hc == 0:
                        b_.wait("t", b_.ps_free[tt])
                    mm = E["t"].matmul(bank[:], HHT[:, hc, ts(tt)], W2BF[:, sl, 0:512],
                                       start=(hc == 0), stop=False)
                st2["rd"][sl] = b_.sig("t", mm)
                issue2()
            for tt in range(4):
                t = grp * 4 + tt
                bank = b_.ps[tt]
                b_.wait("t", tk_b2)
                tk_mm = b_.sig("t", E["t"].matmul(bank[:], ONESR[0:1, :], B2H[0:1, ts(half, 512)],
                                                  start=False, stop=True))
                b_.waits("v", [tk_mm, tk_mask])
                tk_xh = b_.sig("v", E["v"].scalar_tensor_tensor(out=X[:, t, ts(half, 512)], in0=bank[:],
                                                                scalar=F["MASK05"][:, t:t + 1],
                                                                in1=X[:, t, ts(half, 512)],
                                                                op0=ALU.mult, op1=ALU.add))
                b_.ps_free[tt] = tk_xh
    b_.barrier()




def emit_l1attn(b_, nc, H, X, LNXP, LNXT, SCRB, A, F, ONESR, ONESC, inwt_h, POOLC, IDB):
    E = b_.e
    SH = SCRB.rearrange("p (c f) -> p c f", c=16)
    IWB = LNXP[:, 0:2048].rearrange("p (j dc f) -> p j dc f", j=2, dc=IC)
    CV = LNXP[:, 2048:4096].bitcast(F32)
    HALO = LNXP[:, 4096:4224].rearrange("p (c j f) -> p c j f", c=16, j=2)
    XPT1 = LNXP[0:16, 4224:5248].bitcast(F32)
    XPT2 = LNXP[0:16, 5248:6272].bitcast(F32)
    AT = LNXP[0:16, 6272:10368].bitcast(F32)
    BT = LNXP[0:16, 10368:14464].bitcast(F32)
    ROWS = LNXP[0:1, 14464:15488].bitcast(F32)
    YSB = LNXP[0:16, 10368:12416]
    OWTP = POOLC[0:16, 0:1024]
    OBP = POOLC[0:1, 1024:2048]

    ow_stg = CV.bitcast(BF16)[:, 0:128].rearrange("p (ic j) -> p ic j", ic=IC)
    b_.dma("g", ow_stg, H["L1_attn_out_w"].rearrange("(ic p) j -> p ic j", p=128),
                   sem="m3", allow_slow_non_contiguous=True)
    tk_ow = b_.dma("g", OBP, H["L1_attn_out_b"].rearrange("(u d) -> u d", u=1), sem="m3")
    b_.wait("t", tk_ow)
    tk_owt = None
    for icx in range(IC):
        bank = b_.pb[icx % 2]
        b_.wait("t", b_.pb_free[icx % 2])
        tk_t = b_.sig("t", E["t"].transpose(bank[0:16, 0:128], ow_stg[:, icx, :], IDB[:]))
        b_.wait("v", tk_t)
        tk_owt = b_.sig("v", E["v"].tensor_copy(out=OWTP[0:16, ts(icx)], in_=bank[0:16, 0:128]))
        b_.pb_free[icx % 2] = tk_owt

    tk_halo0 = b_.sig("v", E["v"].memset(HALO[:, :, 0, 0:3], 0.0))
    DGB = A["SIGT"].rearrange("p j f -> p (j f)").rearrange("p (j k f) -> p j k f", j=2, k=4)
    CPAD = A["ATT"].rearrange("p j f -> p (j f)")[:, 0:515]
    st_dg = {}
    tk_scan_dep = {}
    tk_used = [None, None]
    tk_shread = None
    tk_ab_guard = None
    tk_bc_prev = None
    for blk in range(2):
        bs = slice(blk * 1024, (blk + 1) * 1024)
        tk_iwl = [None, None]
        b_.wait("g", tk_used[0])
        tk_iwl[0] = b_.dma("g", IWB[:, 0, :, :], inwt_h[0], sem="wa")
        b_.wait("s", tk_shread)
        tk_sh = {}
        for c in range(16):
            sl = c % 2
            if c + 1 < 16:
                b_.wait("g", tk_used[1 - sl])
                tk_iwl[1 - sl] = b_.dma("g", IWB[:, 1 - sl, :, :], inwt_h[c + 1],
                                        sem=("wa" if (1 - sl) == 0 else "wb"))
            for nb2 in range(2):
                bank = b_.ps[nb2]
                b_.wait("t", b_.ps_free[nb2])
                b_.wait("t", tk_iwl[sl])
                for dc in range(IC):
                    mm = E["t"].matmul(bank[:], IWB[:, sl, dc, :],
                                       LNXT[:, dc, slice(blk * 1024 + nb2 * 512, blk * 1024 + nb2 * 512 + 512)],
                                       start=(dc == 0), stop=(dc == IC - 1))
                tk_mm = b_.sig("t", mm)
                if nb2 == 1:
                    tk_used[sl] = tk_mm
                b_.wait("s", tk_mm)
                tk_raw = b_.sig("s", E["s"].activation(out=SH[:, c, nb2 * 512:(nb2 + 1) * 512],
                                                       in_=bank[:], func=AF.Identity,
                                                       bias=F["INBC"][:, c:c + 1]))
                b_.ps_free[nb2] = tk_raw
            raw = SH[:, c, :]
            hal = HALO[:, c, blk % 2, 0:3]
            b_.wait("v", tk_raw)
            b_.wait("v", tk_halo0)
            b_.wait("v", st_dg.get(c % 2))
            tk_dg = None
            for k in range(4):
                tk_dg = b_.seq("v", E["v"].tensor_scalar(out=DGB[:, c % 2, k, :], in0=IDB[:],
                                                         scalar1=F["CONVW"][:, c, k:k + 1],
                                                         scalar2=None, op0=ALU.mult))
            tk_hn = b_.seq("v", E["v"].tensor_copy(out=HALO[:, c, (blk + 1) % 2, 0:3],
                                                   in_=raw[:, 1021:1024]))
            b_.seq("v", E["v"].tensor_copy(out=CPAD[:, 0:3], in_=hal))
            tk_pad = b_.seq("v", E["v"].tensor_copy(out=CPAD[:, 3:515], in_=raw[:, 0:512]))
            tk_cmm = {}
            for nb2 in range(2):
                bank = b_.ps[2 if nb2 == 0 else 5]
                b_.wait("t", b_.ps_free[2 if nb2 == 0 else 5])
                b_.wait("t", tk_dg)
                b_.wait("t", tk_raw)
                b_.wait("t", tk_pad)
                for k in range(4):
                    if nb2 == 1:
                        mm = E["t"].matmul(bank[:], DGB[:, c % 2, k, :], raw[:, 509 + k:1021 + k],
                                           start=(k == 0), stop=(k == 3))
                    else:
                        mm = E["t"].matmul(bank[:], DGB[:, c % 2, k, :], CPAD[:, k:512 + k],
                                           start=(k == 0), stop=(k == 3))
                tk_cmm[nb2] = b_.sig("t", mm)
            st_dg[c % 2] = tk_cmm[1]
            b_.waits("s", [tk_cmm[0], tk_cmm[1], tk_hn])
            E["s"].activation(out=SH[:, c, 0:512], in_=b_.ps[2][:], func=AF.Silu,
                              bias=F["CBC"][:, c:c + 1])
            tk_sil = b_.sig("s", E["s"].activation(out=SH[:, c, 512:1024], in_=b_.ps[5][:],
                                                   func=AF.Silu, bias=F["CBC"][:, c:c + 1]))
            b_.ps_free[2] = tk_sil
            b_.ps_free[5] = tk_sil
            tk_sh[c] = tk_sil
            b_.wait("v", tk_sil)
        for nb2 in range(2):
            nsl = slice(nb2 * 512, (nb2 + 1) * 512)
            bank = b_.ps[2]
            bank5 = b_.ps[5]
            b_.wait("t", b_.ps_free[2])
            b_.wait("t", b_.ps_free[5])
            b_.wait("t", tk_sh[15])
            for c in range(16):
                E["t"].matmul(bank[0:16, :], F["XPWT"][:, c, 0:16], SH[:, c, nsl],
                              start=(c == 0), stop=False)
            E["t"].matmul(bank[0:16, :], F["XPBROW"][0:1, 0:16], A["ONES512B"][:],
                          start=False, stop=True)
            for c in range(16):
                E["t"].matmul(bank5[0:16, :], F["XPWT"][:, c, 16:32], SH[:, c, nsl],
                              start=(c == 0), stop=False)
            tk_mm = b_.sig("t", E["t"].matmul(bank5[0:16, :], F["XPBROW"][0:1, 16:32], A["ONES512B"][:],
                                              start=False, stop=True))
            tk_shread = tk_mm
            b_.wait("v", tk_mm)
            b_.wait("v", tk_ab_guard)
            tk_xp = b_.seq("v", E["v"].tensor_copy(out=XPT1[:], in_=bank[0:16, :]))
            tk_xp = b_.seq("v", E["v"].tensor_copy(out=XPT2[:], in_=bank5[0:16, :]))
            b_.ps_free[2] = tk_xp
            b_.ps_free[5] = tk_xp
            bank = b_.ps[3]
            b_.wait("t", b_.ps_free[3])
            for c in range(16):
                mm = E["t"].matmul(bank[0:1, :], A["ONESB"][:], SH[:, c, nsl],
                                   start=(c == 0), stop=(c == 15))
            tk_rs = b_.sig("t", mm)
            tk_shread = tk_rs
            b_.wait("v", tk_rs)
            b_.wait("v", tk_bc_prev)
            tk_ro = b_.seq("v", E["v"].tensor_copy(out=ROWS[:], in_=bank[0:1, :]))
            b_.ps_free[3] = tk_ro
            bank = b_.ps[4]
            b_.wait("t", b_.ps_free[4])
            b_.wait("t", tk_ro)
            tk_bc = b_.sig("t", E["t"].matmul(bank[0:16, :], A["ONESRF"][0:1, 0:16], ROWS[:],
                                              start=True, stop=True))
            tk_bc_prev = tk_bc
            b_.wait("s", tk_xp)
            ab = slice(blk * 1024 + nb2 * 512, blk * 1024 + nb2 * 512 + 512)
            b_.seq("s", E["s"].activation(out=AT[:, ab], in_=XPT1[:], func=AF.Exp))
            tk_a = b_.seq("s", E["s"].activation(out=AT[:, ab], in_=AT[:, ab], func=AF.Exp, scale=-1.0))
            tk_b1 = b_.sig("s", E["s"].activation(out=BT[:, ab], in_=XPT2[:], func=AF.Silu))
            b_.waits("v", [tk_b1, tk_bc])
            tk_b2 = b_.seq("v", E["v"].tensor_tensor(out=BT[:, ab], in0=BT[:, ab],
                                                     in1=bank[0:16, :], op=ALU.mult))
            b_.ps_free[4] = tk_b2
            tk_ab_guard = tk_a
            tk_scan_dep[(blk, nb2)] = (tk_a, tk_b2)
    b_.wait("s", tk_scan_dep[(1, 1)][0])
    b_.wait("v", tk_scan_dep[(1, 1)][0])
    tk_scan = b_.seq("v", E["v"].tensor_tensor_scan(out=AT[:, 0:2048],
                                                    data0=AT[:, 0:2048], data1=BT[:, 0:2048],
                                                    initial=0.0, op0=ALU.mult, op1=ALU.add))
    tk_ys = b_.seq("v", E["v"].tensor_copy(out=YSB[:], in_=AT[:, 0:2048]))
    for t in range(NT):
        for half in range(2):
            bank = b_.ps[(2 * t + half) % 2]
            b_.wait("t", b_.ps_free[(2 * t + half) % 2])
            b_.waits("t", [tk_ys, tk_owt])
            E["t"].matmul(bank[:], YSB[0:16, ts(t)], OWTP[0:16, ts(half, 512)],
                          start=True, stop=False)
            tk_mm = b_.sig("t", E["t"].matmul(bank[:], ONESR[0:1, :], OBP[0:1, ts(half, 512)],
                                              start=False, stop=True))
            b_.wait("v", tk_mm)
            tk_x = b_.sig("v", E["v"].tensor_tensor(out=X[:, t, ts(half, 512)], in0=bank[:],
                                                    in1=X[:, t, ts(half, 512)], op=ALU.add))
            b_.ps_free[(2 * t + half) % 2] = tk_x
    b_.barrier()



_NC_CACHE = {}


def kernel(**inputs):
    key = STAGES
    if key not in _NC_CACHE:
        _NC_CACHE[key] = build()
    nc = _NC_CACHE[key]
    x = np.ascontiguousarray(np.asarray(inputs["L0_attn_x"], dtype=np.float32))
    in_maps = []
    for c in range(8):
        m = {k: np.ascontiguousarray(np.asarray(v, dtype=np.float32))
             for k, v in inputs.items() if k != "L0_attn_x"}
        m["L0_attn_x"] = x[2 * c:2 * c + 2]
        in_maps.append(m)
    res = run_bass_kernel_spmd(nc, in_maps, core_ids=list(range(8)))
    out = np.concatenate([np.asarray(r["out"], dtype=np.float32)
                          for r in res.results], axis=0)
    return out.reshape(16, T, D)


# version 20
# speedup vs baseline: 1.0577x; 1.0025x over previous
import math
import numpy as np
import concourse.bass as bass
import concourse.mybir as mybir
from concourse.bass_utils import run_bass_kernel_spmd
from contextlib import ExitStack

F32 = mybir.dt.float32
BF16 = mybir.dt.bfloat16
AF = mybir.ActivationFunctionType
ALU = mybir.AluOpType
AX = mybir.AxisListType

D = 1024
T = 2048
NT = 16
IC = 8
NB = 2
NE = 8
LN_EPS = 1e-5

STAGES = 6
RACE_CHECK = False
DBG = 0
L1RAW = 0

ENGINES = "tvsgy"


class Bld:

    def __init__(self, nc, ctx):
        self.nc = nc
        self.e = {
            "t": nc.tensor, "v": nc.vector, "s": nc.scalar,
            "g": nc.gpsimd, "y": nc.sync,
        }
        self.sem = {k: ctx.enter_context(nc.semaphore(f"sp_{k}")) for k in ENGINES}
        dkeys = ["pp", "ppst", "wa", "wb", "xls", "sto", "mw", "m2", "m3", "m4", "qa", "qb", "ta", "tb",
                 "ha", "hb", "hg", "hh", "pw1", "pw2", "pp2", "pp3", "pp4", "pst2",
                 "so0", "so1", "so2", "so3"]
        self.dsem = {k: ctx.enter_context(nc.semaphore(f"sd_{k}")) for k in dkeys}
        self.cnt = {k: 0 for k in ENGINES}
        self.dcnt = {k: 0 for k in dkeys}
        self.waited = {}
        self.ps = [ctx.enter_context(nc.psum_tensor(f"ps{i}", [128, 512], F32))
                   for i in range(6)]
        self.pb = [ctx.enter_context(nc.psum_tensor(f"pb{i}", [128, 1024], BF16))
                   for i in range(2)]
        self.ps_free = [None] * 6
        self.pb_free = [None] * 2

    def sig(self, k, inst):
        self.cnt[k] += 1
        inst.then_inc(self.sem[k], 1)
        return ("p", k, self.cnt[k])

    def seq(self, k, inst):
        tk = self.sig(k, inst)
        self.wait(k, tk)
        return tk

    def tok(self, k):
        return ("p", k, self.cnt[k])

    def dma(self, k, out, in_, sem="mw", **kw):
        inst = self.e[k].dma_start(out=out, in_=in_, **kw)
        self.dcnt[sem] += 16
        inst.then_inc(self.dsem[sem], 16)
        return ("d", sem, self.dcnt[sem])

    ORDERED = {"ppst"}

    def wait(self, ck, tok):
        if tok is None:
            return
        kind, pk, v = tok
        key = (ck, kind, pk)
        if self.waited.get(key, 0) >= v:
            return
        if kind == "d" and pk not in self.ORDERED:
            assert v == self.dcnt[pk], (pk, v, self.dcnt[pk])
        sem = self.sem[pk] if kind == "p" else self.dsem[pk]
        self.e[ck].wait_ge(sem, v)
        self.waited[key] = v

    def waits(self, ck, toks):
        for t in toks:
            self.wait(ck, t)

    def barrier(self):
        for dk, v in self.dcnt.items():
            if v:
                self.wait("y", ("d", dk, v))
        toks = {}
        for k in ENGINES:
            self.e[k].drain()
            toks[k] = self.sig(k, self.e[k].nop())
        for ck in ENGINES:
            for pk in ENGINES:
                self.wait(ck, toks[pk])
        return toks


def ts(i, n=128):
    return slice(i * n, (i + 1) * n)


def build():
    nc = bass.Bass("TRN2", target_bir_lowering=False,
                   detect_race_conditions=RACE_CHECK)

    H = {}
    def di(name, shape):
        H[name] = nc.dram_tensor(name, list(shape), F32, kind="ExternalInput").ap()
    di("L0_attn_x", (NB, T, D))
    for n, s in [
        ("L0_attn_Wq", (D, D)), ("L0_attn_Wk", (D, D)), ("L0_attn_Wv", (D, D)),
        ("L0_attn_Wo", (D, D)), ("L0_attn_splat_positions", (64, D)),
        ("L0_attn_splat_scales", (64,)), ("L0_attn_splat_importance", (64,)),
        ("L0_attn_We", (D, 512)), ("L0_attn_be", (512,)),
        ("L0_attn_Wt", (64, 512, 512)), ("L0_attn_bt", (64, 512)),
        ("L0_attn_Wd", (512, D)), ("L0_attn_bd", (D,)),
        ("L0_attn_Wg", (D, D)), ("L0_attn_bg", (D,)),
        ("L0_ffn_Wg", (D, NE)), ("L0_ffn_bg", (NE,)),
        ("L0_ffn_We", (NE, D, D)), ("L0_ffn_be", (NE, D)),
        ("L1_attn_in_w", (2 * D, D)), ("L1_attn_in_b", (2 * D,)),
        ("L1_attn_conv_w", (2 * D, 1, 4)), ("L1_attn_conv_b", (2 * D,)),
        ("L1_attn_xp_w", (32, 2 * D)), ("L1_attn_xp_b", (32,)),
        ("L1_attn_out_w", (D, 16)), ("L1_attn_out_b", (D,)),
        ("L1_ffn_routing_token", (D,)), ("L1_ffn_gamma_light", (D,)),
        ("L1_ffn_w1_light", (D, 512)), ("L1_ffn_b1_light", (512,)),
        ("L1_ffn_w2_light", (512, D)), ("L1_ffn_b2_light", (D,)),
        ("L1_ffn_gamma_heavy", (D,)), ("L1_ffn_w1_heavy", (D, 4096)),
        ("L1_ffn_b1_heavy", (4096,)), ("L1_ffn_w2_heavy", (4096, D)),
        ("L1_ffn_b2_heavy", (D,)),
    ]:
        di(n, s)
    out_h = nc.dram_tensor("out", [NB, T, D], F32, kind="ExternalOutput").ap()
    if DBG:
        dbg_lnx = nc.dram_tensor("dbg_lnx", [128, NT, D], BF16, kind="ExternalOutput").ap()
        dbg_lnxt = nc.dram_tensor("dbg_lnxt", [128, IC, T], BF16, kind="ExternalOutput").ap()
        dbg_g05 = nc.dram_tensor("dbg_g05", [128, NT, NE], F32, kind="ExternalOutput").ap()
        dbg_gexp = nc.dram_tensor("dbg_gexp", [128, NT, NE], F32, kind="ExternalOutput").ap()
        dbg_stat = nc.dram_tensor("dbg_stat", [128, 4 * NT], F32, kind="ExternalOutput").ap()
        dbg_va = nc.dram_tensor("dbg_va", [128, 4 * NT], F32, kind="ExternalOutput").ap()
        dbg_wet = nc.dram_tensor("dbg_wet", [2, 128, IC * 1024], BF16, kind="ExternalOutput").ap()

    wet_h = nc.dram_tensor("wet_scr", [NE, 128, IC * 1024], BF16, kind="Internal").ap()
    trd_h = nc.dram_tensor("trd_scr", [64, 512], BF16, kind="Internal").ap()
    w1hp_h = nc.dram_tensor("w1hp_scr", [IC, 128, 4096], BF16, kind="Internal").ap()
    w1pk_h = nc.dram_tensor("w1pk_scr", [32, 128, 1024], BF16, kind="Internal").ap()
    w2hp_h = nc.dram_tensor("w2hp_scr", [32, 128, 1024], BF16, kind="Internal").ap()
    inwt_h = nc.dram_tensor("inwt_scr", [16, 128, IC, 128], BF16, kind="Internal").ap()

    ctx = ExitStack()
    with ctx:
        sb = lambda name, shape, dt=F32: ctx.enter_context(nc.sbuf_tensor(name, list(shape), dt))
        X = sb("X", [128, NT, D])
        LNXP = sb("LNXP", [128, 16640], BF16)
        LNX = LNXP[:, 0:NT * D].rearrange("p (t d) -> p t d", t=NT)
        LNXT = sb("LNXT", [128, IC, T], BF16)
        SCRB = sb("SCRB", [128, 16384], BF16)
        SUM = sb("SUM", [128, NT])
        SQ = sb("SQ", [128, NT])
        VAR = sb("VAR", [128, NT])
        RSTD = sb("RSTD", [128, NT])
        BIASC = sb("BIASC", [128, NT])
        TMP16 = sb("TMP16", [128, NT])
        VA = sb("VA", [128, NT])
        VB = sb("VB", [128, NT])
        SCR2 = sb("SCR2", [128, 2048], BF16)
        JUNK = SCR2.rearrange("p (j d) -> p j d", j=2)
        GEXP = sb("GEXP", [128, NT, NE])
        GSEL = sb("GSEL", [128, NT, NE])
        G05 = sb("G05", [128, NT, NE])
        M8 = sb("M8", [128, NT, NE])
        GSUM = sb("GSUM", [128, NT])
        WGMOE = sb("WGMOE", [128, IC, NE], BF16)
        BEROW = SCR2[0:1, 0:2048].rearrange("p (j d) -> p j d", j=2)
        BGMOE = sb("BGMOE", [1, NE], BF16)
        ONESR = sb("ONESR", [1, 128], BF16)
        ONESC = sb("ONESC", [128, 1])
        IDB = sb("IDB", [128, 128], BF16)
        IDF = sb("IDF", [128, 128])
        IOTA = sb("IOTA", [128, NT])
        IOTA16 = sb("IOTA16", [1, 16])
        CONSTS = sb("CONSTS", [128, 8])
        A = {}
        A["SPTN"] = sb("SPTN", [128, IC, 64], BF16)
        A["S2HB"] = sb("S2HB", [1, 64], BF16)
        A["NH"] = sb("NH", [64, 1])
        A["AIMP"] = sb("AIMP", [64, 1])
        INFLF = sb("INFL32", [128, T])
        A["INFL32"] = INFLF[0:64, :]
        A["QTH"] = A["INFL32"][:, 0:D].bitcast(BF16)
        A["KTH"] = A["INFL32"][:, D:T].bitcast(BF16)
        A["Q2"] = INFLF[:, 0:D].bitcast(BF16)
        A["K2"] = INFLF[:, D:T].bitcast(BF16)
        A["INFLB"] = sb("INFLB", [64, T], BF16)
        A["AVS"] = A["INFLB"][:, 0:512]
        A["AVN"] = A["INFLB"][:, 512:1024]
        A["INFLTM"] = sb("INFLTM", [128, NT, 64], BF16)
        A["TOTR"] = sb("TOTR", [64, 1])
        POOLC = sb("POOLC", [128, 2048], BF16)
        A["GATH"] = POOLC[0:64, 0:D]
        A["GATHT"] = POOLC[:, D:D + 512].rearrange("p (ic f) -> p ic f", ic=IC)
        A["ENC"] = POOLC[0:64, D + 512:T]
        A["WQS"] = POOLC[:, 0:1024].rearrange("p (ic f) -> p ic f", ic=IC)
        A["WKS"] = POOLC[:, 1024:2048].rearrange("p (ic f) -> p ic f", ic=IC)
        A["RDENB"] = POOLC[0:64, D + 512:T]
        A["ENCT"] = sb("ENCT", [128, 4, 64], BF16)
        A["ROWBUF"] = sb("ROWBUF", [1, 2, 512], BF16)
        TRB128 = sb("TRANSB", [128, 512], BF16)
        A["TRANSB"] = TRB128[0:64, :]
        A["SQB2"] = TRB128
        A["RDENB2"] = TRB128[0:64, :]
        A["BT16"] = sb("BT16", [64, 512], BF16)
        A["TRANST"] = sb("TRANST", [128, 4, 64], BF16)
        DPOOL = sb("DPOOL", [128, D], BF16)
        A["DEC"] = DPOOL[0:64, :]
        A["SQB"] = DPOOL[0:64, :].rearrange("p (j f) -> p j f", j=2)
        A["RDEN"] = DPOOL[64:65, :].bitcast(F32)
        A["ONESR64"] = sb("ONESR64", [128, 64])
        A["SIGT"] = SCR2[:, 0:1024].rearrange("p (j f) -> p j f", j=2)
        A["ATT"] = SCR2[:, 1024:2048].rearrange("p (j f) -> p j f", j=2)
        A["X2P"] = SCR2[0:1, 1024:2048].rearrange("p (j f) -> p j f", j=2)
        A["WSPL"] = LNXP[:, 0:4096].rearrange("p (ic f) -> p ic f", ic=IC)
        A["WBIGH"] = sb("WBIGH", [128, IC, 512], BF16)
        A["NEGM"] = sb("NEGM", [128, 2])
        A["TMPS"] = sb("TMPS", [1, 16])
        A["BROW"] = A["INFL32"][0:1, 0:1280].bitcast(BF16)
        A["ONES512B"] = sb("ONES512B", [1, 512], BF16)
        A["ONESB"] = sb("ONESB", [128, 1], BF16)
        A["ONESRF"] = sb("ONESRF", [1, 128])
        F = {}
        F["B1HC"] = sb("B1HC", [128, 32])
        F["B1LC"] = sb("B1LC", [128, 4])
        F["GAML"] = sb("GAML", [128, IC])
        F["GAMH"] = sb("GAMH", [128, IC])
        F["RLN"] = sb("RLN", [128, NT])
        F["XRT"] = sb("XRT", [128, NT])
        F["SALL"] = sb("SALL", [128, NT])
        F["W16"] = sb("W16", [128, NT])
        F["SCO"] = sb("SCO", [128, NT])
        F["Z"] = sb("Z", [128, NT])
        F["MASKB"] = sb("MASKB", [128, NT])
        F["TIE"] = sb("TIE", [128, NT])
        F["MASK05"] = sb("MASK05", [128, NT])
        F["RED"] = sb("RED", [128, 1])
        F["L1S"] = sb("L1S", [1, 32])
        F["THB"] = sb("THB", [128, 16])
        F["NAB"] = sb("NAB", [128, 2])
        F["SRTB"] = sb("SRTB", [128, 1])
        F["ZTH"] = sb("ZTH", [128, 1])
        F["EJ"] = sb("EJ", [128, NT], BF16)
        F["IND"] = sb("IND", [128, 16, 16], BF16)
        F["CNT"] = sb("CNT", [128, 16])
        F["CROW"] = sb("CROW", [1, 16])
        F["XPWT"] = sb("XPWT", [128, 16, 32], BF16)
        F["XPBROW"] = sb("XPBROW", [1, 32], BF16)
        F["CONVW"] = sb("CONVW", [128, 16, 4])
        F["INBC"] = sb("INBC", [128, 16])
        F["CBC"] = sb("CBC", [128, 16])

        b_ = Bld(nc, ctx)
        b_.VA, b_.VB = VA, VB
        b_.F = F
        b_.RLN = F["RLN"]
        E = b_.e

        E["v"].memset(ONESR[:], 1.0)
        E["v"].memset(ONESC[:], 1.0)
        E["g"].memset(IDB[:], 0.0)
        tk0 = b_.sig("g", E["g"].memset(IDF[:], 0.0))
        b_.wait("g", tk0)
        E["g"].affine_select(out=IDB[:], in_=IDB[:], compare_op=ALU.not_equal,
                             fill=1.0, base=0, pattern=[[-1, 128]], channel_multiplier=1)
        E["g"].affine_select(out=IDF[:], in_=IDF[:], compare_op=ALU.not_equal,
                             fill=1.0, base=0, pattern=[[-1, 128]], channel_multiplier=1)
        E["g"].iota(IOTA[:], pattern=[[128, NT]], base=0, channel_multiplier=1,
                    allow_small_or_imprecise_dtypes=True)
        tok_const = b_.sig("g", E["g"].iota(IOTA16[:], pattern=[[1, 16]], base=0,
                                            channel_multiplier=0,
                                            allow_small_or_imprecise_dtypes=True))
        for _j, _v in enumerate([LN_EPS, 1e-30]):
            E["v"].memset(CONSTS[:, _j:_j + 1], _v)
            nc.const_aps.register(F32, _v, CONSTS[:, _j:_j + 1])
        E["v"].memset(A["ONES512B"][:], 1.0)
        E["v"].memset(A["ONESB"][:], 1.0)
        E["v"].memset(A["ONESRF"][:], 1.0)
        E["v"].memset(A["ONESR64"][:], 1.0)

        xsrc0 = H["L0_attn_x"][0].rearrange("(t p) d -> p t d", p=128)
        for j in range(4):
            tk_x0 = b_.dma("y", X[:, ts(j, 4), :], xsrc0[:, ts(j, 4), :], sem="xls")

        wsrc = SCRB.rearrange("p (h ic o) -> p h ic o", h=2, ic=IC, o=1024)
        tk_asm_store = None
        tk_wet_store = [None] * NE
        tk_tr_last = None
        for e in range(NE):
            src = wsrc[:, 0, :, :]
            asm = wsrc[:, 1, :, :]
            b_.wait("g", tk_tr_last)
            wesrc = H["L0_ffn_We"][e].rearrange("(oc p) i -> p oc i", p=128)
            tk_lds = []
            for q, sm in enumerate(("pp", "pp2", "pp3", "pp4")):
                tk_lds.append(b_.dma("g", src[:, 2 * q:2 * q + 2, :], wesrc[:, 2 * q:2 * q + 2, :], sem=sm))
            b_.waits("t", tk_lds + [tok_const] + (tk_asm_store or []))
            ev_toks = []
            for icb in range(IC):
                bank = b_.pb[icb % 2]
                b_.wait("t", b_.pb_free[icb % 2])
                mm = None
                for oc in range(IC):
                    mm = E["t"].transpose(bank[:, ts(oc)], src[:, oc, ts(icb)], IDB[:])
                tk_t = b_.sig("t", mm)
                tk_tr_last = tk_t
                b_.wait("v", tk_t)
                cp = E["v"].tensor_copy(out=asm[:, icb, :], in_=bank[:, 0:1024])
                tk_e = b_.sig("v", cp)
                b_.pb_free[icb % 2] = tk_e
                ev_toks.append(tk_e)
            b_.waits("y", ev_toks)
            asm_flat = asm.rearrange("p ic o -> p (ic o)")
            tk_asm_store = [b_.dma("y", wet_h[e][:, 0:4096], asm_flat[:, 0:4096], sem="ppst"),
                            b_.dma("y", wet_h[e][:, 4096:8192], asm_flat[:, 4096:8192], sem="pst2")]
            tk_wet_store[e] = tk_asm_store

        b_.dma("g", WGMOE[:], H["L0_ffn_Wg"].rearrange("(ic p) e -> p ic e", p=128), sem="mw")
        b_.dma("g", BGMOE[:], H["L0_ffn_bg"].rearrange("(u e) -> u e", u=1), sem="mw")

        SPP = LNXP[0:64, 0:2048].bitcast(F32)
        b_.dma("y", SPP, H["L0_attn_splat_positions"][:], sem="m2")
        b_.dma("y", A["NH"][:], H["L0_attn_splat_scales"].rearrange("(s u) -> s u", u=1), sem="m2")
        tk_im = b_.dma("y", A["AIMP"][:], H["L0_attn_splat_importance"].rearrange("(s u) -> s u", u=1), sem="m2")
        b_.wait("s", tk_im)
        b_.wait("s", tk_tr_last)
        tk_s2 = b_.sig("s", E["s"].activation(out=SCRB[0:64, 0:D], in_=SPP, func=AF.Square,
                                              accum_out=A["TOTR"][:]))
        b_.waits("t", [tk_s2, tk0])
        tk_t2 = b_.sig("t", E["t"].transpose(b_.ps[5][0:1, 0:64], A["TOTR"][:], IDF[0:64, 0:64]))
        b_.wait("s", tk_t2)
        tk_s2r = b_.sig("s", E["s"].activation(out=A["S2HB"][:], in_=b_.ps[5][0:1, 0:64],
                                               func=AF.Identity, scale=0.5, bias=0.0))
        for icx in range(IC):
            bank = b_.ps[icx % 2]
            b_.wait("t", b_.ps_free[icx % 2])
            tk_tt = b_.sig("t", E["t"].transpose(bank[:, 0:64], SPP[:, ts(icx)], IDF[0:64, 0:64]))
            b_.wait("s", tk_tt)
            tk_ev = b_.seq("s", E["s"].activation(out=A["SPTN"][:, icx, :], in_=bank[:, 0:64],
                                                  func=AF.Identity, scale=-1.0, bias=0.0))
            b_.ps_free[icx % 2] = tk_ev
        b_.wait("s", tk_im)
        tk_ab = b_.seq("s", E["s"].activation(out=A["NH"][:], in_=A["NH"][:], func=AF.Abs))
        tk_ab2 = b_.sig("s", E["s"].activation(out=A["AIMP"][:], in_=A["AIMP"][:], func=AF.Abs))
        b_.waits("v", [tk_ab, tk_ab2])
        b_.seq("v", E["v"].tensor_scalar_max(A["NH"][:], A["NH"][:], 1e-6))
        b_.seq("v", E["v"].tensor_tensor(out=A["NH"][:], in0=A["NH"][:], in1=A["NH"][:], op=ALU.mult))
        b_.seq("v", E["v"].reciprocal(out=A["NH"][:], in_=A["NH"][:]))
        b_.seq("v", E["v"].tensor_scalar(out=A["NH"][:], in0=A["NH"][:], scalar1=-0.5,
                                         scalar2=None, op0=ALU.mult))
        tk_moe_w = b_.dma("g", A["BT16"][:], H["L0_attn_bt"][:], sem="mw")

        b_.barrier()
        b_.dma("y", F["B1HC"][:], H["L1_ffn_b1_heavy"].rearrange("(hc p) -> p hc", p=128), sem="m2", allow_slow_non_contiguous=True)
        b_.dma("y", F["B1LC"][:], H["L1_ffn_b1_light"].rearrange("(hc p) -> p hc", p=128), sem="m2", allow_slow_non_contiguous=True)
        b_.dma("y", F["GAML"][:], H["L1_ffn_gamma_light"].rearrange("(ic p) -> p ic", p=128), sem="m2", allow_slow_non_contiguous=True)
        tk_gam = b_.dma("y", F["GAMH"][:], H["L1_ffn_gamma_heavy"].rearrange("(ic p) -> p ic", p=128), sem="m2", allow_slow_non_contiguous=True)
        b_.seq("v", E["v"].memset(F["L1S"][0:1, 0:1], 1.0))
        b_.seq("v", E["v"].tensor_scalar(out=F["L1S"][0:1, 1:2], in0=F["L1S"][0:1, 0:1],
                                         scalar1=2.0 ** -30, scalar2=-1.0, op0=ALU.add, op1=ALU.add))
        b_.seq("v", E["v"].tensor_scalar(out=F["L1S"][0:1, 1:2], in0=F["L1S"][0:1, 1:2],
                                         scalar1=0.0, scalar2=None, op0=ALU.is_gt))
        b_.seq("v", E["v"].tensor_scalar(out=F["L1S"][0:1, 1:2], in0=F["L1S"][0:1, 1:2],
                                         scalar1=-1.0, scalar2=1.0, op0=ALU.mult, op1=ALU.add))
        tk_zt = b_.seq("v", E["v"].tensor_scalar(out=F["L1S"][0:1, 1:2], in0=F["L1S"][0:1, 1:2],
                                                 scalar1=2.0 ** -126, scalar2=None, op0=ALU.mult))
        b_.wait("t", tk_zt)
        tk_ztb = b_.sig("t", E["t"].matmul(b_.ps[5][:, 0:1], A["ONESRF"][:], F["L1S"][0:1, 1:2],
                                           start=True, stop=True))
        b_.wait("v", tk_ztb)
        tk_zthc = b_.sig("v", E["v"].tensor_copy(out=F["ZTH"][:], in_=b_.ps[5][:, 0:1]))
        b_.ps_free[5] = tk_zthc
        w1src = H["L1_ffn_w1_heavy"].rearrange("(ic p) hh -> p ic hh", p=128)
        tk_pw1 = None
        tk_pw2 = None
        for hc in range(32):
            tk_pw1 = b_.dma("g", w1pk_h[hc].rearrange("p (ic f) -> p ic f", ic=IC),
                            w1src[:, :, hc * 128:(hc + 1) * 128], sem="pw1",
                            allow_slow_non_contiguous=True)
        for hc in range(32):
            tk_pw2 = b_.dma("g", w2hp_h[hc], H["L1_ffn_w2_heavy"][ts(hc), :], sem="pw2")

        b_.dma("y", F["INBC"][:], H["L1_attn_in_b"].rearrange("(c p) -> p c", p=128), sem="m2",
               allow_slow_non_contiguous=True)
        b_.dma("y", F["CBC"][:], H["L1_attn_conv_b"].rearrange("(c p) -> p c", p=128), sem="m2",
               allow_slow_non_contiguous=True)
        b_.dma("g", F["XPBROW"][:], H["L1_attn_xp_b"].rearrange("(u j) -> u j", u=1), sem="m3")
        tk_cw = b_.dma("y", F["CONVW"][:],
                       H["L1_attn_conv_w"].rearrange("(c p) one k -> p c (one k)", p=128),
                       sem="m2", allow_slow_non_contiguous=True)
        xw_stg = SCRB[0:32, 0:2048]
        b_.waits("g", tk_asm_store)
        tk_xw = b_.dma("g", xw_stg, H["L1_attn_xp_w"][:], sem="m3")
        b_.waits("t", [tk_xw, tk_cw])
        for c in range(16):
            bank = b_.pb[c % 2]
            b_.wait("t", b_.pb_free[c % 2])
            tk_t = b_.sig("t", E["t"].transpose(bank[:, 0:32], xw_stg[:, ts(c)], IDB[0:32, 0:32]))
            b_.wait("v", tk_t)
            tk_e = b_.sig("v", E["v"].tensor_copy(out=F["XPWT"][:, c, :], in_=bank[:, 0:32]))
            b_.pb_free[c % 2] = tk_e
        iw_stg = SCRB.rearrange("p (j f) -> p j f", j=8)
        tk_iw = {}
        tk_iwst = {}
        b_.wait("g", tk_e)
        for ocb in range(16):
            sl = ocb % 2
            b_.wait("g", tk_iwst.get(ocb - 2))
            tk_l = b_.dma("g", iw_stg[:, sl, 0:1024], H["L1_attn_in_w"][ts(ocb), :],
                          sem=("wa" if sl == 0 else "wb"))
            asm = iw_stg[:, 2 + sl, 0:1024].rearrange("p (dc f) -> p dc f", dc=IC)
            b_.wait("v", tk_iwst.get(ocb - 2))
            ev = None
            for dc in range(IC):
                bank = b_.pb[dc % 2]
                b_.wait("t", b_.pb_free[dc % 2])
                b_.wait("t", tk_l)
                tk_t = b_.sig("t", E["t"].transpose(bank[:, 0:128], iw_stg[:, sl, dc * 128:(dc + 1) * 128],
                                                    IDB[:]))
                b_.wait("v", tk_t)
                ev = b_.sig("v", E["v"].tensor_copy(out=asm[:, dc, 0:128], in_=bank[:, 0:128]))
                b_.pb_free[dc % 2] = ev
            b_.wait("y", ev)
            tk_iwst[ocb] = b_.dma("y", inwt_h[ocb].rearrange("p dc f -> p (dc f)"),
                                  iw_stg[:, 2 + sl, 0:1024], sem=("ta" if sl == 0 else "tb"))

        b_.barrier()

        for b in range(NB):
            tk_x = tk_x0
            b_.wait("v", tk_x)
            b_.wait("s", tk_x)

            n_done = 0
            if STAGES >= 1:
                emit_ln(b_, X, LNX, LNXT, SUM, SQ, VAR, RSTD, BIASC, TMP16, JUNK,
                        IDB, rms=False)
                if DBG and b == 0:
                    b_.dma("y", dbg_va[:, 0:NT], TMP16[:], sem="m2")
                    b_.dma("y", dbg_va[:, NT:2 * NT], VA[:], sem="m2")
                    b_.dma("y", dbg_va[:, 2 * NT:3 * NT], VB[:], sem="m2")
                    b_.dma("y", dbg_va[:, 3 * NT:4 * NT], VAR[:], sem="m2")
                    b_.dma("y", dbg_lnx[:], LNX[:], sem="m2")
                    b_.dma("y", dbg_lnxt[:], LNXT[:], sem="m2")
                    b_.dma("y", dbg_stat[:, 0:NT], SUM[:], sem="m2")
                    b_.dma("y", dbg_stat[:, NT:2 * NT], SQ[:], sem="m2")
                    b_.dma("y", dbg_stat[:, 2 * NT:3 * NT], RSTD[:], sem="m2")
                    b_.dma("y", dbg_stat[:, 3 * NT:4 * NT], BIASC[:], sem="m2")
                    b_.dma("y", dbg_wet[0], wet_h[0], sem="m2")
                    b_.dma("y", dbg_wet[1], wet_h[7], sem="m2")
                emit_moe(b_, nc, H, X, LNXT, SCRB, wet_h,
                         WGMOE, BEROW, BGMOE, ONESR,
                         GEXP, GSEL, G05, M8, GSUM, tk_moe_w, LNXP, IDF)
                if DBG and b == 0:
                    b_.dma("y", dbg_g05[:], G05[:], sem="m2")
                    b_.dma("y", dbg_gexp[:], GEXP[:], sem="m2")
                n_done = 1

            if STAGES >= 2:
                emit_ln(b_, X, LNX, LNXT, SUM, SQ, VAR, RSTD, BIASC, TMP16, JUNK,
                        IDB, rms=False)
                emit_l0attn(b_, nc, H, X, LNX, LNXP, LNXT, SCRB, A, IDB, ONESR, trd_h)
                n_done = 2

            if STAGES >= 3:
                emit_ln(b_, X, LNX, LNXT, SUM, SQ, VAR, RSTD, BIASC, TMP16, JUNK,
                        IDB, rms=False)
                emit_moe(b_, nc, H, X, LNXT, SCRB, wet_h,
                         WGMOE, BEROW, BGMOE, ONESR,
                         GEXP, GSEL, G05, M8, GSUM, tk_moe_w, LNXP, IDF)
                n_done = 3

            if STAGES >= 4:
                emit_ln(b_, X, LNX, LNXT, SUM, SQ, VAR, RSTD, BIASC, TMP16, JUNK,
                        IDB, rms=True)
                emit_l1ffn(b_, nc, H, X, LNXP, LNXT, SCRB, A, F, TMP16, IOTA, IOTA16,
                           ONESC, ONESR, IDF, w1pk_h, w2hp_h, tk_pw1, tk_pw2)
                n_done = 4

            if STAGES >= 5:
                emit_ln(b_, X, LNX, LNXT, SUM, SQ, VAR, RSTD, BIASC, TMP16, JUNK,
                        IDB, rms=False)
                emit_l1attn(b_, nc, H, X, LNXP, LNXT, SCRB, A, F, ONESR, ONESC, inwt_h, POOLC, IDB)
                n_done = 5

            if STAGES >= 6:
                emit_ln(b_, X, LNX, LNXT, SUM, SQ, VAR, RSTD, BIASC, TMP16, JUNK,
                        IDB, rms=True)
                emit_l1ffn(b_, nc, H, X, LNXP, LNXT, SCRB, A, F, TMP16, IOTA, IOTA16,
                           ONESC, ONESR, IDF, w1pk_h, w2hp_h, tk_pw1, tk_pw2)
                n_done = 6

            b_.barrier()
            xdst = out_h[b].rearrange("(t p) d -> p t d", p=128)
            tk_st_g = {}
            for j in range(4):
                tk_st_g[j] = b_.dma("y", xdst[:, ts(j, 4), :], X[:, ts(j, 4), :], sem=f"so{j}")
            if b + 1 < NB:
                xsrc = H["L0_attn_x"][b + 1].rearrange("(t p) d -> p t d", p=128)
                for j in range(4):
                    b_.wait("y", tk_st_g[j])
                    tk_x0 = b_.dma("y", X[:, ts(j, 4), :], xsrc[:, ts(j, 4), :], sem="xls")
            else:
                for j in range(4):
                    b_.wait("y", tk_st_g[j])
            b_.barrier()

    return nc


def emit_ln(b_, X, LNX, LNXT, SUM, SQ, VAR, RSTD, BIASC, TMP16, JUNK, IDB, rms=False):
    E = b_.e
    tk_sum = tk_sq = None
    for t in range(NT):
        tk_sum = b_.sig("v", E["v"].reduce_sum(out=SUM[:, t:t + 1], in_=X[:, t, :], axis=AX.X))
        tk_sq = b_.seq("s", E["s"].activation(out=JUNK[:, t % 2, :], in_=X[:, t, :], func=AF.Square,
                                              accum_out=SQ[:, t:t + 1]))
    b_.wait("v", tk_sq)
    b_.wait("v", tk_sum)
    VA, VB = b_.VA, b_.VB
    b_.seq("v", E["v"].tensor_scalar(out=TMP16[:], in0=SUM[:], scalar1=1.0 / D,
                                     scalar2=None, op0=ALU.mult))
    b_.seq("v", E["v"].tensor_tensor(out=VA[:], in0=TMP16[:], in1=TMP16[:], op=ALU.mult))
    b_.seq("v", E["v"].scalar_tensor_tensor(out=VB[:], in0=SQ[:], scalar=1.0 / D,
                                            in1=VA[:], op0=ALU.mult, op1=ALU.subtract))
    eps = 1e-30 if rms else LN_EPS
    tk_v = b_.seq("v", E["v"].tensor_scalar(out=VAR[:], in0=VB[:], scalar1=eps,
                                            scalar2=None, op0=ALU.add))
    b_.wait("s", tk_v)
    tk_sq2 = b_.sig("s", E["s"].activation(out=RSTD[:], in_=VAR[:], func=AF.Sqrt))
    if rms:
        RLN = b_.RLN
        tk_rl = b_.sig("s", E["s"].activation(out=RLN[:], in_=VAR[:], func=AF.Sqrt, bias=LN_EPS))
        b_.wait("v", tk_rl)
        b_.seq("v", E["v"].reciprocal(out=RLN[:], in_=RLN[:]))
    b_.wait("v", tk_sq2)
    tk_r = b_.seq("v", E["v"].reciprocal(out=RSTD[:], in_=RSTD[:]))
    b_.seq("v", E["v"].tensor_tensor(out=BIASC[:], in0=TMP16[:], in1=RSTD[:], op=ALU.mult))
    tk_b = b_.sig("v", E["v"].tensor_scalar(out=BIASC[:], in0=BIASC[:], scalar1=-1.0,
                                            scalar2=None, op0=ALU.mult))
    b_.waits("s", [tk_r, tk_b])
    for t in range(NT):
        tk_n = b_.sig("s", E["s"].activation(out=LNX[:, t, :], in_=X[:, t, :], func=AF.Identity,
                                             bias=BIASC[:, t:t + 1], scale=RSTD[:, t:t + 1]))
        b_.wait("t", tk_n)
        for half in range(2):
            bank = b_.pb[half]
            b_.wait("t", b_.pb_free[half])
            mm = None
            for j in range(4):
                icx = half * 4 + j
                mm = E["t"].transpose(bank[:, ts(j)], LNX[:, t, ts(icx)], IDB[:])
            tk_t = b_.sig("t", mm)
            b_.wait("v", tk_t)
            cp = E["v"].tensor_copy(out=LNXT[:, ts(half, 4), ts(t)], in_=bank[:, 0:512])
            tk_e = b_.sig("v", cp)
            b_.pb_free[half] = tk_e
    b_.barrier()


def emit_moe(b_, nc, H, X, LNXT, SCRB, wet_h, WGMOE, BEROW, BGMOE, ONESR,
             GEXP, GSEL, G05, M8, GSUM, tk_moe_w, LNXP, IDF):
    E = b_.e
    b_.wait("t", tk_moe_w)
    gb = 0
    ev = []
    for t in range(NT):
        bank = b_.ps[gb]
        b_.wait("t", b_.ps_free[gb])
        for icx in range(IC):
            E["t"].matmul(bank[:, 0:NE], LNXT[:, icx, ts(t)], WGMOE[:, icx, :],
                          start=(icx == 0), stop=False)
        mm = E["t"].matmul(bank[:, 0:NE], ONESR[:], BGMOE[:], start=False, stop=True)
        tk_mm = b_.sig("t", mm)
        b_.wait("v", tk_mm)
        tk_ev = b_.sig("v", E["v"].tensor_copy(out=GEXP[:, t, :], in_=bank[:, 0:NE]))
        b_.ps_free[gb] = tk_ev
        ev.append(tk_ev)
        gb = 1 - gb
    b_.waits("s", ev)
    tk_e = b_.sig("s", E["s"].activation(out=GEXP.rearrange("p t e -> p (t e)"),
                                         in_=GEXP.rearrange("p t e -> p (t e)"), func=AF.Exp))
    b_.wait("v", tk_e)
    for t in range(NT):
        mx = E["v"].max(out=M8[:, t, :], in_=GEXP[:, t, :])
    b_.wait("v", b_.sig("v", mx))
    m2 = M8[:, :, 1:2].to_broadcast([128, NT, NE])
    b_.seq("v", E["v"].tensor_tensor(out=GSEL[:], in0=GEXP[:], in1=m2, op=ALU.is_ge))
    b_.seq("v", E["v"].tensor_tensor(out=GSEL[:], in0=GEXP[:], in1=GSEL[:], op=ALU.mult))
    b_.seq("v", E["v"].reduce_sum(out=GSUM.rearrange("p (t u) -> p t u", u=1), in_=GSEL[:], axis=AX.X))
    b_.seq("v", E["v"].reciprocal(out=GSUM[:], in_=GSUM[:]))
    b_.seq("v", E["v"].tensor_scalar(out=GSUM[:], in0=GSUM[:], scalar1=0.5, scalar2=None, op0=ALU.mult))
    tk_g = b_.sig("v", E["v"].tensor_tensor(
        out=G05[:], in0=GSEL[:],
        in1=GSUM.rearrange("p (t u) -> p t u", u=1).to_broadcast([128, NT, NE]), op=ALU.mult))

    wbuf = SCRB.rearrange("p (h ic o) -> p h ic o", h=2, ic=IC, o=1024)
    BE8 = LNXP[0:8, 0:2048].bitcast(F32)
    GTE = LNXP[0:8, 2048:6144].bitcast(F32).rearrange("p (t f) -> p t f", t=NT)
    tk_be = b_.dma("g", BE8, H["L0_ffn_be"][:, :], sem="qa")
    tk_wload = [None, None]
    tk_lastmm = [None, None]
    tk_wload[0] = b_.dma("g", wbuf[:, 0, :, :],
                         wet_h[0].rearrange("p (ic o) -> p ic o", ic=IC), sem="wa")
    tk_gt = {}
    for t in range(NT):
        bank = b_.ps[4 + t % 2]
        b_.wait("t", b_.ps_free[4 + t % 2])
        b_.wait("t", tk_g)
        tk_tr = b_.sig("t", E["t"].transpose(bank[0:8, 0:128], G05[:, t, :], IDF[:]))
        b_.wait("v", tk_tr)
        tk_gt[t] = b_.sig("v", E["v"].tensor_copy(out=GTE[:, t, :], in_=bank[0:8, 0:128]))
        b_.ps_free[4 + t % 2] = tk_gt[t]
    for e in range(NE):
        h = e % 2
        if e + 1 < NE:
            b_.wait("g", tk_lastmm[1 - h])
            sm = "wa" if (1 - h) == 0 else "wb"
            tk_wload[1 - h] = b_.dma("g", wbuf[:, 1 - h, :, :],
                                     wet_h[e + 1].rearrange("p (ic o) -> p ic o", ic=IC), sem=sm)
        b_.wait("t", tk_wload[h])
        b_.wait("t", tk_g)
        we = wbuf[:, h, :, :]
        for t in range(NT):
            bk0, bk1 = (t % 2) * 2, (t % 2) * 2 + 1
            b_.wait("t", b_.ps_free[bk0])
            b_.wait("t", b_.ps_free[bk1])
            for icx in range(IC):
                E["t"].matmul(b_.ps[bk0][:], LNXT[:, icx, ts(t)], we[:, icx, 0:512],
                              start=(icx == 0), stop=(icx == IC - 1))
                mm = E["t"].matmul(b_.ps[bk1][:], LNXT[:, icx, ts(t)], we[:, icx, 512:1024],
                                   start=(icx == 0), stop=(icx == IC - 1))
            tk_mm = b_.sig("t", mm)
            tk_lastmm[h] = tk_mm
            b_.wait("v", tk_mm)
            g = G05[:, t, e:e + 1]
            E["v"].scalar_tensor_tensor(out=X[:, t, 0:512], in0=b_.ps[bk0][:], scalar=g,
                                        in1=X[:, t, 0:512], op0=ALU.mult, op1=ALU.add)
            tk_ev = b_.sig("v", E["v"].scalar_tensor_tensor(
                out=X[:, t, 512:1024], in0=b_.ps[bk1][:], scalar=g,
                in1=X[:, t, 512:1024], op0=ALU.mult, op1=ALU.add))
            b_.ps_free[bk0] = tk_ev
            b_.ps_free[bk1] = tk_ev
    for t in range(NT):
        for half in range(2):
            bank = b_.ps[(2 * t + half) % 4]
            b_.wait("t", b_.ps_free[(2 * t + half) % 4])
            b_.waits("t", [tk_be, tk_gt[t]])
            tk_mm = b_.sig("t", E["t"].matmul(bank[:], GTE[:, t, :], BE8[:, ts(half, 512)],
                                              start=True, stop=True))
            b_.wait("v", tk_mm)
            tk_ev = b_.sig("v", E["v"].tensor_tensor(out=X[:, t, ts(half, 512)], in0=bank[:],
                                                     in1=X[:, t, ts(half, 512)], op=ALU.add))
            b_.ps_free[(2 * t + half) % 4] = tk_ev
    b_.barrier()




def emit_l0attn(b_, nc, H, X, LNX, LNXP, LNXT, SCRB, A, IDB, ONESR, trd_h):
    E = b_.e
    MIXT = SCRB.rearrange("p (c n) -> p c n", c=IC)
    V = LNXP.rearrange("p (t c) -> p t c", c=1040)
    Vh = LNXP.rearrange("p (t h c) -> p t h c", t=NT, h=16)

    tk_ib = None
    tk_d2mm = {}
    for nb in range(4):
        nbs = slice(nb * 512, (nb + 1) * 512)
        bank4 = b_.ps[4]
        b_.wait("t", b_.ps_free[4])
        tk_mm = {}
        for icx in range(IC):
            sqv = A["SIGT"][:, icx % 2, :]
            if icx >= 2:
                b_.wait("s", tk_mm[icx - 2])
            tk_sq = b_.sig("s", E["s"].activation(out=sqv, in_=LNXT[:, icx, nbs], func=AF.Square))
            b_.wait("t", tk_sq)
            tk_mm[icx] = b_.sig("t", E["t"].matmul(bank4[0:1, :], A["ONESB"][:], sqv,
                                                   start=(icx == 0), stop=(icx == IC - 1)))
        b_.wait("s", tk_mm[IC - 1])
        b_.wait("s", tk_d2mm.get(nb - 2))
        tk_x2 = b_.sig("s", E["s"].activation(out=A["X2P"][0:1, nb % 2, :], in_=bank4[0:1, :],
                                              func=AF.Identity, scale=0.5, bias=0.0))
        b_.ps_free[4] = tk_x2
        bank = b_.ps[nb % 2]
        b_.wait("t", b_.ps_free[nb % 2])
        for icx in range(IC):
            E["t"].matmul(bank[0:64, :], A["SPTN"][:, icx, :], LNXT[:, icx, nbs],
                          start=(icx == 0), stop=False)
        E["t"].matmul(bank[0:64, :], A["S2HB"][:], A["ONES512B"][:], start=False, stop=False)
        b_.wait("t", tk_x2)
        tk_mm = b_.sig("t", E["t"].matmul(bank[0:64, :], ONESR[0:1, 0:64], A["X2P"][0:1, nb % 2, :],
                                          start=False, stop=True))
        tk_d2mm[nb] = tk_mm
        b_.wait("v", tk_mm)
        tk_r = b_.seq("v", E["v"].tensor_scalar(out=A["INFL32"][:, nbs], in0=bank[0:64, :],
                                                scalar1=2.0, scalar2=0.0,
                                                op0=ALU.mult, op1=ALU.max))
        b_.ps_free[nb % 2] = tk_r
        b_.wait("s", tk_r)
        tk_e = b_.sig("s", E["s"].activation(out=A["INFL32"][:, nbs], in_=A["INFL32"][:, nbs],
                                             func=AF.Exp, scale=A["NH"][:]))
        b_.wait("v", tk_e)
        b_.seq("v", E["v"].tensor_scalar(out=A["INFL32"][:, nbs], in0=A["INFL32"][:, nbs],
                                         scalar1=A["AIMP"][:], scalar2=None, op0=ALU.mult))
        tk_ib = b_.seq("v", E["v"].tensor_copy(out=A["INFLB"][:, nbs], in_=A["INFL32"][:, nbs]))
    b_.seq("v", E["v"].reduce_sum(out=A["TOTR"][:], in_=A["INFL32"][:], axis=AX.X))
    b_.seq("v", E["v"].tensor_scalar(out=A["TOTR"][:], in0=A["TOTR"][:], scalar1=1e-8,
                                     scalar2=None, op0=ALU.max))
    tk_tot = b_.seq("v", E["v"].reciprocal(out=A["TOTR"][:], in_=A["TOTR"][:]))
    b_.wait("g", tk_tot)
    b_.dma("g", A["BROW"][0:1, 0:D], H["L0_attn_bg"].rearrange("(u d) -> u d", u=1), sem="m3")
    b_.dma("g", A["BROW"][0:1, D:2 * D], H["L0_attn_bd"].rearrange("(u d) -> u d", u=1), sem="m3")
    tk_brow = b_.dma("g", A["BROW"][0:1, 2 * D:2560], H["L0_attn_be"].rearrange("(u d) -> u d", u=1), sem="m3")
    tk_itm = None
    for g in range(2):
        bank = b_.pb[g]
        b_.wait("t", b_.pb_free[g])
        b_.wait("t", tk_ib)
        mm = None
        for j in range(8):
            t = g * 8 + j
            mm = E["t"].transpose(bank[:, j * 64:(j + 1) * 64], A["INFLB"][:, ts(t)], IDB[0:64, 0:64])
        tk_t = b_.sig("t", mm)
        b_.wait("v", tk_t)
        tk_itm = b_.sig("v", E["v"].tensor_copy(out=A["INFLTM"][:, g * 8:(g + 1) * 8, :],
                                                in_=bank[:, 0:512]))
        b_.pb_free[g] = tk_itm

    tk_g = None
    for dh in range(2):
        bank = b_.ps[2 + dh]
        b_.wait("t", b_.ps_free[2 + dh])
        b_.wait("t", tk_itm)
        for t in range(NT):
            mm = E["t"].matmul(bank[0:64, :], A["INFLTM"][:, t, :], LNX[:, t, ts(dh, 512)],
                               start=(t == 0), stop=(t == NT - 1))
        tk_mm = b_.sig("t", mm)
        b_.waits("v", [tk_mm, tk_tot])
        tk_g = b_.sig("v", E["v"].tensor_scalar(out=A["GATH"][:, ts(dh, 512)], in0=bank[0:64, :],
                                                scalar1=A["TOTR"][:], scalar2=None, op0=ALU.mult))
        b_.ps_free[2 + dh] = tk_g
    bank = b_.pb[0]
    b_.wait("t", b_.pb_free[0])
    b_.wait("t", tk_g)
    for icx in range(IC):
        mm = E["t"].transpose(bank[:, icx * 64:(icx + 1) * 64], A["GATH"][:, ts(icx)], IDB[0:64, 0:64])
    tk_t = b_.sig("t", mm)
    b_.wait("v", tk_t)
    tk_gt = b_.sig("v", E["v"].tensor_copy(out=A["GATHT"].rearrange("p ic f -> p (ic f)"),
                                           in_=bank[:, 0:512]))
    b_.pb_free[0] = tk_gt
    b_.wait("g", tk_g)
    tk_we = b_.dma("g", A["WSPL"][:], H["L0_attn_We"].rearrange("(ic p) f -> p ic f", p=128), sem="m3")
    bank = b_.ps[0]
    b_.wait("t", b_.ps_free[0])
    b_.waits("t", [tk_we, tk_gt])
    for icx in range(IC):
        E["t"].matmul(bank[0:64, :], A["GATHT"][:, icx, :], A["WSPL"][:, icx, :],
                      start=(icx == 0), stop=False)
    tk_mm = b_.sig("t", E["t"].matmul(bank[0:64, :], ONESR[0:1, 0:64], A["BROW"][0:1, 2048:2560],
                                      start=False, stop=True))
    b_.wait("s", tk_mm)
    tk_enc = b_.sig("s", E["s"].activation(out=A["ENC"][:], in_=bank[0:64, :], func=AF.Relu))
    b_.ps_free[0] = tk_enc
    bank = b_.pb[1]
    b_.wait("t", b_.pb_free[1])
    b_.wait("t", tk_enc)
    for ec in range(4):
        mm = E["t"].transpose(bank[:, ec * 64:(ec + 1) * 64], A["ENC"][:, ts(ec)], IDB[0:64, 0:64])
    tk_t = b_.sig("t", mm)
    b_.wait("v", tk_t)
    tk_enct = b_.sig("v", E["v"].tensor_copy(out=A["ENCT"].rearrange("p ic f -> p (ic f)"),
                                             in_=bank[:, 0:256]))
    b_.pb_free[1] = tk_enct

    WT4 = LNXP[:, 0:8192].rearrange("p (h ec f) -> p h ec f", h=4, ec=4)
    wt_sems = ["wa", "wb", "ha", "hb"]
    tk_wt = [None] * 4
    tk_used = [None] * 4
    tk_rowst = {0: None, 1: None}
    b_.wait("g", tk_enc)
    b_.wait("g", tk_g)

    def wt_load(s):
        sl = s % 4
        b_.wait("g", tk_used[sl])
        tk_wt[sl] = b_.dma("g", WT4[:, sl, :, :],
                           H["L0_attn_Wt"][s].rearrange("(ec p) f -> p ec f", p=128),
                           sem=wt_sems[sl])

    for s in range(4):
        wt_load(s)
    tk_wgh = None
    gate_units = [(dc, nb) for dc in range(IC) for nb in range(4)]
    gi = 0
    tk_gate_mm = None
    tk_sig_last = None

    def gate_unit():
        nonlocal gi, tk_wgh, tk_gate_mm, tk_sig_last
        if gi >= len(gate_units):
            return
        dc, nb = gate_units[gi]
        if gi % 16 == 0:
            ob = gi // 16
            b_.wait("g", tk_gate_mm)
            tk_wgh = b_.dma("g", A["WBIGH"][:],
                            H["L0_attn_Wg"][:, ob * 512:(ob + 1) * 512]
                            .rearrange("(ic p) o -> p ic o", p=128), sem="qa")
        nbs = slice(nb * 512, (nb + 1) * 512)
        bk = b_.ps[2 + gi % 2]
        b_.wait("t", b_.ps_free[2 + gi % 2])
        b_.wait("t", tk_wgh)
        for icx in range(IC):
            E["t"].matmul(bk[:], A["WBIGH"][:, icx, ts(dc % 4)], LNXT[:, icx, nbs],
                          start=(icx == 0), stop=False)
        tk_gate_mm = b_.sig("t", E["t"].matmul(bk[:], A["BROW"][0:1, ts(dc)], A["ONES512B"][:],
                                               start=False, stop=True))
        b_.wait("s", tk_gate_mm)
        tk_sig_last = b_.sig("s", E["s"].activation(out=MIXT[:, dc, nbs], in_=bk[:],
                                                    func=AF.Sigmoid))
        b_.ps_free[2 + gi % 2] = tk_sig_last
        gi += 1

    for s in range(64):
        sl = s % 4
        hh = s % 2
        bank = b_.ps[4 + hh]
        b_.wait("t", b_.ps_free[4 + hh])
        b_.wait("t", tk_wt[sl])
        b_.wait("t", tk_enct)
        mm = None
        for ec in range(4):
            mm = E["t"].matmul(bank[0:1, :], A["ENCT"][:, ec, s:s + 1], WT4[:, sl, ec, :],
                               start=(ec == 0), stop=(ec == 3))
        tk_mm = b_.sig("t", mm)
        tk_used[sl] = tk_mm
        if s + 4 < 64:
            wt_load(s + 4)
        eng = "s" if hh == 0 else "v"
        b_.wait(eng, tk_mm)
        b_.wait(eng, tk_rowst[hh])
        if eng == "s":
            tk_ev = b_.sig("s", E["s"].activation(out=A["ROWBUF"][0:1, hh, :],
                                                  in_=bank[0:1, :], func=AF.Identity))
        else:
            tk_ev = b_.sig("v", E["v"].tensor_copy(out=A["ROWBUF"][0:1, hh, :],
                                                   in_=bank[0:1, :]))
        b_.ps_free[4 + hh] = tk_ev
        b_.wait("y", tk_ev)
        tk_rowst[hh] = b_.dma("y", trd_h[s].rearrange("(u f) -> u f", u=1),
                              A["ROWBUF"][0:1, hh, :], sem=("ta" if hh == 0 else "tb"))
        gate_unit()
    while gi < len(gate_units):
        gate_unit()

    b_.wait("g", tk_rowst[0])
    b_.wait("g", tk_rowst[1])
    tk_sc = b_.dma("g", A["TRANSB"][:], trd_h[:], sem="m4")
    b_.wait("v", tk_sc)
    tk_tr = b_.seq("v", E["v"].tensor_tensor(out=A["TRANSB"][:], in0=A["TRANSB"][:],
                                             in1=A["BT16"][:], op=ALU.add))
    bank = b_.pb[0]
    b_.wait("t", b_.pb_free[0])
    b_.wait("t", tk_tr)
    for ec in range(4):
        mm = E["t"].transpose(bank[:, ec * 64:(ec + 1) * 64], A["TRANSB"][:, ts(ec)], IDB[0:64, 0:64])
    tk_t = b_.sig("t", mm)
    b_.wait("v", tk_t)
    tk_trt = b_.sig("v", E["v"].tensor_copy(out=A["TRANST"].rearrange("p ic f -> p (ic f)"),
                                            in_=bank[:, 0:256]))
    b_.pb_free[0] = tk_trt
    b_.wait("g", tk_used[0])
    b_.wait("g", tk_used[1])
    WD4 = A["WSPL"].rearrange("p ic f -> p (ic f)").rearrange("p (ec f) -> p ec f", ec=4)
    tk_wd = b_.dma("g", WD4[:], H["L0_attn_Wd"].rearrange("(ec p) f -> p ec f", p=128), sem="m3")
    tk_dec = None
    for dh in range(2):
        bank = b_.ps[dh]
        b_.wait("t", b_.ps_free[dh])
        b_.waits("t", [tk_wd, tk_trt])
        for ec in range(4):
            E["t"].matmul(bank[0:64, :], A["TRANST"][:, ec, :], WD4[:, ec, ts(dh, 512)],
                          start=(ec == 0), stop=False)
        tk_mm = b_.sig("t", E["t"].matmul(bank[0:64, :], ONESR[0:1, 0:64],
                                          A["BROW"][0:1, D + dh * 512:D + (dh + 1) * 512],
                                          start=False, stop=True))
        b_.wait("v", tk_mm)
        tk_dec = b_.sig("v", E["v"].tensor_copy(out=A["DEC"][:, ts(dh, 512)], in_=bank[0:64, :]))
        b_.ps_free[dh] = tk_dec

    idx = 0
    tk_mix04 = None
    for dc in range(IC):
        for nb in range(4):
            nbs = slice(nb * 512, (nb + 1) * 512)
            bf_ = b_.ps[2 + idx % 2]
            b_.wait("t", b_.ps_free[2 + idx % 2])
            b_.wait("t", tk_dec)
            tk_fmm = b_.sig("t", E["t"].matmul(bf_[:], A["DEC"][:, ts(dc)], A["INFLB"][:, nbs],
                                               start=True, stop=True))
            b_.waits("v", [tk_fmm, tk_sig_last])
            tk_mix04 = b_.sig("v", E["v"].scalar_tensor_tensor(out=MIXT[:, dc, nbs], in0=bf_[:],
                                                               scalar=0.4, in1=MIXT[:, dc, nbs],
                                                               op0=ALU.mult, op1=ALU.mult))
            b_.ps_free[2 + idx % 2] = tk_mix04
            idx += 1

    tk_vone = b_.sig("v", E["v"].memset(Vh[:, :, :, 64:65], 1.0))
    tk_vev = None
    tk_wv = None
    for ob in range(2):
        b_.wait("g", tk_gate_mm if ob == 0 else tk_vused)
        tk_wv = b_.dma("g", A["WBIGH"][:],
                       H["L0_attn_Wv"][:, ob * 512:(ob + 1) * 512]
                       .rearrange("(ic p) o -> p ic o", p=128), sem="qa")
        for t in range(NT):
            bank = b_.ps[t % 2]
            b_.wait("t", b_.ps_free[t % 2])
            b_.wait("t", tk_wv)
            for icx in range(IC):
                mm = E["t"].matmul(bank[:], LNXT[:, icx, ts(t)], A["WBIGH"][:, icx, :],
                                   start=(icx == 0), stop=(icx == IC - 1))
            tk_mm = b_.sig("t", mm)
            tk_vused = tk_mm
            b_.waits("v", [tk_mm, tk_vone])
            dst = Vh[:, t, ob * 8:(ob + 1) * 8, 0:64]
            tk_vev = b_.sig("v", E["v"].tensor_copy(out=dst, in_=bank[:]))
            b_.ps_free[t % 2] = tk_vev

    Q2 = A["Q2"]
    K2 = A["K2"]
    ATTS = [A["SIGT"][:, 0, :], A["SIGT"][:, 1, :], A["ATT"][:, 0, :], A["ATT"][:, 1, :]]
    SQV = A["SQB2"]
    st = {"qk_used": tk_vused, "negm": None, "lastexp": None, "sqmm": None}

    def emit_qk_dma(h):
        b_.wait("g", st["qk_used"])
        b_.wait("g", tk_gt)
        hs = slice(64 * h, 64 * (h + 1))
        wq = H["L0_attn_Wq"][:, hs].rearrange("(ic p) o -> p ic o", p=128)
        wk = H["L0_attn_Wk"][:, hs].rearrange("(ic p) o -> p ic o", p=128)
        b_.dma("g", A["WQS"][:, :, 0:64], wq, sem="qa")
        st["dq", h] = b_.dma("g", A["WQS"][:, :, 64:128], wq, sem="qa")
        b_.dma("g", A["WKS"][:, :, 0:64], wk, sem="qb")
        st["dk", h] = b_.dma("g", A["WKS"][:, :, 64:128], wk, sem="qb")

    def emit_proj_stab(h, fin_hook=None):
        mxq = A["TMPS"][0:1, 0:1]
        mxk = A["TMPS"][0:1, 1:2]
        tmp = A["TMPS"][0:1, 2:3]
        pending = []

        def pop_stab():
            j, nb, tk_s2 = pending.pop(0)
            mx = mxq if j == 0 else mxk
            pbk = b_.pb[nb % 2][:].bitcast(F32)
            b_.wait("t", b_.pb_free[nb % 2])
            b_.wait("t", tk_s2)
            tk_nm = b_.sig("t", E["t"].matmul(pbk[0:1, 0:512], A["ONESB"][:], ATTS[nb],
                                              start=True, stop=True))
            st["slot_rd", nb] = tk_nm
            b_.wait("v", tk_nm)
            if j == 0 and nb == 0:
                tk_r = b_.seq("v", E["v"].reduce_max(out=mx, in_=pbk[0:1, 0:512], axis=AX.X))
            else:
                tk_r = b_.seq("v", E["v"].reduce_max(out=tmp, in_=pbk[0:1, 0:512], axis=AX.X))
                tk_r = b_.seq("v", E["v"].tensor_tensor(out=mx, in0=(mx if nb > 0 else tmp),
                                                        in1=tmp, op=ALU.max))
            b_.pb_free[nb % 2] = tk_r

        for j, (which, WS, OUT) in enumerate((("q", A["WQS"], Q2), ("k", A["WKS"], K2))):
            for nb in range(4):
                nbs = slice(nb * 512, (nb + 1) * 512)
                bank = b_.ps[5]
                b_.wait("t", b_.ps_free[5])
                b_.wait("t", st["dq", h] if which == "q" else st["dk", h])
                for icx in range(IC):
                    mm = E["t"].matmul(bank[:], WS[:, icx, :], LNXT[:, icx, nbs],
                                       start=(icx == 0), stop=(icx == IC - 1))
                tk_mm = b_.sig("t", mm)
                if which == "k" and nb == 3:
                    st["qk_used"] = tk_mm
                b_.wait("v", tk_mm)
                tk_ev = b_.sig("v", E["v"].tensor_copy(out=OUT[:, nbs], in_=bank[:]))
                b_.ps_free[5] = tk_ev
                b_.wait("v", st.get(("slot_rd", nb)))
                b_.wait("v", tk_ev)
                tk_s2 = b_.seq("v", E["v"].tensor_tensor(out=ATTS[nb], in0=OUT[:, nbs],
                                                         in1=OUT[:, nbs], op=ALU.mult))
                pending.append((j, nb, tk_s2))
                if len(pending) > 2:
                    pop_stab()
                if j == 0 and nb == 1 and fin_hook is not None:
                    fin_hook()
        while pending:
            pop_stab()
        ms = A["TMPS"][0:1, 3:4]
        tk_m = b_.seq("v", E["v"].tensor_tensor(out=ms, in0=mxq, in1=mxk, op=ALU.add))
        tk_m = b_.seq("v", E["v"].tensor_scalar(out=ms, in0=ms, scalar1=-0.03125,
                                                scalar2=None, op0=ALU.mult))
        bank = b_.pb[0][:].bitcast(F32)
        b_.wait("t", b_.pb_free[0])
        b_.wait("t", tk_m)
        tk_bc = b_.sig("t", E["t"].matmul(bank[:, 0:1], A["ONESRF"][:], ms,
                                          start=True, stop=True))
        b_.waits("v", [tk_bc, st["lastexp"]])
        st["negm"] = b_.sig("v", E["v"].tensor_copy(out=A["NEGM"][:, 0:1], in_=bank[:, 0:1]))
        b_.pb_free[0] = st["negm"]

    def emit_fin(h, qq):
        qqs = slice(qq * 512, (qq + 1) * 512)
        b_.seq("v", E["v"].reciprocal(out=A["RDEN"][:], in_=A["RDEN"][:]))
        tk_rd = b_.seq("v", E["v"].tensor_scalar(out=A["RDEN"][:], in0=A["RDEN"][:],
                                                 scalar1=0.6, scalar2=None, op0=ALU.mult))
        bankb = b_.pb[1][:].bitcast(F32)
        b_.wait("t", b_.pb_free[1])
        b_.wait("t", tk_rd)
        tk_bc = b_.sig("t", E["t"].matmul(bankb[0:64, 0:512], A["ONESR64"][64:65, :], A["RDEN"][:],
                                          start=True, stop=True))
        b_.wait("v", tk_bc)
        tk_rb = b_.seq("v", E["v"].tensor_copy(out=A["RDENB2"][:], in_=bankb[0:64, 0:512]))
        tk_avs = b_.seq("v", E["v"].tensor_tensor(out=A["AVS"][:], in0=A["AVN"][:],
                                                  in1=A["RDENB2"][:], op=ALU.mult))
        b_.wait("v", tk_mix04)
        if h % 2 == 0:
            tk_acc = b_.sig("v", E["v"].tensor_tensor(out=MIXT[0:64, h // 2, qqs],
                                                      in0=A["AVS"][:],
                                                      in1=MIXT[0:64, h // 2, qqs],
                                                      op=ALU.add))
            b_.pb_free[1] = tk_rb
        else:
            b_.wait("t", tk_avs)
            tk_cp = b_.sig("t", E["t"].matmul(bankb[64:128, 0:512], IDB[0:64, 0:64], A["AVS"][:],
                                              start=True, stop=True, tile_position=(0, 64)))
            b_.wait("v", tk_cp)
            tk_acc = b_.sig("v", E["v"].tensor_tensor(out=MIXT[64:128, h // 2, qqs],
                                                      in0=bankb[64:128, 0:512],
                                                      in1=MIXT[64:128, h // 2, qqs],
                                                      op=ALU.add))
            b_.pb_free[1] = tk_acc
        st["acc"] = tk_acc
        st["avs_done"] = tk_avs

    def pop_fin():
        if st.get("fin") is not None:
            fh, fqq = st["fin"]
            st["fin"] = None
            emit_fin(fh, fqq)

    emit_qk_dma(0)
    st["fin"] = None
    st["acc"] = None
    for h in range(16):
        emit_proj_stab(h, fin_hook=pop_fin)
        if h + 1 < 16:
            emit_qk_dma(h + 1)
        tk_negm = st["negm"]
        for qq in range(4):
            qqs = slice(qq * 512, (qq + 1) * 512)
            av = b_.ps[2 + qq % 2]
            b_.wait("t", b_.ps_free[2 + qq % 2])
            tk_av = {}
            tk_sc = {}
            SCB = (0, 1, 4)

            def emit_scores(kt):
                sb_ = b_.ps[SCB[kt % 3]]
                b_.wait("t", b_.ps_free[SCB[kt % 3]])
                tk_sc[kt] = b_.sig("t", E["t"].matmul(sb_[:], K2[:, ts(kt)], Q2[:, qqs],
                                                      start=True, stop=True))

            def emit_exp(j):
                b_.waits("s", [tk_sc[j], tk_negm])
                if j >= 4:
                    b_.wait("s", tk_av[j - 4])
                tk_e = b_.sig("s", E["s"].activation(out=ATTS[j % 4], in_=b_.ps[SCB[j % 3]][:],
                                                     func=AF.Exp, scale=0.0625,
                                                     bias=A["NEGM"][:, 0:1]))
                st["lastexp"] = tk_e
                tk_exp[j] = tk_e
                b_.ps_free[SCB[j % 3]] = tk_e

            tk_exp = {}
            emit_scores(0)
            emit_scores(1)
            emit_scores(2)
            emit_exp(0)
            for kt in range(NT):
                if kt + 3 < NT:
                    emit_scores(kt + 3)
                if kt + 1 < NT:
                    emit_exp(kt + 1)
                b_.wait("t", tk_exp[kt])
                tk_av[kt] = b_.sig("t", E["t"].matmul(av[0:65, :], V[:, kt, 65 * h:65 * h + 65],
                                                      ATTS[kt % 4],
                                                      start=(kt == 0), stop=(kt == NT - 1)))
                if kt == 5:
                    pop_fin()
            b_.wait("v", tk_av[NT - 1])
            b_.wait("v", st.get("avs_done"))
            b_.seq("v", E["v"].tensor_copy(out=A["RDEN"][:], in_=av[64:65, :]))
            tk_an = b_.seq("v", E["v"].tensor_copy(out=A["AVN"][:], in_=av[0:64, :]))
            b_.ps_free[2 + qq % 2] = tk_an
            st["fin"] = (h, qq)
    pop_fin()
    tk_heads = st["acc"]

    tk_wo_used = tk_vused
    for dhf in range(2):
        b_.wait("g", tk_wo_used)
        tk_wo = b_.dma("g", A["WBIGH"][:],
                       H["L0_attn_Wo"][:, dhf * 512:(dhf + 1) * 512]
                       .rearrange("(ic p) o -> p ic o", p=128), sem="m4")
        for t in range(NT):
            bank = b_.ps[t % 2]
            b_.wait("t", b_.ps_free[t % 2])
            b_.waits("t", [tk_wo, tk_heads])
            for dc in range(IC):
                mm = E["t"].matmul(bank[:], MIXT[:, dc, ts(t)], A["WBIGH"][:, dc, :],
                                   start=(dc == 0), stop=(dc == IC - 1))
            tk_mm = b_.sig("t", mm)
            tk_wo_used = tk_mm
            b_.wait("v", tk_mm)
            tk_x = b_.sig("v", E["v"].tensor_tensor(out=X[:, t, ts(dhf, 512)], in0=bank[:],
                                                    in1=X[:, t, ts(dhf, 512)], op=ALU.add))
            b_.ps_free[t % 2] = tk_x
    b_.barrier()




C_ROUTE = 0.1 * math.log(8.0)


def emit_l1ffn(b_, nc, H, X, LNXP, LNXT, SCRB, A, F, TMP16, IOTA, IOTA16, ONESC, ONESR, IDF, w1pk_h, w2hp_h, tk_pw1, tk_pw2):
    E = b_.e
    eps_i = 10.0
    W1PKs = [LNXP[:, 0:1024].rearrange("p (ic f) -> p ic f", ic=IC),
             LNXP[:, 1024:2048].rearrange("p (ic f) -> p ic f", ic=IC),
             LNXP[:, 12288:13312].rearrange("p (ic f) -> p ic f", ic=IC),
             LNXP[:, 13312:14336].rearrange("p (ic f) -> p ic f", ic=IC)]
    W2BF = LNXP[:, 2048:4096].rearrange("p (j f) -> p j f", j=4)
    W1L = LNXP[:, 4096:8192].rearrange("p (ic f) -> p ic f", ic=IC)
    W2L = LNXP[:, 8192:12288].rearrange("p (hc f) -> p hc f", hc=4)
    RTM = LNXP[:, 12288:14336].bitcast(F32)
    B2L = LNXP[0:1, 14336:15360]
    B2H = LNXP[0:1, 15360:16384]
    H1T = SCRB.rearrange("p (hc f) -> p hc f", hc=4)
    HHT = SCRB.rearrange("p (hc f) -> p hc f", hc=32)
    L1S = F["L1S"]

    def bcast128(src11, dst, tag):
        bank = b_.ps[5]
        b_.wait("t", b_.ps_free[5])
        b_.wait("t", tag)
        tk = b_.sig("t", E["t"].matmul(bank[:, 0:1], A["ONESRF"][:], src11, start=True, stop=True))
        b_.wait("v", tk)
        tk2 = b_.seq("v", E["v"].tensor_copy(out=dst, in_=bank[:, 0:1]))
        b_.ps_free[5] = tk2
        return tk2

    def psum_row(src, dst11, red, tag_v):
        tk = b_.seq("v", E["v"].reduce_sum(out=red, in_=src, axis=AX.X))
        bank = b_.ps[5]
        b_.wait("t", b_.ps_free[5])
        b_.wait("t", tk)
        tk2 = b_.sig("t", E["t"].matmul(bank[0:1, 0:1], ONESC[:], red, start=True, stop=True))
        b_.wait("v", tk2)
        tk3 = b_.seq("v", E["v"].tensor_copy(out=dst11, in_=bank[0:1, 0:1]))
        b_.ps_free[5] = tk3
        return tk3

    w1_sems = ["wa", "wb", "ha", "hb"]
    w2_sems = ["qa", "qb", "hg", "hh"]
    st1 = {"n": 0, "rd": [None] * 4, "ld": [None] * 4}
    st2 = {"n": 0, "rd": [None] * 4, "ld": [None] * 4}

    def issue1():
        if st1["n"] >= 128:
            return
        k = st1["n"]
        sl = k % 4
        b_.wait("g", st1["rd"][sl])
        st1["ld"][sl] = b_.dma("g", W1PKs[sl][:],
                               w1pk_h[k % 32].rearrange("p (ic f) -> p ic f", ic=IC),
                               sem=w1_sems[sl])
        st1["n"] += 1

    def issue2():
        if st2["n"] >= 256:
            return
        j = st2["n"]
        hc2 = j % 32
        half2 = (j // 32) % 2
        sl = j % 4
        b_.wait("g", st2["rd"][sl])
        st2["ld"][sl] = b_.dma("g", W2BF[:, sl, :], w2hp_h[hc2][:, ts(half2, 512)], sem=w2_sems[sl])
        st2["n"] += 1

    tk_rt = b_.dma("y", RTM[0:1, :], H["L1_ffn_routing_token"].rearrange("(u d) -> u d", u=1), sem="m2")
    b_.dma("g", B2L, H["L1_ffn_b2_light"].rearrange("(u d) -> u d", u=1), sem="m3")
    tk_b2 = b_.dma("g", B2H, H["L1_ffn_b2_heavy"].rearrange("(u d) -> u d", u=1), sem="m3")
    b_.wait("g", tk_pw1)
    b_.wait("g", tk_pw2)
    issue1()
    issue1()
    for _ in range(4):
        issue2()
    tk_w2l = b_.dma("g", W2L[:], H["L1_ffn_w2_light"].rearrange("(hc p) o -> p hc o", p=128), sem="m4")
    tk_wl = b_.dma("g", W1L[:], H["L1_ffn_w1_light"].rearrange("(ic p) o -> p ic o", p=128), sem="m4")
    b_.wait("v", tk_wl)
    tk_gl = None
    for icx in range(IC):
        tk_gl = b_.seq("v", E["v"].tensor_scalar(out=W1L[:, icx, :], in0=W1L[:, icx, :],
                                                 scalar1=F["GAML"][:, icx:icx + 1],
                                                 scalar2=None, op0=ALU.mult))
    b_.wait("t", tk_rt)
    for half in range(2):
        bank = b_.ps[4]
        b_.wait("t", b_.ps_free[4])
        tk = b_.sig("t", E["t"].matmul(bank[:], A["ONESRF"][:], RTM[0:1, ts(half, 512)],
                                       start=True, stop=True))
        b_.wait("v", tk)
        tk_rtm = b_.seq("v", E["v"].tensor_copy(out=RTM[:, ts(half, 512)], in_=bank[:]))
        b_.ps_free[4] = tk_rtm
    tk_s1 = b_.seq("v", E["v"].reduce_sum(out=L1S[0:1, 2:3], in_=RTM[0:1, :], axis=AX.X))
    tk_srt = bcast128(L1S[0:1, 2:3], F["SRTB"][:], tk_s1)
    tk_x = None
    for t in range(NT):
        tk_x = b_.seq("v", E["v"].scalar_tensor_tensor(out=SCRB[:, 0:2048].bitcast(F32),
                                                       in0=X[:, t, :], scalar=1.0,
                                                       in1=RTM[:], op0=ALU.bypass, op1=ALU.mult,
                                                       accum_out=F["XRT"][:, t:t + 1]))
    b_.seq("v", E["v"].tensor_tensor(out=F["MASKB"][:], in0=TMP16[:],
                                     in1=F["SRTB"][:].to_broadcast([128, NT]), op=ALU.mult))
    b_.seq("v", E["v"].tensor_tensor(out=F["SALL"][:], in0=F["XRT"][:], in1=F["MASKB"][:], op=ALU.subtract))
    tk_s = b_.seq("v", E["v"].tensor_tensor(out=F["SALL"][:], in0=F["SALL"][:], in1=F["RLN"][:], op=ALU.mult))
    tk1 = b_.seq("v", E["v"].reduce_max(out=F["RED"][:], in_=F["SALL"][:], axis=AX.X))
    bank = b_.ps[4]
    b_.wait("t", b_.ps_free[4])
    b_.wait("t", tk1)
    tkt = b_.sig("t", E["t"].transpose(bank[0:1, 0:128], F["RED"][:], IDF[:]))
    b_.wait("v", tkt)
    tk2 = b_.seq("v", E["v"].reduce_max(out=L1S[0:1, 3:4], in_=bank[0:1, 0:128], axis=AX.X))
    b_.ps_free[4] = tk2
    b_.wait("g", tk_x)
    issue1()
    issue1()

    cell = {"h1": None, "mm1last": None}

    def u_mm1(oc, nb):
        bank_ = b_.ps[nb]
        b_.wait("t", b_.ps_free[nb])
        b_.wait("t", tk_gl)
        for icx in range(IC):
            mm = E["t"].matmul(bank_[:], W1L[:, icx, ts(oc)], LNXT[:, icx, slice(nb * 512, nb * 512 + 512)],
                               start=(icx == 0), stop=(icx == IC - 1))
        tk_mm = b_.sig("t", mm)
        cell["mm1last"] = tk_mm
        b_.wait("v", tk_mm)
        tk_h1 = b_.sig("v", E["v"].tensor_copy(out=H1T[:, oc, slice(2048 + nb * 512, 2048 + nb * 512 + 512)],
                                               in_=bank_[:]))
        cell["pre", oc, nb] = tk_h1
        b_.ps_free[nb] = tk_h1

    def u_mm2(t, half):
        bank_ = b_.ps[(2 * t + half) % 4]
        b_.wait("t", b_.ps_free[(2 * t + half) % 4])
        b_.waits("t", [tk_wl, cell["h1"]])
        for hc in range(4):
            E["t"].matmul(bank_[:], H1T[:, hc, ts(t)], W2L[:, hc, ts(half, 512)],
                          start=(hc == 0), stop=False)
        b_.wait("t", tk_b2)
        tk_mm = b_.sig("t", E["t"].matmul(bank_[:], ONESR[0:1, :], B2L[0:1, ts(half, 512)],
                                          start=False, stop=True))
        b_.wait("v", tk_mm)
        tk_xl = b_.sig("v", E["v"].scalar_tensor_tensor(out=X[:, t, ts(half, 512)], in0=bank_[:],
                                                        scalar=0.5, in1=X[:, t, ts(half, 512)],
                                                        op0=ALU.mult, op1=ALU.add))
        b_.ps_free[(2 * t + half) % 4] = tk_xl

    fillers = [(lambda oc=oc, nb=nb: u_mm1(oc, nb)) for oc in range(4) for nb in range(4)]
    fillers_mm2 = [(lambda t=t, half=half: u_mm2(t, half)) for t in range(NT) for half in range(2)]

    def fill(n=1):
        for _ in range(n):
            if fillers:
                fillers.pop(0)()
    b_.seq("v", E["v"].memset(L1S[0:1, 0:1], 0.0))
    tk_a = None
    for it in range(50):
        b_.seq("v", E["v"].tensor_scalar(out=L1S[0:1, 5:6], in0=L1S[0:1, 0:1], scalar1=-1.0,
                                         scalar2=None, op0=ALU.mult))
        b_.seq("v", E["v"].tensor_tensor(out=L1S[0:1, 6:7], in0=L1S[0:1, 3:4],
                                         in1=L1S[0:1, 5:6], op=ALU.min))
        tk_nab = b_.seq("v", E["v"].tensor_scalar(out=L1S[0:1, 7:8], in0=L1S[0:1, 6:7],
                                                  scalar1=-eps_i, scalar2=None, op0=ALU.mult))
        bank = b_.ps[5]
        b_.wait("t", b_.ps_free[5])
        b_.wait("t", tk_nab)
        tkb = b_.sig("t", E["t"].matmul(bank[:, 0:2], A["ONESRF"][:], L1S[0:1, 5:8:2],
                                        start=True, stop=True))
        b_.wait("v", tkb)
        tk_nb = b_.seq("v", E["v"].tensor_copy(out=F["NAB"][:], in_=bank[:, 0:2]))
        b_.ps_free[5] = tk_nb
        tk_u = b_.seq("v", E["v"].tensor_scalar(out=F["W16"][:], in0=F["SALL"][:],
                                                scalar1=F["NAB"][:, 0:1], scalar2=None, op0=ALU.min))
        b_.wait("s", tk_u)
        tk_e = b_.sig("s", E["s"].activation(out=F["EJ"][:], in_=F["W16"][:], func=AF.Exp,
                                             scale=eps_i, bias=F["NAB"][:, 1:2],
                                             accum_out=F["RED"][:]))
        bank = b_.ps[5]
        b_.wait("t", b_.ps_free[5])
        b_.wait("t", tk_e)
        tks = b_.sig("t", E["t"].matmul(bank[0:1, 0:1], ONESC[:], F["RED"][:], start=True, stop=True))
        b_.wait("s", tks)
        tk_ln = b_.sig("s", E["s"].activation(out=L1S[0:1, 8:9], in_=bank[0:1, 0:1], func=AF.Ln))
        b_.ps_free[5] = tk_ln
        b_.wait("v", tk_ln)
        b_.seq("v", E["v"].tensor_scalar(out=L1S[0:1, 8:9], in0=L1S[0:1, 8:9], scalar1=-0.1,
                                         scalar2=C_ROUTE, op0=ALU.mult, op1=ALU.add))
        tk_a = b_.seq("v", E["v"].tensor_tensor(out=L1S[0:1, 0:1], in0=L1S[0:1, 8:9],
                                                in1=L1S[0:1, 6:7], op=ALU.subtract))
        fill(1)
    tk_ab = bcast128(L1S[0:1, 0:1], F["NAB"][:, 0:1], tk_a)
    b_.seq("v", E["v"].tensor_scalar(out=F["W16"][:], in0=F["SALL"][:], scalar1=F["NAB"][:, 0:1],
                                     scalar2=0.0, op0=ALU.add, op1=ALU.min))
    tk_w = b_.seq("v", E["v"].tensor_scalar(out=F["W16"][:], in0=F["W16"][:], scalar1=10.0,
                                            scalar2=None, op0=ALU.mult))
    b_.wait("s", tk_w)
    tk_sc = b_.sig("s", E["s"].activation(out=F["SCO"][:], in_=F["W16"][:], func=AF.Exp))
    b_.wait("v", tk_sc)
    for oc in range(4):
        for nb in range(4):
            b_.waits("s", [cell["pre", oc, nb], tk_x])
            cell["h1"] = b_.sig("s", E["s"].activation(
                out=H1T[:, oc, slice(nb * 512, nb * 512 + 512)],
                in_=H1T[:, oc, slice(2048 + nb * 512, 2048 + nb * 512 + 512)],
                func=AF.Gelu, bias=F["B1LC"][:, oc:oc + 1]))
    fillers.extend(fillers_mm2)
    b_.wait("v", cell["mm1last"])
    tk_gsc = None
    for icx in range(IC):
        tk_gsc = b_.seq("v", E["v"].tensor_scalar(out=LNXT[:, icx, :], in0=LNXT[:, icx, :],
                                                  scalar1=F["GAMH"][:, icx:icx + 1],
                                                  scalar2=None, op0=ALU.mult))
    b_.seq("v", E["v"].tensor_scalar(out=F["Z"][:], in0=F["SCO"][:], scalar1=F["ZTH"][:, 0:1],
                                     scalar2=None, op0=ALU.is_lt))
    b_.seq("v", E["v"].tensor_scalar(out=F["MASKB"][:], in0=F["SCO"][:], scalar1=0.0,
                                     scalar2=None, op0=ALU.is_equal))
    tk_z = b_.seq("v", E["v"].tensor_tensor(out=F["Z"][:], in0=F["Z"][:], in1=F["MASKB"][:],
                                            op=ALU.logical_or))
    tk_nz = psum_row(F["Z"][:], L1S[0:1, 9:10], F["RED"][:], tk_z)
    b_.seq("v", E["v"].tensor_scalar(out=L1S[0:1, 9:10], in0=L1S[0:1, 9:10], scalar1=-1.0,
                                     scalar2=2048.0, op0=ALU.mult, op1=ALU.add))
    tk_gg = b_.seq("v", E["v"].tensor_scalar(out=L1S[0:1, 10:11], in0=L1S[0:1, 9:10],
                                             scalar1=1024.0, scalar2=None, op0=ALU.is_ge))
    b_.seq("v", E["v"].memset(L1S[0:1, 11:12], -21000.0))
    b_.seq("v", E["v"].memset(L1S[0:1, 12:13], 1.0))
    for lev in range(8):
        b_.seq("v", E["v"].tensor_tensor(out=L1S[0:1, 13:14], in0=L1S[0:1, 12:13],
                                         in1=L1S[0:1, 11:12], op=ALU.subtract))
        b_.seq("v", E["v"].tensor_scalar(out=L1S[0:1, 13:14], in0=L1S[0:1, 13:14],
                                         scalar1=1.0 / 16, scalar2=None, op0=ALU.mult))
        tk_tr = b_.seq("v", E["v"].tensor_scalar(out=F["CROW"][:], in0=IOTA16[:],
                                                 scalar1=L1S[0:1, 13:14], scalar2=L1S[0:1, 11:12],
                                                 op0=ALU.mult, op1=ALU.add))
        bank = b_.ps[5]
        b_.wait("t", b_.ps_free[5])
        b_.wait("t", tk_tr)
        tkb = b_.sig("t", E["t"].matmul(bank[:, 0:16], A["ONESRF"][:], F["CROW"][:],
                                        start=True, stop=True))
        b_.wait("v", tkb)
        tk_th = b_.seq("v", E["v"].tensor_copy(out=F["THB"][:], in_=bank[:, 0:16]))
        b_.ps_free[5] = tk_th
        b_.seq("v", E["v"].tensor_tensor(
            out=F["IND"][:],
            in0=F["W16"].rearrange("p (u t) -> p u t", u=1).to_broadcast([128, 16, NT]),
            in1=F["THB"].rearrange("p (j u) -> p j u", u=1).to_broadcast([128, 16, NT]),
            op=ALU.is_ge))
        tk_c = b_.seq("v", E["v"].reduce_sum(out=F["CNT"].rearrange("p (j u) -> p j u", u=1),
                                             in_=F["IND"][:], axis=AX.X))
        bank = b_.ps[5]
        b_.wait("t", b_.ps_free[5])
        b_.wait("t", tk_c)
        tks = b_.sig("t", E["t"].matmul(bank[0:1, 0:16], ONESC[:], F["CNT"][:], start=True, stop=True))
        b_.wait("v", tks)
        tk_cr = b_.seq("v", E["v"].tensor_copy(out=F["CROW"][:], in_=bank[0:1, 0:16]))
        b_.ps_free[5] = tk_cr
        b_.seq("v", E["v"].tensor_scalar(out=F["CROW"][:], in0=F["CROW"][:], scalar1=1024.0,
                                         scalar2=None, op0=ALU.is_ge))
        b_.seq("v", E["v"].reduce_sum(out=L1S[0:1, 14:15], in_=F["CROW"][:], axis=AX.X))
        b_.seq("v", E["v"].tensor_scalar(out=L1S[0:1, 14:15], in0=L1S[0:1, 14:15], scalar1=-1.0,
                                         scalar2=None, op0=ALU.add))
        b_.seq("v", E["v"].tensor_tensor(out=L1S[0:1, 15:16], in0=L1S[0:1, 14:15],
                                         in1=L1S[0:1, 13:14], op=ALU.mult))
        b_.seq("v", E["v"].tensor_tensor(out=L1S[0:1, 11:12], in0=L1S[0:1, 11:12],
                                         in1=L1S[0:1, 15:16], op=ALU.add))
        tk_lo = b_.seq("v", E["v"].tensor_tensor(out=L1S[0:1, 12:13], in0=L1S[0:1, 11:12],
                                                 in1=L1S[0:1, 13:14], op=ALU.add))
        fill(2)
    tk_tb = bcast128(L1S[0:1, 11:12], F["NAB"][:, 1:2], tk_lo)
    b_.seq("v", E["v"].tensor_scalar(out=F["MASKB"][:], in0=F["W16"][:], scalar1=F["NAB"][:, 1:2],
                                     scalar2=None, op0=ALU.is_gt))
    b_.seq("v", E["v"].tensor_scalar(out=F["TIE"][:], in0=F["W16"][:], scalar1=F["NAB"][:, 1:2],
                                     scalar2=None, op0=ALU.is_ge))
    b_.seq("v", E["v"].tensor_tensor(out=F["TIE"][:], in0=F["TIE"][:], in1=F["MASKB"][:],
                                     op=ALU.subtract))
    tk_gb = bcast128(L1S[0:1, 10:11], F["NAB"][:, 0:1], tk_gg)
    b_.seq("v", E["v"].tensor_scalar(out=F["SCO"][:], in0=F["Z"][:], scalar1=-1.0, scalar2=1.0,
                                     op0=ALU.mult, op1=ALU.add))
    b_.seq("v", E["v"].tensor_tensor(out=F["MASKB"][:], in0=F["MASKB"][:], in1=F["SCO"][:],
                                     op=ALU.subtract))
    b_.seq("v", E["v"].scalar_tensor_tensor(out=F["MASKB"][:], in0=F["MASKB"][:],
                                            scalar=F["NAB"][:, 0:1], in1=F["SCO"][:],
                                            op0=ALU.mult, op1=ALU.add))
    b_.seq("v", E["v"].tensor_tensor(out=F["TIE"][:], in0=F["TIE"][:], in1=F["Z"][:],
                                     op=ALU.subtract))
    tk_tie = b_.seq("v", E["v"].scalar_tensor_tensor(out=F["TIE"][:], in0=F["TIE"][:],
                                                     scalar=F["NAB"][:, 0:1], in1=F["Z"][:],
                                                     op0=ALU.mult, op1=ALU.add))
    tk_cb = psum_row(F["MASKB"][:], L1S[0:1, 16:17], F["RED"][:], tk_tie)
    tk_R = b_.seq("v", E["v"].tensor_scalar(out=L1S[0:1, 16:17], in0=L1S[0:1, 16:17], scalar1=-1.0,
                                            scalar2=1024.0, op0=ALU.mult, op1=ALU.add))
    b_.seq("v", E["v"].memset(L1S[0:1, 17:18], -1.0))
    b_.seq("v", E["v"].memset(L1S[0:1, 18:19], 256.0))
    for lev in range(3):
        tk_tr = b_.seq("v", E["v"].tensor_scalar(out=F["CROW"][:], in0=IOTA16[:],
                                                 scalar1=L1S[0:1, 18:19], scalar2=L1S[0:1, 17:18],
                                                 op0=ALU.mult, op1=ALU.add))
        bank = b_.ps[5]
        b_.wait("t", b_.ps_free[5])
        b_.wait("t", tk_tr)
        tkb = b_.sig("t", E["t"].matmul(bank[:, 0:16], A["ONESRF"][:], F["CROW"][:],
                                        start=True, stop=True))
        b_.wait("v", tkb)
        tk_th = b_.seq("v", E["v"].tensor_copy(out=F["THB"][:], in_=bank[:, 0:16]))
        b_.ps_free[5] = tk_th
        b_.seq("v", E["v"].tensor_tensor(
            out=F["IND"][:],
            in0=IOTA.rearrange("p (u t) -> p u t", u=1).to_broadcast([128, 16, NT]),
            in1=F["THB"].rearrange("p (j u) -> p j u", u=1).to_broadcast([128, 16, NT]),
            op=ALU.is_le))
        b_.seq("v", E["v"].tensor_tensor(
            out=F["IND"][:], in0=F["IND"][:],
            in1=F["TIE"].rearrange("p (u t) -> p u t", u=1).to_broadcast([128, 16, NT]),
            op=ALU.mult))
        tk_c = b_.seq("v", E["v"].reduce_sum(out=F["CNT"].rearrange("p (j u) -> p j u", u=1),
                                             in_=F["IND"][:], axis=AX.X))
        bank = b_.ps[5]
        b_.wait("t", b_.ps_free[5])
        b_.wait("t", tk_c)
        tks = b_.sig("t", E["t"].matmul(bank[0:1, 0:16], ONESC[:], F["CNT"][:], start=True, stop=True))
        b_.wait("v", tks)
        tk_cr = b_.seq("v", E["v"].tensor_copy(out=F["CROW"][:], in_=bank[0:1, 0:16]))
        b_.ps_free[5] = tk_cr
        b_.seq("v", E["v"].tensor_scalar(out=F["CROW"][:], in0=F["CROW"][:],
                                         scalar1=L1S[0:1, 16:17], scalar2=None, op0=ALU.is_le))
        b_.seq("v", E["v"].reduce_sum(out=L1S[0:1, 14:15], in_=F["CROW"][:], axis=AX.X))
        b_.seq("v", E["v"].tensor_scalar(out=L1S[0:1, 14:15], in0=L1S[0:1, 14:15], scalar1=-1.0,
                                         scalar2=None, op0=ALU.add))
        b_.seq("v", E["v"].tensor_tensor(out=L1S[0:1, 15:16], in0=L1S[0:1, 14:15],
                                         in1=L1S[0:1, 18:19], op=ALU.mult))
        b_.seq("v", E["v"].tensor_tensor(out=L1S[0:1, 17:18], in0=L1S[0:1, 17:18],
                                         in1=L1S[0:1, 15:16], op=ALU.add))
        tk_st = b_.seq("v", E["v"].tensor_scalar(out=L1S[0:1, 18:19], in0=L1S[0:1, 18:19],
                                                 scalar1=1.0 / 16, scalar2=None, op0=ALU.mult))
        fill(2)
    tk_ib = bcast128(L1S[0:1, 17:18], F["NAB"][:, 1:2], tk_st)
    b_.seq("v", E["v"].tensor_scalar(out=F["SCO"][:], in0=IOTA[:], scalar1=F["NAB"][:, 1:2],
                                     scalar2=None, op0=ALU.is_le))
    b_.seq("v", E["v"].tensor_tensor(out=F["SCO"][:], in0=F["SCO"][:], in1=F["TIE"][:], op=ALU.mult))
    b_.seq("v", E["v"].tensor_tensor(out=F["MASK05"][:], in0=F["MASKB"][:], in1=F["SCO"][:], op=ALU.add))
    tk_mask = b_.seq("v", E["v"].tensor_scalar(out=F["MASK05"][:], in0=F["MASK05"][:], scalar1=0.5,
                                               scalar2=None, op0=ALU.mult))

    fill(len(fillers))

    for grp in range(4):
        gs = slice(grp * 512, (grp + 1) * 512)
        tk_hht = {}
        for hc in range(32):
            k = grp * 32 + hc
            sl = k % 4
            bank = b_.ps[4 + hc % 2]
            b_.wait("t", b_.ps_free[4 + hc % 2])
            b_.wait("t", st1["ld"][sl])
            b_.wait("t", tk_gsc)
            for icx in range(IC):
                mm = E["t"].matmul(bank[:], W1PKs[sl][:, icx, :], LNXT[:, icx, gs],
                                   start=(icx == 0), stop=(icx == IC - 1))
            tk_mm = b_.sig("t", mm)
            st1["rd"][sl] = tk_mm
            issue1()
            b_.wait("s", tk_mm)
            tk_h = b_.sig("s", E["s"].activation(out=HHT[:, hc, :], in_=bank[:], func=AF.Gelu,
                                                 bias=F["B1HC"][:, hc:hc + 1]))
            tk_hht[hc] = tk_h
            b_.ps_free[4 + hc % 2] = tk_h
        for half in range(2):
            for hc in range(32):
                j = grp * 64 + half * 32 + hc
                sl = j % 4
                b_.wait("t", st2["ld"][sl])
                b_.wait("t", tk_hht.get(31))
                for tt in range(4):
                    bank = b_.ps[tt]
                    if hc == 0:
                        b_.wait("t", b_.ps_free[tt])
                    mm = E["t"].matmul(bank[:], HHT[:, hc, ts(tt)], W2BF[:, sl, 0:512],
                                       start=(hc == 0), stop=False)
                st2["rd"][sl] = b_.sig("t", mm)
                issue2()
            for tt in range(4):
                t = grp * 4 + tt
                bank = b_.ps[tt]
                b_.wait("t", tk_b2)
                tk_mm = b_.sig("t", E["t"].matmul(bank[:], ONESR[0:1, :], B2H[0:1, ts(half, 512)],
                                                  start=False, stop=True))
                b_.waits("v", [tk_mm, tk_mask])
                tk_xh = b_.sig("v", E["v"].scalar_tensor_tensor(out=X[:, t, ts(half, 512)], in0=bank[:],
                                                                scalar=F["MASK05"][:, t:t + 1],
                                                                in1=X[:, t, ts(half, 512)],
                                                                op0=ALU.mult, op1=ALU.add))
                b_.ps_free[tt] = tk_xh
    b_.barrier()




def emit_l1attn(b_, nc, H, X, LNXP, LNXT, SCRB, A, F, ONESR, ONESC, inwt_h, POOLC, IDB):
    E = b_.e
    SH = SCRB.rearrange("p (c f) -> p c f", c=16)
    IWB = LNXP[:, 0:2048].rearrange("p (j dc f) -> p j dc f", j=2, dc=IC)
    CV = LNXP[:, 2048:4096].bitcast(F32)
    HALO = LNXP[:, 4096:4224].rearrange("p (c j f) -> p c j f", c=16, j=2)
    XPT1 = LNXP[0:16, 4224:5248].bitcast(F32)
    XPT2 = LNXP[0:16, 5248:6272].bitcast(F32)
    AT = LNXP[0:16, 6272:10368].bitcast(F32)
    BT = LNXP[0:16, 10368:14464].bitcast(F32)
    ROWS = LNXP[0:1, 14464:15488].bitcast(F32)
    YSB = LNXP[0:16, 10368:12416]
    OWTP = POOLC[0:16, 0:1024]
    OBP = POOLC[0:1, 1024:2048]

    ow_stg = CV.bitcast(BF16)[:, 0:128].rearrange("p (ic j) -> p ic j", ic=IC)
    b_.dma("g", ow_stg, H["L1_attn_out_w"].rearrange("(ic p) j -> p ic j", p=128),
                   sem="m3", allow_slow_non_contiguous=True)
    tk_ow = b_.dma("g", OBP, H["L1_attn_out_b"].rearrange("(u d) -> u d", u=1), sem="m3")
    b_.wait("t", tk_ow)
    tk_owt = None
    for icx in range(IC):
        bank = b_.pb[icx % 2]
        b_.wait("t", b_.pb_free[icx % 2])
        tk_t = b_.sig("t", E["t"].transpose(bank[0:16, 0:128], ow_stg[:, icx, :], IDB[:]))
        b_.wait("v", tk_t)
        tk_owt = b_.sig("v", E["v"].tensor_copy(out=OWTP[0:16, ts(icx)], in_=bank[0:16, 0:128]))
        b_.pb_free[icx % 2] = tk_owt

    tk_halo0 = b_.sig("v", E["v"].memset(HALO[:, :, 0, 0:3], 0.0))
    DGB = A["SIGT"].rearrange("p j f -> p (j f)").rearrange("p (j k f) -> p j k f", j=2, k=4)
    CPAD = A["ATT"].rearrange("p j f -> p (j f)")[:, 0:515]
    st_dg = {}
    tk_scan_dep = {}
    tk_used = [None, None]
    tk_shread = None
    tk_ab_guard = None
    tk_bc_prev = None
    for blk in range(2):
        bs = slice(blk * 1024, (blk + 1) * 1024)
        tk_iwl = [None, None]
        b_.wait("g", tk_used[0])
        tk_iwl[0] = b_.dma("g", IWB[:, 0, :, :], inwt_h[0], sem="wa")
        b_.wait("s", tk_shread)
        tk_sh = {}
        for c in range(16):
            sl = c % 2
            if c + 1 < 16:
                b_.wait("g", tk_used[1 - sl])
                tk_iwl[1 - sl] = b_.dma("g", IWB[:, 1 - sl, :, :], inwt_h[c + 1],
                                        sem=("wa" if (1 - sl) == 0 else "wb"))
            for nb2 in range(2):
                bank = b_.ps[nb2]
                b_.wait("t", b_.ps_free[nb2])
                b_.wait("t", tk_iwl[sl])
                for dc in range(IC):
                    mm = E["t"].matmul(bank[:], IWB[:, sl, dc, :],
                                       LNXT[:, dc, slice(blk * 1024 + nb2 * 512, blk * 1024 + nb2 * 512 + 512)],
                                       start=(dc == 0), stop=(dc == IC - 1))
                tk_mm = b_.sig("t", mm)
                if nb2 == 1:
                    tk_used[sl] = tk_mm
                b_.wait("s", tk_mm)
                tk_raw = b_.sig("s", E["s"].activation(out=SH[:, c, nb2 * 512:(nb2 + 1) * 512],
                                                       in_=bank[:], func=AF.Identity,
                                                       bias=F["INBC"][:, c:c + 1]))
                b_.ps_free[nb2] = tk_raw
            raw = SH[:, c, :]
            hal = HALO[:, c, blk % 2, 0:3]
            b_.wait("v", tk_raw)
            b_.wait("v", tk_halo0)
            b_.wait("v", st_dg.get(c % 2))
            tk_dg = None
            for k in range(4):
                tk_dg = b_.seq("v", E["v"].tensor_scalar(out=DGB[:, c % 2, k, :], in0=IDB[:],
                                                         scalar1=F["CONVW"][:, c, k:k + 1],
                                                         scalar2=None, op0=ALU.mult))
            tk_hn = b_.seq("v", E["v"].tensor_copy(out=HALO[:, c, (blk + 1) % 2, 0:3],
                                                   in_=raw[:, 1021:1024]))
            b_.seq("v", E["v"].tensor_copy(out=CPAD[:, 0:3], in_=hal))
            tk_pad = b_.seq("v", E["v"].tensor_copy(out=CPAD[:, 3:515], in_=raw[:, 0:512]))
            tk_cmm = {}
            for nb2 in range(2):
                bank = b_.ps[2 if nb2 == 0 else 5]
                b_.wait("t", b_.ps_free[2 if nb2 == 0 else 5])
                b_.wait("t", tk_dg)
                b_.wait("t", tk_raw)
                b_.wait("t", tk_pad)
                for k in range(4):
                    if nb2 == 1:
                        mm = E["t"].matmul(bank[:], DGB[:, c % 2, k, :], raw[:, 509 + k:1021 + k],
                                           start=(k == 0), stop=(k == 3))
                    else:
                        mm = E["t"].matmul(bank[:], DGB[:, c % 2, k, :], CPAD[:, k:512 + k],
                                           start=(k == 0), stop=(k == 3))
                tk_cmm[nb2] = b_.sig("t", mm)
            st_dg[c % 2] = tk_cmm[1]
            b_.waits("s", [tk_cmm[0], tk_cmm[1], tk_hn])
            E["s"].activation(out=SH[:, c, 0:512], in_=b_.ps[2][:], func=AF.Silu,
                              bias=F["CBC"][:, c:c + 1])
            tk_sil = b_.sig("s", E["s"].activation(out=SH[:, c, 512:1024], in_=b_.ps[5][:],
                                                   func=AF.Silu, bias=F["CBC"][:, c:c + 1]))
            b_.ps_free[2] = tk_sil
            b_.ps_free[5] = tk_sil
            tk_sh[c] = tk_sil
            b_.wait("v", tk_sil)
        for nb2 in range(2):
            nsl = slice(nb2 * 512, (nb2 + 1) * 512)
            bank = b_.ps[2]
            bank5 = b_.ps[5]
            b_.wait("t", b_.ps_free[2])
            b_.wait("t", b_.ps_free[5])
            b_.wait("t", tk_sh[15])
            for c in range(16):
                E["t"].matmul(bank[0:16, :], F["XPWT"][:, c, 0:16], SH[:, c, nsl],
                              start=(c == 0), stop=False)
            E["t"].matmul(bank[0:16, :], F["XPBROW"][0:1, 0:16], A["ONES512B"][:],
                          start=False, stop=True)
            for c in range(16):
                E["t"].matmul(bank5[0:16, :], F["XPWT"][:, c, 16:32], SH[:, c, nsl],
                              start=(c == 0), stop=False)
            tk_mm = b_.sig("t", E["t"].matmul(bank5[0:16, :], F["XPBROW"][0:1, 16:32], A["ONES512B"][:],
                                              start=False, stop=True))
            tk_shread = tk_mm
            b_.wait("v", tk_mm)
            b_.wait("v", tk_ab_guard)
            tk_xp = b_.seq("v", E["v"].tensor_copy(out=XPT1[:], in_=bank[0:16, :]))
            tk_xp = b_.seq("v", E["v"].tensor_copy(out=XPT2[:], in_=bank5[0:16, :]))
            b_.ps_free[2] = tk_xp
            b_.ps_free[5] = tk_xp
            bank = b_.ps[3]
            b_.wait("t", b_.ps_free[3])
            for c in range(16):
                mm = E["t"].matmul(bank[0:1, :], A["ONESB"][:], SH[:, c, nsl],
                                   start=(c == 0), stop=(c == 15))
            tk_rs = b_.sig("t", mm)
            tk_shread = tk_rs
            b_.wait("v", tk_rs)
            b_.wait("v", tk_bc_prev)
            tk_ro = b_.seq("v", E["v"].tensor_copy(out=ROWS[:], in_=bank[0:1, :]))
            b_.ps_free[3] = tk_ro
            bank = b_.ps[4]
            b_.wait("t", b_.ps_free[4])
            b_.wait("t", tk_ro)
            tk_bc = b_.sig("t", E["t"].matmul(bank[0:16, :], A["ONESRF"][0:1, 0:16], ROWS[:],
                                              start=True, stop=True))
            tk_bc_prev = tk_bc
            b_.wait("s", tk_xp)
            ab = slice(blk * 1024 + nb2 * 512, blk * 1024 + nb2 * 512 + 512)
            b_.seq("s", E["s"].activation(out=AT[:, ab], in_=XPT1[:], func=AF.Exp))
            tk_a = b_.seq("s", E["s"].activation(out=AT[:, ab], in_=AT[:, ab], func=AF.Exp, scale=-1.0))
            tk_b1 = b_.sig("s", E["s"].activation(out=BT[:, ab], in_=XPT2[:], func=AF.Silu))
            b_.waits("v", [tk_b1, tk_bc])
            tk_b2 = b_.seq("v", E["v"].tensor_tensor(out=BT[:, ab], in0=BT[:, ab],
                                                     in1=bank[0:16, :], op=ALU.mult))
            b_.ps_free[4] = tk_b2
            tk_ab_guard = tk_a
            tk_scan_dep[(blk, nb2)] = (tk_a, tk_b2)
    b_.wait("s", tk_scan_dep[(1, 1)][0])
    b_.wait("v", tk_scan_dep[(1, 1)][0])
    tk_scan = b_.seq("v", E["v"].tensor_tensor_scan(out=AT[:, 0:2048],
                                                    data0=AT[:, 0:2048], data1=BT[:, 0:2048],
                                                    initial=0.0, op0=ALU.mult, op1=ALU.add))
    tk_ys = b_.seq("v", E["v"].tensor_copy(out=YSB[:], in_=AT[:, 0:2048]))
    for t in range(NT):
        for half in range(2):
            bank = b_.ps[(2 * t + half) % 2]
            b_.wait("t", b_.ps_free[(2 * t + half) % 2])
            b_.waits("t", [tk_ys, tk_owt])
            E["t"].matmul(bank[:], YSB[0:16, ts(t)], OWTP[0:16, ts(half, 512)],
                          start=True, stop=False)
            tk_mm = b_.sig("t", E["t"].matmul(bank[:], ONESR[0:1, :], OBP[0:1, ts(half, 512)],
                                              start=False, stop=True))
            b_.wait("v", tk_mm)
            tk_x = b_.sig("v", E["v"].tensor_tensor(out=X[:, t, ts(half, 512)], in0=bank[:],
                                                    in1=X[:, t, ts(half, 512)], op=ALU.add))
            b_.ps_free[(2 * t + half) % 2] = tk_x
    b_.barrier()



_NC_CACHE = {}


def kernel(**inputs):
    key = STAGES
    if key not in _NC_CACHE:
        _NC_CACHE[key] = build()
    nc = _NC_CACHE[key]
    x = np.ascontiguousarray(np.asarray(inputs["L0_attn_x"], dtype=np.float32))
    in_maps = []
    for c in range(8):
        m = {k: np.ascontiguousarray(np.asarray(v, dtype=np.float32))
             for k, v in inputs.items() if k != "L0_attn_x"}
        m["L0_attn_x"] = x[2 * c:2 * c + 2]
        in_maps.append(m)
    res = run_bass_kernel_spmd(nc, in_maps, core_ids=list(range(8)))
    out = np.concatenate([np.asarray(r["out"], dtype=np.float32)
                          for r in res.results], axis=0)
    return out.reshape(16, T, D)
